# version 23
# baseline (speedup 1.0000x reference)
import numpy as np
from contextlib import ExitStack

import concourse.bass as bass
import concourse.mybir as mybir
from concourse.bass_utils import run_bass_kernel_spmd

F32 = mybir.dt.float32
BF16 = mybir.dt.bfloat16
AF = mybir.ActivationFunctionType
ALU = mybir.AluOpType
AX = mybir.AxisListType

B, T, D = 2, 16384, 1024
DI, DS, DC = 2048, 16, 4
C16, NN, DH = 16, 256, 512
CN = C16 * NN
CHS = 512
NCHK = 8
RPC = 512
NPT = 4
NSUB = 16
TS = 1024
EPS = 1e-5
GROUPS = [[0, 1, 2, 3], [4, 5, 6, 7]]
NROWS = NCHK * RPC
NCORE = 8
DBG = False
DBG_PHASE = 1


class Ctr:
    def __init__(self, stack, nc, name):
        self.sem = stack.enter_context(nc.semaphore(name))
        self.v = 0

    def inc(self, instr, n=1):
        instr.then_inc(self.sem, n)
        self.v += n

    def wait(self, eng, v=None):
        eng.wait_ge(self.sem, self.v if v is None else v)


def build_nc():
    nc = bass.Bass("TRN2", target_bir_lowering=False, debug=False)

    def din(name, shape):
        return nc.dram_tensor(name, shape, F32, kind="ExternalInput").ap()

    xrows = din("xrows", [NROWS, D])
    win_t = din("win_t", [D, 2 * CHS])
    gam_m = din("gam_m", [D, 1])
    bet_m = din("bet_m", [D, 1])
    wout_t = din("wout_t", [CHS, D])
    wx_t = din("wx_t", [CHS, 33])
    convw = din("convw", [CHS, DC])
    convb = din("convb", [CHS, 1])
    dparam = din("dparam", [CHS, 1])
    wdt = din("wdt", [CHS, 1])
    bdt = din("bdt", [CHS, 1])
    alog = din("alog", [CHS, DS])
    gw1 = din("gw1", [D, DH])
    gb1 = din("gb1", [DH, 1])
    gw2 = din("gw2", [DH, C16])
    gb2 = din("gb2", [1, C16])
    wcols_t = din("wcols_t", [D, CN])
    bcols = din("bcols", [CN, 1])
    projw = din("projw", [CN, D])
    projb = din("projb", [1, D])
    lng = din("lng", [1, D])
    lnb = din("lnb", [1, D])
    colemb = din("colemb", [C16, D])
    colembt = din("colembt", [D, C16])
    ident = din("ident", [128, 128])

    outrows = nc.dram_tensor("outrows", [NROWS, D], F32, kind="ExternalOutput").ap()
    dbgt = {}
    if DBG:
        for nm, cols, dt in [("u1b", NPT * D, BF16), ("st1", NPT * 12, F32),
                             ("U1", 8 * RPC, BF16), ("h1b", 4 * RPC, BF16),
                             ("wtok", NPT * C16, F32), ("lgt", NPT * C16, F32),
                             ("colsw", 32 * RPC, BF16), ("ybraw", NPT * D, F32),
                             ("yb2", NPT * D, F32), ("x0f", NPT * D, F32)]:
            dbgt[nm] = nc.dram_tensor("dbg_" + nm, [128, cols], dt,
                                      kind="ExternalOutput").ap()
        dbg_x1 = nc.dram_tensor("dbg_x1", [NROWS, D], F32, kind="ExternalOutput").ap()
        dbg_xn2 = nc.dram_tensor("dbg_xn2", [NROWS, D], BF16, kind="ExternalOutput").ap()
        dbg_cag = nc.dram_tensor("dbg_cag", [4 * NROWS, D], BF16, kind="ExternalOutput").ap()
        dbg_ssm = nc.dram_tensor("dbg_ssm", [NSUB, 33, TS], F32, kind="ExternalOutput").ap()
        dbg_rs = nc.dram_tensor("dbg_rs", [NROWS, D], F32, kind="ExternalOutput").ap()
        dbg_wp = nc.dram_tensor("dbg_wp", [4 * NROWS, D], F32, kind="ExternalOutput").ap()

    wcols_bf2 = nc.dram_tensor("wcols_bf2", [32 * 128, 8 * 128], BF16).ap()
    projw_bf = nc.dram_tensor("projw_bf", [CN, D], BF16).ap()
    u1b_dram = nc.dram_tensor("u1b_dram", [RPC, D], BF16).ap()
    xn2_slice = nc.dram_tensor("xn2_slice", [NROWS, D], BF16).ap()
    cag2 = nc.dram_tensor("cag2", [NCHK, 4 * RPC, D], BF16).ap()
    x1_dram = nc.dram_tensor("x1_dram", [NROWS, D], F32).ap()
    ssm_in = nc.dram_tensor("ssm_in", [NSUB, 33, TS], F32).ap()
    ssm_out = nc.dram_tensor("ssm_out", [NSUB, 33, TS], F32).ap()
    ssm_bf = nc.dram_tensor("ssm_bf", [NSUB, 32, TS], BF16).ap()
    winb_dram = nc.dram_tensor("winb_dram", [128, 8 * 1024], BF16).ap()
    woutb_dram = nc.dram_tensor("woutb_dram", [128, 4 * D + 4 * 33], BF16).ap()
    wpart = nc.dram_tensor("wpart", [4 * NROWS, D], BF16).ap()
    rsout = nc.dram_tensor("rsout", [NROWS, D], BF16).ap()
    wfm_dram = nc.dram_tensor("wfm_dram", [C16, RPC], BF16).ap()
    colr_dram = nc.dram_tensor("colr_dram", [C16, 1], F32).ap()

    with ExitStack() as perm:
        E = perm.enter_context
        block = E(nc.Block())

        def sy(fn): block.sync(fn)
        def ve(fn): block.vector(fn)
        def sc(fn): block.scalar(fn)
        def te(fn): block.tensor(fn)
        def gp(fn): block.gpsimd(fn)

        dL = Ctr(perm, nc, "dL")
        sx0 = Ctr(perm, nc, "sx0")
        su1 = Ctr(perm, nc, "su1")
        swc = [Ctr(perm, nc, f"swc{i}") for i in range(4)]
        spw = [Ctr(perm, nc, f"spw{i}") for i in range(4)]
        swr = [Ctr(perm, nc, f"swr{i}") for i in range(4)]
        sxs = [Ctr(perm, nc, "sxsA"), Ctr(perm, nc, "sxsB")]
        swp = Ctr(perm, nc, "swp")
        sxn = Ctr(perm, nc, "sxn")
        sdtr = Ctr(perm, nc, "sdtr")
        sbc = [Ctr(perm, nc, "sbcA"), Ctr(perm, nc, "sbcB")]
        dS = Ctr(perm, nc, "dS")
        dG = Ctr(perm, nc, "dG")
        eV = Ctr(perm, nc, "eV")
        eS = Ctr(perm, nc, "eS")
        eT = Ctr(perm, nc, "eT")
        eG = Ctr(perm, nc, "eG")
        bar = Ctr(perm, nc, "bar")

        def dma(q, ctr, out, in_, **kw):
            ctr.inc(q.dma_start(out=out, in_=in_, **kw), 16)
            return ctr.v

        def dump(name, sb, prod_ctr):
            if not DBG:
                return
            snap = prod_ctr.v
            def _(q, snap=snap, name=name, sb=sb, ctr=prod_ctr):
                q.wait_ge(ctr.sem, snap)
                dma(q, dS, dbgt[name][:, :], sb)
                q.wait_ge(dS.sem, dS.v)
            sy(_)
            barrier()

        def barrier():
            def _(q):
                for c in (dS, dL, dG, sxs[0], sxs[1], swp, sx0, su1, sxn):
                    q.wait_ge(c.sem, c.v)
            sy(_)
            for emit in (sy, ve, sc, te, gp):
                def _(q):
                    q.drain()
                    q.sem_inc(bar.sem, 1)
                emit(_)
            bar.v += 5
            tgt = bar.v
            for emit in (sy, ve, sc, te, gp):
                def _(q, tgt=tgt):
                    q.wait_ge(bar.sem, tgt)
                emit(_)

        convw_sb = E(nc.sbuf_tensor("convw_sb", [128, 4 * DC], F32))
        convb_sb = E(nc.sbuf_tensor("convb_sb", [128, 4], F32))
        dparam_sb = E(nc.sbuf_tensor("dparam_sb", [128, 4], F32))
        wdt_sb = E(nc.sbuf_tensor("wdt_sb", [128, 4], F32))
        bdt_sb = E(nc.sbuf_tensor("bdt_sb", [128, 4], F32))
        A_sb = E(nc.sbuf_tensor("A_sb", [128, 4 * DS], F32))
        gb1_sb = E(nc.sbuf_tensor("gb1_sb", [128, 4], F32))
        bcols_sb = E(nc.sbuf_tensor("bcols_sb", [128, 32], F32))
        cb_sb = E(nc.sbuf_tensor("cb_sb", [128, 8], F32))
        pbb = E(nc.sbuf_tensor("pbb", [128, D], F32))
        gls05 = E(nc.sbuf_tensor("gls05", [128, D], F32))
        bls05 = E(nc.sbuf_tensor("bls05", [128, D], F32))
        b2b = E(nc.sbuf_tensor("b2b", [128, C16], F32))
        colrb = E(nc.sbuf_tensor("colrb", [128, C16], F32))
        idsb = E(nc.sbuf_tensor("idsb", [128, 128], F32))
        g1b = E(nc.sbuf_tensor("g1b", [128, 8 * DH], BF16))
        g2b = E(nc.sbuf_tensor("g2b", [128, 4 * C16], BF16))
        cetb = E(nc.sbuf_tensor("cetb", [128, 8 * C16], BF16))
        stats3 = E(nc.sbuf_tensor("stats3", [128, NCHK * NPT * 2], F32))
        state = E(nc.sbuf_tensor("state", [128, 4 * DS], F32))
        gam_sb = E(nc.sbuf_tensor("gam_sb", [128, 8], F32))
        bet_sb = E(nc.sbuf_tensor("bet_sb", [128, 8], F32))
        epsc = E(nc.sbuf_tensor("epsc", [128, 1], F32))
        nc.const_aps.register(F32, EPS, epsc[:, 0:1])

        ps = [E(nc.psum_tensor(f"ps{i}", [128, 512], F32)) for i in range(8)]

        with ExitStack() as ph0:
            E0 = ph0.enter_context
            wscr = E0(nc.sbuf_tensor("wscr", [128, 8 * 1024], F32))
            sscr = E0(nc.sbuf_tensor("sscr", [128, 1024], F32))
            wbf = E0(nc.sbuf_tensor("wbf", [128, 8 * 1024], BF16))

            def _(q):
                for m in range(32):
                    dma(q, dG,
                        wcols_bf2[m*128:(m+1)*128, :].rearrange("p (g q) -> p g q", g=8),
                        wcols_t[:, m*128:(m+1)*128].rearrange("(g p) q -> p g q", p=128))
                dma(q, dG, projw_bf[:, :], projw[:, :])
            gp(_)

            def _(q):
                eV.inc(q.memset(epsc[:, :], EPS), 1)
            ve(_)

            def _(q):
                for c in range(4):
                    dma(q, dL, convw_sb[:, c*DC:(c+1)*DC], convw[c*128:(c+1)*128, :])
                    dma(q, dL, convb_sb[:, c:c+1], convb[c*128:(c+1)*128, :])
                    dma(q, dL, dparam_sb[:, c:c+1], dparam[c*128:(c+1)*128, :])
                    dma(q, dL, wdt_sb[:, c:c+1], wdt[c*128:(c+1)*128, :])
                    dma(q, dL, bdt_sb[:, c:c+1], bdt[c*128:(c+1)*128, :])
                    dma(q, dL, A_sb[:, c*DS:(c+1)*DS], alog[c*128:(c+1)*128, :])
                    dma(q, dL, gb1_sb[:, c:c+1], gb1[c*128:(c+1)*128, :])
                for m in range(32):
                    dma(q, dL, bcols_sb[:, m:m+1], bcols[m*128:(m+1)*128, :])
                dma(q, dL, pbb[:, :], projb[0:1, :].broadcast_to((128, D)))
                dma(q, dL, gls05[:, :], lng[0:1, :].broadcast_to((128, D)))
                dma(q, dL, bls05[:, :], lnb[0:1, :].broadcast_to((128, D)))
                dma(q, dL, b2b[:, :], gb2[0:1, :].broadcast_to((128, C16)))
                dma(q, dL, idsb[:, :], ident[:, :])
                for k in range(8):
                    dma(q, dL, gam_sb[:, k:k+1], gam_m[k*128:(k+1)*128, :])
                    dma(q, dL, bet_sb[:, k:k+1], bet_m[k*128:(k+1)*128, :])
                for k in range(8):
                    dma(q, dL, wscr[:, k*1024:(k+1)*1024], win_t[k*128:(k+1)*128, :])
                dma(q, dL, sscr[0:16, 0:1024], colemb[:, :])
            sy(_)
            ld_done = dL.v

            def _(q, w=ld_done):
                q.wait_ge(dL.sem, w)
                last = None
                for k in range(8):
                    for m in range(8):
                        last = q.matmul(out=ps[6][0:128, m:m+1],
                                        lhsT=wscr[:, k*1024 + m*128:k*1024 + (m+1)*128],
                                        rhs=bet_sb[:, k:k+1],
                                        start=(k == 0), stop=(k == 7))
                eT.inc(last, 1)
            te(_)
            cbeta_done = eT.v

            def _(q, w=ld_done):
                q.wait_ge(dL.sem, w)
                eV.inc(q.tensor_tensor(out=sscr[0:16, 0:1024], in0=sscr[0:16, 0:1024],
                                       in1=sscr[0:16, 0:1024], op=ALU.mult), 1)
            ve(_)
            def _(q):
                eV.wait(q)
                q.tensor_reduce(out=colrb[0:16, 0:1], in_=sscr[0:16, 0:1024], op=ALU.add,
                                axis=AX.X)
                q.drain()
                q.tensor_scalar_max(out=colrb[0:16, 0:1], in0=colrb[0:16, 0:1],
                                    scalar1=1e-24)
                eV.inc(q.drain(), 1)
            ve(_)
            def _(q):
                eV.wait(q)
                q.activation(out=colrb[0:16, 1:2], in_=colrb[0:16, 0:1], func=AF.Ln)
                q.drain()
                q.activation(out=colrb[0:16, 0:1], in_=colrb[0:16, 1:2], func=AF.Exp, scale=-0.5)
                q.drain()
                eS.inc(q.activation(out=A_sb[:, :], in_=A_sb[:, :], func=AF.Exp), 1)
            sc(_)
            def _(q):
                eS.wait(q)
                eV.inc(q.tensor_scalar_mul(out=A_sb[:, :], in0=A_sb[:, :], scalar1=-1.0), 1)
            ve(_)
            def _(q):
                eS.wait(q)
                dma(q, dS, colr_dram[:, :], colrb[0:16, 0:1])
            sc(_)
            def _(q):
                dS.wait(q)
                dma(q, dL, colrb[:, :],
                    colr_dram.rearrange("a b -> b a").broadcast_to((128, C16)))
            sy(_)

            def _(q, w=cbeta_done):
                q.wait_ge(eT.sem, w)
                eS.inc(q.copy(out=cb_sb[:, 0:8], in_=ps[6][0:128, 0:8]), 1)
            sc(_)
            def _(q, w=cbeta_done):
                q.wait_ge(eT.sem, w)
                for k in range(8):
                    q.tensor_scalar_mul(out=wscr[:, k*1024:(k+1)*1024],
                                        in0=wscr[:, k*1024:(k+1)*1024],
                                        scalar1=gam_sb[:, k:k+1])
                eV.inc(q.tensor_copy(out=wbf[:, :], in_=wscr[:, :]), 1)
            ve(_)
            win_cast = eV.v
            def _(q, w=win_cast):
                q.wait_ge(eV.sem, w)
                dma(q, dS, winb_dram[:, :], wbf[:, :])
            sc(_)
            winb_st = dS.v

            def _(q, w=win_cast):
                q.wait_ge(eV.sem, w)
                for c in range(4):
                    dma(q, dL, wscr[:, c*1024:(c+1)*1024], wout_t[c*128:(c+1)*128, :])
                for c in range(4):
                    dma(q, dL, sscr[:, c*33:(c+1)*33], wx_t[c*128:(c+1)*128, :])
            sy(_)
            w2 = dL.v
            def _(q, w=w2, ws=winb_st):
                q.wait_ge(dL.sem, w)
                q.wait_ge(dS.sem, ws)
                q.tensor_copy(out=wbf[:, 0:4*D], in_=wscr[:, 0:4*D])
                q.tensor_copy(out=wbf[:, 4*D:4*D + 4*33], in_=sscr[:, 0:4*33])
                q.tensor_scalar_mul(out=gls05[:, :], in0=gls05[:, :], scalar1=0.5)
                eV.inc(q.tensor_scalar_mul(out=bls05[:, :], in0=bls05[:, :], scalar1=0.5), 1)
            ve(_)
            w3v = eV.v
            def _(q, w=w3v):
                q.wait_ge(eV.sem, w)
                dma(q, dS, woutb_dram[:, :], wbf[:, 0:4*D + 4*33])
            sc(_)
            def _(q, w=w3v):
                q.wait_ge(eV.sem, w)
                for k in range(8):
                    dma(q, dL, wscr[:, k*512:(k+1)*512], gw1[k*128:(k+1)*128, :])
                for k in range(4):
                    dma(q, dL, sscr[:, k*C16:(k+1)*C16], gw2[k*128:(k+1)*128, :])
                for k in range(8):
                    dma(q, dL, sscr[:, 128 + k*C16:128 + (k+1)*C16],
                        colembt[k*128:(k+1)*128, :])
            sy(_)
            w4 = dL.v
            def _(q, w=w4):
                q.wait_ge(dL.sem, w)
                q.tensor_copy(out=g1b[:, :], in_=wscr[:, 0:8*DH])
                q.tensor_copy(out=g2b[:, :], in_=sscr[:, 0:4*C16])
                q.tensor_copy(out=cetb[:, :], in_=sscr[:, 128:128 + 8*C16])
                eV.inc(q.memset(state[:, :], 0.0), 1)
            ve(_)
            def _(q):
                dG.wait(q)
            sy(_)
            barrier()

        ag_done_list = []
        rs_done_list = []
        def ffn_phase(phase):
            with ExitStack() as ph:
                Ep = ph.enter_context
                sfx = f"_p{phase}"
                x0 = Ep(nc.sbuf_tensor("x0" + sfx, [128, 2 * NPT * D], F32))
                xex = Ep(nc.sbuf_tensor("xex" + sfx, [128, 2 * NPT * D], F32))
                yb = Ep(nc.sbuf_tensor("yb" + sfx, [128, NPT * D], F32))
                U1 = Ep(nc.sbuf_tensor("U1" + sfx, [128, 8 * RPC], BF16))
                u1b = Ep(nc.sbuf_tensor("u1b" + sfx, [128, 2 * NPT * D], BF16))
                h1b = Ep(nc.sbuf_tensor("h1b" + sfx, [128, 4 * RPC], BF16))
                colsw = Ep(nc.sbuf_tensor("colsw" + sfx, [128, 32 * RPC], BF16))
                colsg = Ep(nc.sbuf_tensor("colsg" + sfx, [128, 2 * RPC], BF16))
                wcpan = Ep(nc.sbuf_tensor("wcpan" + sfx, [128, 4 * 1024], BF16))
                pwwin = Ep(nc.sbuf_tensor("pwwin" + sfx, [128, 4 * D], BF16))
                wrep = Ep(nc.sbuf_tensor("wrep" + sfx, [128, 4 * RPC], BF16))
                wtok = Ep(nc.sbuf_tensor("wtok" + sfx, [128, NPT * C16], F32))
                lgt = Ep(nc.sbuf_tensor("lgt" + sfx, [128, NPT * C16], F32))
                sgt = Ep(nc.sbuf_tensor("sgt" + sfx, [128, NPT * C16], F32))
                wfm = Ep(nc.sbuf_tensor("wfm" + sfx, [C16, RPC], BF16))
                st1 = Ep(nc.sbuf_tensor("st1" + sfx, [128, 2 * NPT * 12], F32))
                rstg = Ep(nc.sbuf_tensor("rstg" + sfx, [128, NPT * D], BF16))

                def stc(P, pt, i):
                    o = (P * NPT + pt) * 12 + i
                    return st1[:, o:o+1]

                def xs(P, pt):
                    return x0[:, (P * NPT + pt)*D:(P * NPT + pt + 1)*D]

                def xe(P, pt):
                    return xex[:, (P * NPT + pt)*D:(P * NPT + pt + 1)*D]

                def us(P, pt):
                    return u1b[:, (P * NPT + pt)*D:(P * NPT + pt + 1)*D]

                st = dict(prev_xn2_store=None, prev_xn2_par=0,
                          prev_proj_mm=eT.v, prev_proj_evac=eS.v,
                          prev_colsw_last=eV.v, prev_cols_mm_last=eT.v,
                          prev_stores=dS.v, u1ld=0, u1st=0)

                def emit_head_load(j):
                    P = j % 2
                    r0 = j * RPC
                    def _(q, r0=r0, d=st["prev_stores"], j=j, P=P):
                        q.wait_ge(dS.sem, d)
                        if phase == 2:
                            q.wait_ge(eG.sem, rs_done_list[j])
                        if phase == 1:
                            for pt in range(NPT):
                                dma(q, sx0, xs(P, pt),
                                    xrows[r0 + pt*128: r0 + (pt+1)*128, :])
                        else:
                            for pt in range(NPT):
                                dma(q, sx0, xe(P, pt),
                                    x1_dram[r0 + pt*128: r0 + (pt+1)*128, :])
                            for pt in range(NPT):
                                dma(q, sx0, rstg[:, pt*D:(pt+1)*D],
                                    rsout[r0 + pt*128: r0 + (pt+1)*128, :])
                    sy(_)
                    st["xload"] = sx0.v

                def emit_head_rest(j):
                    P = j % 2
                    xload = st["xload"]
                    if phase == 2:
                        def _(q, j=j, P=P, w=xload):
                            q.wait_ge(sx0.sem, w)
                            for pt in range(NPT):
                                scol = (j * NPT + pt) * 2
                                q.tensor_tensor(out=stc(P, pt, 11),
                                                in0=stats3[:, scol:scol+1],
                                                in1=stats3[:, scol+1:scol+2],
                                                op=ALU.mult)
                            q.drain()
                            for pt in range(NPT):
                                q.tensor_scalar_mul(out=stc(P, pt, 11),
                                                    in0=stc(P, pt, 11), scalar1=-1.0)
                            q.drain()
                            for pt in range(NPT):
                                scol = (j * NPT + pt) * 2
                                r3 = stats3[:, scol+1:scol+2]
                                q.scalar_tensor_tensor(out=xs(P, pt), in0=xe(P, pt),
                                                       scalar=r3,
                                                       in1=rstg[:, pt*D:(pt+1)*D],
                                                       op0=ALU.mult, op1=ALU.add)
                            for pt in range(NPT):
                                q.tensor_scalar(out=xs(P, pt), in0=xs(P, pt),
                                                scalar1=stc(P, pt, 11), scalar2=None,
                                                op0=ALU.add)
                            for pt in range(NPT):
                                q.tensor_tensor(out=xs(P, pt), in0=xs(P, pt),
                                                in1=xe(P, pt), op=ALU.add)
                            eV.inc(q.drain(), 1)
                        ve(_)

                    def _(q, w=xload, P=P):
                        q.wait_ge(sx0.sem, w)
                        eV.wait(q)
                        for pt in range(NPT):
                            ii = q.activation(out=xe(P, pt), in_=xs(P, pt),
                                              func=AF.Square)
                        eS.inc(ii, 1)
                    sc(_)
                    sq1 = eS.v
                    def _(q, w=xload, w2=sq1, P=P):
                        q.wait_ge(sx0.sem, w)
                        q.wait_ge(eS.sem, w2)
                        eV.wait(q)
                        for pt in range(NPT):
                            q.tensor_reduce(out=stc(P, pt, 0), in_=xs(P, pt),
                                            op=ALU.add, axis=AX.X)
                        for pt in range(NPT):
                            q.tensor_reduce(out=stc(P, pt, 1), in_=xe(P, pt),
                                            op=ALU.add, axis=AX.X)
                        for pt in range(NPT):
                            q.tensor_scalar_mul(out=stc(P, pt, 2), in0=stc(P, pt, 0),
                                                scalar1=1.0 / D)
                        for pt in range(NPT):
                            q.tensor_scalar_mul(out=stc(P, pt, 3), in0=stc(P, pt, 1),
                                                scalar1=1.0 / D)
                        q.drain()
                        for pt in range(NPT):
                            q.tensor_tensor(out=stc(P, pt, 11), in0=stc(P, pt, 2),
                                            in1=stc(P, pt, 2), op=ALU.mult)
                        q.drain()
                        for pt in range(NPT):
                            q.tensor_tensor(out=stc(P, pt, 3), in0=stc(P, pt, 3),
                                            in1=stc(P, pt, 11), op=ALU.subtract)
                        eV.inc(q.drain(), 1)
                    ve(_)
                    def _(q, P=P):
                        eV.wait(q)
                        for pt in range(NPT):
                            q.activation(out=stc(P, pt, 11), in_=stc(P, pt, 3),
                                         func=AF.Ln, bias=EPS)
                        q.drain()
                        for pt in range(NPT):
                            q.activation(out=stc(P, pt, 4), in_=stc(P, pt, 11),
                                         func=AF.Exp, scale=-0.5)
                        eS.inc(q.drain(), 1)
                    sc(_)
                    def _(q, P=P):
                        eS.wait(q)
                        for pt in range(NPT):
                            q.tensor_tensor(out=stc(P, pt, 5), in0=stc(P, pt, 2),
                                            in1=stc(P, pt, 4), op=ALU.mult)
                        for pt in range(NPT):
                            q.tensor_tensor(out=stc(P, pt, 6), in0=stc(P, pt, 4),
                                            in1=stc(P, pt, 4), op=ALU.mult)
                        q.drain()
                        for pt in range(NPT):
                            q.tensor_scalar_mul(out=stc(P, pt, 5), in0=stc(P, pt, 5),
                                                scalar1=-1.0)
                        for pt in range(NPT):
                            q.tensor_tensor(out=stc(P, pt, 6), in0=stc(P, pt, 6),
                                            in1=stc(P, pt, 3), op=ALU.mult)
                        q.drain()
                        for pt in range(NPT):
                            q.tensor_scalar_mul(out=stc(P, pt, 6), in0=stc(P, pt, 6),
                                                scalar1=float(D))
                        q.drain()
                        for pt in range(NPT):
                            q.tensor_scalar_max(out=stc(P, pt, 6), in0=stc(P, pt, 6),
                                                scalar1=1e-24)
                        eV.inc(q.drain(), 1)
                    ve(_)
                    def _(q, P=P, xw=st["prev_xn2_store"], xp=st["prev_xn2_par"]):
                        eV.wait(q)
                        if phase == 1 and xw is not None:
                            q.wait_ge(sxs[xp].sem, xw)
                        for pt in range(NPT):
                            q.activation(out=stc(P, pt, 11), in_=stc(P, pt, 6),
                                         func=AF.Ln)
                        q.drain()
                        for pt in range(NPT):
                            q.activation(out=stc(P, pt, 6), in_=stc(P, pt, 11),
                                         func=AF.Exp, scale=-0.5)
                        for pt in range(NPT):
                            q.activation(out=us(P, pt), in_=xs(P, pt),
                                         func=AF.Identity, scale=stc(P, pt, 4),
                                         bias=stc(P, pt, 5))
                        eS.inc(q.drain(), 1)
                    sc(_)

                    def _(q, P=P):
                        eS.wait(q)
                        dma(q, dS, u1b_dram.rearrange("(a p) d -> p a d", p=128),
                            u1b[:, P*NPT*D:(P+1)*NPT*D].rearrange("p (a d) -> p a d",
                                                                  a=NPT))
                    sc(_)
                    st["u1st"] = dS.v
                    def _(q, w=st["u1st"], pt_=st["prev_cols_mm_last"]):
                        q.wait_ge(dS.sem, w)
                        q.wait_ge(eT.sem, pt_)
                        for dt8 in range(8):
                            dma(q, su1, U1[:, dt8*RPC:(dt8+1)*RPC],
                                u1b_dram[:, dt8*128:(dt8+1)*128], transpose=True)
                    sy(_)
                    st["u1ld"] = su1.v

                def emit_gates(j):
                    P = j % 2
                    def _(q, w=st["u1ld"], pe=st["prev_proj_evac"]):
                        q.wait_ge(su1.sem, w)
                        q.wait_ge(eS.sem, pe)
                        last = None
                        for m4 in range(4):
                            for kk in range(8):
                                last = q.matmul(out=ps[m4][0:128, :],
                                                lhsT=g1b[:, kk*DH + m4*128: kk*DH + (m4+1)*128],
                                                rhs=U1[:, kk*RPC:(kk+1)*RPC],
                                                start=(kk == 0), stop=(kk == 7))
                        eT.inc(last, 1)
                    te(_)
                    g1t = eT.v
                    def _(q, w=g1t):
                        q.wait_ge(eT.sem, w)
                        for m4 in range(4):
                            ii = q.activation(out=h1b[:, m4*RPC:(m4+1)*RPC],
                                              in_=ps[m4][0:128, :],
                                              func=AF.Gelu, bias=gb1_sb[:, m4:m4+1])
                        eS.inc(ii, 1)
                    sc(_)

                    def _(q):
                        eS.wait(q)
                        last = None
                        for tb in range(NPT):
                            for k4 in range(4):
                                q.matmul(out=ps[tb][0:128, 0:C16],
                                         lhsT=h1b[:, k4*RPC + tb*128: k4*RPC + (tb+1)*128],
                                         rhs=g2b[:, k4*C16:(k4+1)*C16],
                                         start=(k4 == 0), stop=(k4 == 3))
                            for k8 in range(8):
                                last = q.matmul(out=ps[4+tb][0:128, 0:C16],
                                                lhsT=U1[:, k8*RPC + tb*128: k8*RPC + (tb+1)*128],
                                                rhs=cetb[:, k8*C16:(k8+1)*C16],
                                                start=(k8 == 0), stop=(k8 == 7))
                        eT.inc(last, 1)
                    te(_)
                    g2t = eT.v
                    def _(q, w=g2t, P=P):
                        q.wait_ge(eT.sem, w)
                        for tb in range(NPT):
                            q.tensor_tensor(out=sgt[:, tb*C16:(tb+1)*C16],
                                            in0=ps[tb][0:128, 0:C16], in1=b2b[:, :],
                                            op=ALU.add)
                        for tb in range(NPT):
                            q.tensor_tensor(out=lgt[:, tb*C16:(tb+1)*C16],
                                            in0=ps[4+tb][0:128, 0:C16], in1=colrb[:, :],
                                            op=ALU.mult)
                        q.drain()
                        for tb in range(NPT):
                            q.tensor_scalar_mul(out=lgt[:, tb*C16:(tb+1)*C16],
                                                in0=lgt[:, tb*C16:(tb+1)*C16],
                                                scalar1=stc(P, tb, 6))
                        eV.inc(q.drain(), 1)
                    ve(_)
                    def _(q):
                        eV.wait(q)
                        for tb in range(NPT):
                            q.activation(out=sgt[:, tb*C16:(tb+1)*C16],
                                         in_=sgt[:, tb*C16:(tb+1)*C16], func=AF.Sigmoid)
                        eS.inc(q.drain(), 1)
                    sc(_)
                    def _(q, P=P):
                        eS.wait(q)
                        for tb in range(NPT):
                            sl = lgt[:, tb*C16:(tb+1)*C16]
                            q.tensor_tensor(out=sl, in0=sl, in1=sgt[:, tb*C16:(tb+1)*C16],
                                            op=ALU.add)
                        q.drain()
                        for tb in range(NPT):
                            q.tensor_reduce(out=stc(P, tb, 11),
                                            in_=lgt[:, tb*C16:(tb+1)*C16],
                                            op=ALU.max, axis=AX.X)
                        q.drain()
                        for tb in range(NPT):
                            q.tensor_single_scalar(out=wtok[:, tb*C16:(tb+1)*C16],
                                                   in_=lgt[:, tb*C16:(tb+1)*C16],
                                                   scalar=stc(P, tb, 11), op=ALU.is_equal)
                        q.drain()
                        for tb in range(NPT):
                            q.tensor_scalar_mul(out=stc(P, tb, 11), in0=stc(P, tb, 11),
                                                scalar1=-1.0)
                        eV.inc(q.drain(), 1)
                    ve(_)
                    def _(q, P=P):
                        eV.wait(q)
                        for tb in range(NPT):
                            q.activation(out=sgt[:, tb*C16:(tb+1)*C16],
                                         in_=lgt[:, tb*C16:(tb+1)*C16],
                                         func=AF.Exp, bias=stc(P, tb, 11))
                        eS.inc(q.drain(), 1)
                    sc(_)
                    def _(q, P=P):
                        eS.wait(q)
                        for tb in range(NPT):
                            q.tensor_reduce(out=stc(P, tb, 11),
                                            in_=sgt[:, tb*C16:(tb+1)*C16],
                                            op=ALU.add, axis=AX.X)
                        q.drain()
                        for tb in range(NPT):
                            q.reciprocal(out=stc(P, tb, 11), in_=stc(P, tb, 11))
                        for tb in range(NPT):
                            q.tensor_tensor(out=wtok[:, tb*C16:(tb+1)*C16],
                                            in0=wtok[:, tb*C16:(tb+1)*C16],
                                            in1=sgt[:, tb*C16:(tb+1)*C16], op=ALU.mult)
                        q.drain()
                        for tb in range(NPT):
                            q.tensor_scalar_mul(out=wtok[:, tb*C16:(tb+1)*C16],
                                                in0=wtok[:, tb*C16:(tb+1)*C16],
                                                scalar1=stc(P, tb, 11))
                        eV.inc(q.drain(), 1)
                    ve(_)

                    def _(q):
                        eV.wait(q)
                        last = None
                        for tb in range(NPT):
                            last = q.transpose(out=ps[6][0:C16, tb*128:(tb+1)*128],
                                               in_=wtok[:, tb*C16:(tb+1)*C16],
                                               identity=idsb[:, :])
                        eT.inc(last, 1)
                    te(_)
                    wtr = eT.v
                    def _(q, w=wtr):
                        q.wait_ge(eT.sem, w)
                        eS.inc(q.copy(out=wfm[:, :], in_=ps[6][0:C16, 0:RPC]), 1)
                    sc(_)
                    def _(q):
                        eS.wait(q)
                        dma(q, dS, wfm_dram[:, :], wfm[:, :])
                    sc(_)
                    st["wfm_st"] = dS.v

                def emit_cols(j):
                    eS0, eV0 = eS.v, eV.v
                    st["eS0"], st["eV0"] = eS0, eV0
                    pan_ld, pan_used, wrep_ld = [], [], []

                    def _(q, w=st["wfm_st"], eV0=eV0, pv=st["prev_colsw_last"]):
                        q.wait_ge(dS.sem, w)
                        for c in range(C16):
                            if c >= 4:
                                q.wait_ge(eV.sem, eV0 + 2*(c-4) + 2)
                            else:
                                q.wait_ge(eV.sem, pv)
                            dma(q, swr[c % 4], wrep[:, (c % 4)*RPC:((c % 4)+1)*RPC],
                                wfm_dram[c:c+1, :].broadcast_to((128, RPC)))
                            wrep_ld.append(swr[c % 4].v)
                    gp(_)
                    def _(q, eT0=eT.v, pc=st["prev_cols_mm_last"]):
                        for m in range(32):
                            if m >= 4:
                                q.wait_ge(eT.sem, eT0 + m - 3)
                            else:
                                q.wait_ge(eT.sem, pc)
                            dma(q, swc[m % 4], wcpan[:, (m % 4)*1024:((m % 4)+1)*1024],
                                wcols_bf2[m*128:(m+1)*128, :])
                            pan_ld.append(swc[m % 4].v)
                    sy(_)
                    del _

                    def _(q):
                        for m in range(32):
                            q.wait_ge(swc[m % 4].sem, pan_ld[m])
                            if m >= 2:
                                q.wait_ge(eS.sem, eS0 + m - 1)
                            last = None
                            for kk in range(8):
                                last = q.matmul(
                                    out=ps[m % 2][0:128, :],
                                    lhsT=wcpan[:, (m % 4)*1024 + kk*128:
                                               (m % 4)*1024 + (kk+1)*128],
                                    rhs=U1[:, kk*RPC:(kk+1)*RPC],
                                    start=(kk == 0), stop=(kk == 7))
                            eT.inc(last, 1)
                            pan_used.append(eT.v)
                    te(_)
                    def _(q):
                        for m in range(32):
                            q.wait_ge(eT.sem, pan_used[m])
                            if m >= 2:
                                q.wait_ge(eV.sem, eV0 + m - 1)
                            eS.inc(q.activation(out=colsg[:, (m % 2)*RPC:((m % 2)+1)*RPC],
                                                in_=ps[m % 2][0:128, :], func=AF.Gelu,
                                                bias=bcols_sb[:, m:m+1]), 1)
                    sc(_)
                    def _(q):
                        for m in range(32):
                            c = m // 2
                            q.wait_ge(eS.sem, eS0 + m + 1)
                            q.wait_ge(swr[c % 4].sem, wrep_ld[c])
                            eV.inc(q.tensor_tensor(out=colsw[:, m*RPC:(m+1)*RPC],
                                                   in0=colsg[:, (m % 2)*RPC:((m % 2)+1)*RPC],
                                                   in1=wrep[:, (c % 4)*RPC:((c % 4)+1)*RPC],
                                                   op=ALU.mult), 1)
                    ve(_)
                    st["colsw_last"] = eV.v
                    st["prev_cols_mm_last"] = eT.v

                def emit_proj(j):
                    pw_ld = []
                    eT1 = eT.v
                    def _(q, ppm=st["prev_proj_mm"]):
                        for kk in range(32):
                            if kk >= 4:
                                q.wait_ge(eT.sem, eT1 + kk - 3)
                            else:
                                q.wait_ge(eT.sem, ppm)
                            dma(q, spw[kk % 4], pwwin[:, (kk % 4)*D:((kk % 4)+1)*D],
                                projw_bf[kk*128:(kk+1)*128, :])
                            pw_ld.append(spw[kk % 4].v)
                    sy(_)
                    def _(q, eS0=st["eS0"], eV0=st["eV0"]):
                        for kk in range(32):
                            q.wait_ge(spw[kk % 4].sem, pw_ld[kk])
                            q.wait_ge(eV.sem, eV0 + kk + 1)
                            if kk == 0:
                                q.wait_ge(eS.sem, eS0 + 32)
                            last = None
                            for tb in range(NPT):
                                for fh in range(2):
                                    last = q.matmul(
                                        out=ps[tb*2 + fh][0:128, :],
                                        lhsT=colsw[:, kk*RPC + tb*128: kk*RPC + (tb+1)*128],
                                        rhs=pwwin[:, (kk % 4)*D + fh*512:
                                                  (kk % 4)*D + (fh+1)*512],
                                        start=(kk == 0), stop=(kk == 31))
                            eT.inc(last, 1)
                    te(_)
                    st["prev_proj_mm"] = eT.v

                def emit_evac(j):
                    def _(q, tv=st["prev_proj_mm"]):
                        q.wait_ge(eT.sem, tv)
                        for tb in range(NPT):
                            for fh in range(2):
                                ii = q.copy(out=yb[:, tb*D + fh*512: tb*D + (fh+1)*512],
                                            in_=ps[tb*2 + fh][0:128, :])
                        eS.inc(ii, 1)
                    sc(_)
                    st["prev_proj_evac"] = eS.v

                def emit_tail(j):
                    P = j % 2
                    r0 = j * RPC
                    k = j
                    def _(q, P=P):
                        eS.wait(q)
                        for pt in range(NPT):
                            ysl = yb[:, pt*D:(pt+1)*D]
                            q.tensor_tensor(out=ysl, in0=ysl, in1=pbb[:, :], op=ALU.add)
                            q.scalar_tensor_tensor(out=ysl, in0=xs(P, pt),
                                                   scalar=stc(P, pt, 4), in1=ysl,
                                                   op0=ALU.mult, op1=ALU.add)
                            q.tensor_scalar(out=ysl, in0=ysl, scalar1=stc(P, pt, 5),
                                            scalar2=None, op0=ALU.add)
                        for pt in range(NPT):
                            q.tensor_reduce(out=stc(P, pt, 7), in_=yb[:, pt*D:(pt+1)*D],
                                            op=ALU.add, axis=AX.X)
                        for pt in range(NPT):
                            q.tensor_tensor(out=xe(P, pt), in0=yb[:, pt*D:(pt+1)*D],
                                            in1=yb[:, pt*D:(pt+1)*D], op=ALU.mult)
                        for pt in range(NPT):
                            q.tensor_reduce(out=stc(P, pt, 8), in_=xe(P, pt),
                                            op=ALU.add, axis=AX.X)
                        for pt in range(NPT):
                            q.tensor_scalar_mul(out=stc(P, pt, 7), in0=stc(P, pt, 7),
                                                scalar1=1.0 / D)
                        for pt in range(NPT):
                            q.tensor_scalar_mul(out=stc(P, pt, 8), in0=stc(P, pt, 8),
                                                scalar1=1.0 / D)
                        q.drain()
                        for pt in range(NPT):
                            q.tensor_tensor(out=stc(P, pt, 11), in0=stc(P, pt, 7),
                                            in1=stc(P, pt, 7), op=ALU.mult)
                        q.drain()
                        for pt in range(NPT):
                            q.tensor_tensor(out=stc(P, pt, 8), in0=stc(P, pt, 8),
                                            in1=stc(P, pt, 11), op=ALU.subtract)
                        eV.inc(q.drain(), 1)
                    ve(_)
                    def _(q, P=P):
                        eV.wait(q)
                        for pt in range(NPT):
                            q.activation(out=stc(P, pt, 11), in_=stc(P, pt, 8),
                                         func=AF.Ln, bias=EPS)
                        q.drain()
                        for pt in range(NPT):
                            q.activation(out=stc(P, pt, 9), in_=stc(P, pt, 11),
                                         func=AF.Exp, scale=-0.5)
                        eS.inc(q.drain(), 1)
                    sc(_)
                    def _(q, P=P):
                        eS.wait(q)
                        for pt in range(NPT):
                            q.tensor_tensor(out=stc(P, pt, 10), in0=stc(P, pt, 7),
                                            in1=stc(P, pt, 9), op=ALU.mult)
                        q.drain()
                        for pt in range(NPT):
                            q.tensor_scalar_mul(out=stc(P, pt, 10), in0=stc(P, pt, 10),
                                                scalar1=-1.0)
                        eV.inc(q.drain(), 1)
                    ve(_)
                    def _(q, P=P):
                        eV.wait(q)
                        for pt in range(NPT):
                            ii = q.activation(out=xe(P, pt), in_=yb[:, pt*D:(pt+1)*D],
                                              func=AF.Identity, scale=stc(P, pt, 9),
                                              bias=stc(P, pt, 10))
                        eS.inc(ii, 1)
                    sc(_)
                    def _(q, P=P):
                        eS.wait(q)
                        for pt in range(NPT):
                            tl = xe(P, pt)
                            q.tensor_tensor(out=tl, in0=tl, in1=gls05[:, :], op=ALU.mult)
                            q.tensor_tensor(out=tl, in0=tl, in1=bls05[:, :], op=ALU.add)
                            ii = q.tensor_tensor(out=xs(P, pt), in0=xs(P, pt), in1=tl,
                                                 op=ALU.add)
                        eV.inc(ii, 1)
                    ve(_)

                    st["prev_colsw_last"] = st["colsw_last"]

                    if phase == 1:
                        def _(q, k=k, P=P):
                            eV.wait(q)
                            for pt in range(NPT):
                                q.tensor_reduce(out=stc(P, pt, 0), in_=xs(P, pt),
                                                op=ALU.add, axis=AX.X)
                            for pt in range(NPT):
                                q.tensor_tensor(out=xe(P, pt), in0=xs(P, pt),
                                                in1=xs(P, pt), op=ALU.mult)
                            for pt in range(NPT):
                                q.tensor_reduce(out=stc(P, pt, 1), in_=xe(P, pt),
                                                op=ALU.add, axis=AX.X)
                            for pt in range(NPT):
                                scol = (k * NPT + pt) * 2
                                q.tensor_scalar_mul(out=stats3[:, scol:scol+1],
                                                    in0=stc(P, pt, 0), scalar1=1.0 / D)
                            for pt in range(NPT):
                                q.tensor_scalar_mul(out=stc(P, pt, 3), in0=stc(P, pt, 1),
                                                    scalar1=1.0 / D)
                            q.drain()
                            for pt in range(NPT):
                                scol = (k * NPT + pt) * 2
                                q.tensor_tensor(out=stc(P, pt, 11),
                                                in0=stats3[:, scol:scol+1],
                                                in1=stats3[:, scol:scol+1], op=ALU.mult)
                            q.drain()
                            for pt in range(NPT):
                                q.tensor_tensor(out=stc(P, pt, 3), in0=stc(P, pt, 3),
                                                in1=stc(P, pt, 11), op=ALU.subtract)
                            eV.inc(q.drain(), 1)
                        ve(_)
                        def _(q, k=k, P=P):
                            eV.wait(q)
                            for pt in range(NPT):
                                q.activation(out=stc(P, pt, 11), in_=stc(P, pt, 3),
                                             func=AF.Ln, bias=EPS)
                            q.drain()
                            for pt in range(NPT):
                                scol = (k * NPT + pt) * 2
                                q.activation(out=stats3[:, scol+1:scol+2],
                                             in_=stc(P, pt, 11), func=AF.Exp, scale=-0.5)
                            eS.inc(q.drain(), 1)
                        sc(_)
                        def _(q, k=k, P=P):
                            eS.wait(q)
                            for pt in range(NPT):
                                scol = (k * NPT + pt) * 2
                                q.tensor_tensor(out=stc(P, pt, 11),
                                                in0=stats3[:, scol+1:scol+2],
                                                in1=stats3[:, scol+1:scol+2], op=ALU.mult)
                            q.drain()
                            for pt in range(NPT):
                                q.tensor_tensor(out=stc(P, pt, 11), in0=stc(P, pt, 11),
                                                in1=stc(P, pt, 3), op=ALU.mult)
                            eV.inc(q.drain(), 1)
                        ve(_)
                        def _(q, P=P):
                            eV.wait(q)
                            for pt in range(NPT):
                                q.activation(out=stc(P, pt, 11), in_=stc(P, pt, 11),
                                             func=AF.Ln, bias=EPS)
                            q.drain()
                            for pt in range(NPT):
                                q.activation(out=stc(P, pt, 11), in_=stc(P, pt, 11),
                                             func=AF.Exp, scale=-0.5)
                            eS.inc(q.drain(), 1)
                        sc(_)
                        def _(q, k=k, P=P):
                            eS.wait(q)
                            for pt in range(NPT):
                                scol = (k * NPT + pt) * 2
                                q.tensor_tensor(out=stc(P, pt, 11), in0=stc(P, pt, 11),
                                                in1=stats3[:, scol+1:scol+2], op=ALU.mult)
                            q.drain()
                            for pt in range(NPT):
                                scol = (k * NPT + pt) * 2
                                q.tensor_tensor(out=stc(P, pt, 10),
                                                in0=stats3[:, scol:scol+1],
                                                in1=stc(P, pt, 11), op=ALU.mult)
                            q.drain()
                            for pt in range(NPT):
                                q.tensor_scalar_mul(out=stc(P, pt, 10), in0=stc(P, pt, 10),
                                                    scalar1=-1.0)
                            eV.inc(q.drain(), 1)
                        ve(_)
                        def _(q, P=P):
                            eV.wait(q)
                            for pt in range(NPT):
                                ii = q.activation(out=us(P, pt), in_=xs(P, pt),
                                                  func=AF.Identity, scale=stc(P, pt, 11),
                                                  bias=stc(P, pt, 10))
                            eS.inc(ii, 1)
                        sc(_)
                        def _(q, r0=r0, k=k, P=P):
                            eS.wait(q)
                            dma(q, sxs[k % 2],
                                xn2_slice[r0:r0+RPC, :].rearrange("(a p) d -> p a d", p=128),
                                u1b[:, P*NPT*D:(P+1)*NPT*D].rearrange("p (a d) -> p a d",
                                                                      a=NPT))
                            dma(q, dS, x1_dram[r0:r0+RPC, :].rearrange("(a p) d -> p a d",
                                                                       p=128),
                                x0[:, P*NPT*D:(P+1)*NPT*D].rearrange("p (a d) -> p a d",
                                                                     a=NPT))
                        sc(_)
                        st["prev_stores"] = dS.v
                        st["prev_xn2_store"] = sxs[k % 2].v
                        st["prev_xn2_par"] = k % 2
                        def _(q, k=k, w=st["prev_xn2_store"]):
                            q.wait_ge(sxs[k % 2].sem, w)
                            eG.inc(q.collective_compute(
                                "AllGather", ALU.bypass, replica_groups=GROUPS,
                                ins=[xn2_slice[k*RPC:(k+1)*RPC, :].opt()],
                                outs=[cag2[k, :, :].opt()]), 1)
                            ag_done_list.append(eG.v)
                        gp(_)
                    else:
                        def _(q, r0=r0, P=P):
                            eV.wait(q)
                            dma(q, dS,
                                outrows[r0:r0+RPC, :].rearrange("(a p) d -> p a d", p=128),
                                x0[:, P*NPT*D:(P+1)*NPT*D].rearrange("p (a d) -> p a d",
                                                                     a=NPT))
                        sc(_)
                        st["prev_stores"] = dS.v

                emit_head_load(0)
                emit_head_rest(0)
                emit_gates(0)
                emit_cols(0)
                for k in range(NCHK):
                    emit_proj(k)
                    if k + 1 < NCHK:
                        emit_head_load(k + 1)
                        emit_head_rest(k + 1)
                    emit_evac(k)
                    if k + 1 < NCHK:
                        emit_gates(k + 1)
                        emit_cols(k + 1)
                    emit_tail(k)
                barrier()

        ffn_phase(1)

        with ExitStack() as ph:
            Em = ph.enter_context
            Xn2 = Em(nc.sbuf_tensor("Xn2", [128, 8 * TS], BF16))
            xbwin = Em(nc.sbuf_tensor("xbwin", [128, TS + 3], F32))
            halo = Em(nc.sbuf_tensor("halo", [128, 12], F32))
            zsb = Em(nc.sbuf_tensor("zsb", [128, 8 * TS], BF16))
            xcsb = Em(nc.sbuf_tensor("xcsb", [128, 4 * TS], F32))
            xcbw = Em(nc.sbuf_tensor("xcbw", [128, TS], BF16))
            dt4 = Em(nc.sbuf_tensor("dt4", [128, 4 * TS], F32))
            dtx4 = Em(nc.sbuf_tensor("dtx4", [128, 4 * TS], BF16))
            dtrawb = Em(nc.sbuf_tensor("dtrawb", [128, TS], F32))
            Ebuf = Em(nc.sbuf_tensor("Ebuf", [128, 2 * TS], F32))
            ubuf = Em(nc.sbuf_tensor("ubuf", [128, 2 * TS], BF16))
            hbuf = Em(nc.sbuf_tensor("hbuf", [128, TS], BF16))
            ytmp = Em(nc.sbuf_tensor("ytmp", [128, TS], BF16))
            tmpb = Em(nc.sbuf_tensor("tmpb", [128, TS], F32))
            acc4 = Em(nc.sbuf_tensor("acc4", [128, 4 * TS], F32))
            Brep = Em(nc.sbuf_tensor("Brep", [128, 2 * TS], BF16))
            Crep = Em(nc.sbuf_tensor("Crep", [128, 2 * TS], BF16))
            ygsb = Em(nc.sbuf_tensor("ygsb", [128, 4 * TS], BF16))
            ssm_sb = Em(nc.sbuf_tensor("ssm_sb", [128, TS], F32))
            evsb = Em(nc.sbuf_tensor("evsb", [128, D], BF16))
            winb = Em(nc.sbuf_tensor("winb", [128, 8 * 1024], BF16))
            woutb = Em(nc.sbuf_tensor("woutb", [128, 4 * D], BF16))
            wxb = Em(nc.sbuf_tensor("wxb", [128, 4 * 33], BF16))
            evsc = Em(nc.sbuf_tensor("evsc", [128, TS], F32))

            def _(q):
                q.memset(halo[:, :], 0.0)
                eV.inc(q.memset(state[:, :], 0.0), 1)
            ve(_)

            def _(q):
                dma(q, dL, winb[:, :], winb_dram[:, :])
                dma(q, dL, woutb[:, :], woutb_dram[:, 0:4*D])
                dma(q, dL, wxb[:, :], woutb_dram[:, 4*D:4*D + 4*33])
            sy(_)
            mwl = dL.v

            st2 = dict(win_done=eT.v, mh_evac=eS.v, wo_mm=eT.v, bc_cast=0)
            prev_scan_eV1 = None

            def emit_mhead(s):
                kch = s // 2
                hh = s % 2
                zb = (s % 2) * 4 * TS

                def _(q, kch=kch, hh=hh, a=st2["win_done"], g=ag_done_list[kch]):
                    q.wait_ge(eG.sem, g)
                    q.wait_ge(eT.sem, a)
                    for rb in range(2):
                        row0 = (2*hh + rb) * RPC
                        for dt8 in range(8):
                            dma(q, sxn, Xn2[:, dt8*TS + rb*512: dt8*TS + (rb+1)*512],
                                cag2[kch, row0:row0+512, dt8*128:(dt8+1)*128], transpose=True)
                sy(_)
                xn_ld = sxn.v

                eT0, eS0, eV0 = eT.v, eS.v, eV.v
                def winpos(m):
                    return m + 1 if m < 4 else 2*m - 2
                def wxpos(c):
                    return 5 + 2*c
                def evacpos(m):
                    return 2*m + 1 if m < 4 else m + 5

                def _(q, w=xn_ld, eT0=eT0, eS0=eS0, eV0=eV0, pw_=st2["mh_evac"], s=s):
                    q.wait_ge(sxn.sem, w)
                    if s == 0:
                        q.wait_ge(dL.sem, mwl)
                    order = [("w", 0), ("w", 1), ("w", 2), ("w", 3),
                             ("x", 0), ("w", 4), ("x", 1), ("w", 5),
                             ("x", 2), ("w", 6), ("x", 3), ("w", 7)]
                    for kind, i in order:
                        if kind == "w":
                            m = i
                            if m == 0:
                                q.wait_ge(eS.sem, pw_)
                            if m >= 1:
                                q.wait_ge(eS.sem, eS0 + evacpos(m-1))
                            last = None
                            for half in range(2):
                                for k8 in range(8):
                                    last = q.matmul(
                                        out=ps[6 + half][0:128, :],
                                        lhsT=winb[:, k8*1024 + m*128: k8*1024 + (m+1)*128],
                                        rhs=Xn2[:, k8*TS + half*512: k8*TS + (half+1)*512],
                                        start=(k8 == 0), stop=(k8 == 7))
                            eT.inc(last, 1)
                        else:
                            c = i
                            q.wait_ge(eV.sem, eV0 + 2*c + 2)
                            last = None
                            for half in range(2):
                                last = q.matmul(out=ps[4 + half][0:33, :],
                                                lhsT=wxb[:, c*33:(c+1)*33],
                                                rhs=xcbw[:, half*512:(half+1)*512],
                                                start=(c == 0), stop=(c == 3))
                            eT.inc(last, 1)
                te(_)
                st2["win_done"] = eT.v
                def _(q, eT0=eT0, eV0=eV0, zb=zb):
                    for m in range(8):
                        q.wait_ge(eT.sem, eT0 + winpos(m))
                        if 1 <= m < 4:
                            q.wait_ge(eV.sem, eV0 + 2*(m-1) + 1)
                        for half in range(2):
                            if m < 4:
                                ii = q.activation(out=xbwin[:, 3 + half*512: 3 + (half+1)*512],
                                                  in_=ps[6 + half][0:128, :], func=AF.Identity,
                                                  bias=cb_sb[:, m:m+1])
                            else:
                                ii = q.activation(out=zsb[:, zb + (m-4)*TS + half*512:
                                                          zb + (m-4)*TS + (half+1)*512],
                                                  in_=ps[6 + half][0:128, :], func=AF.Identity,
                                                  bias=cb_sb[:, m:m+1])
                        eS.inc(ii, 1)
                        if m < 4:
                            c = m
                            q.wait_ge(eV.sem, eV0 + 2*c + 1)
                            eS.inc(q.activation(out=xcsb[:, c*TS:(c+1)*TS],
                                                in_=xcsb[:, c*TS:(c+1)*TS], func=AF.Silu), 1)
                sc(_)
                st2["mh_evac"] = eS.v
                def _(q, eS0=eS0, eT0=eT0):
                    for c in range(4):
                        q.wait_ge(eS.sem, eS0 + 2*c + 1)
                        q.tensor_copy(out=xbwin[:, 0:3], in_=halo[:, c*3:(c+1)*3])
                        q.tensor_scalar(out=xcsb[:, c*TS:(c+1)*TS], in0=xbwin[:, 3:3+TS],
                                        scalar1=convw_sb[:, c*DC+3:c*DC+4],
                                        scalar2=convb_sb[:, c:c+1],
                                        op0=ALU.mult, op1=ALU.add)
                        for kk in (2, 1, 0):
                            q.scalar_tensor_tensor(out=xcsb[:, c*TS:(c+1)*TS],
                                                   in0=xbwin[:, kk:kk+TS],
                                                   scalar=convw_sb[:, c*DC+kk:c*DC+kk+1],
                                                   in1=xcsb[:, c*TS:(c+1)*TS],
                                                   op0=ALU.mult, op1=ALU.add)
                        eV.inc(q.tensor_copy(out=halo[:, c*3:(c+1)*3],
                                             in_=xbwin[:, TS:TS+3]), 1)
                        q.wait_ge(eS.sem, eS0 + 2*c + 2)
                        if c >= 1:
                            q.wait_ge(eT.sem, eT0 + wxpos(c-1))
                        eV.inc(q.tensor_copy(out=xcbw[:, :], in_=xcsb[:, c*TS:(c+1)*TS]), 1)
                ve(_)

                def _(q):
                    eT.wait(q)
                    q.copy(out=ssm_sb[0:33, 0:512], in_=ps[4][0:33, :])
                    eS.inc(q.copy(out=ssm_sb[0:33, 512:1024], in_=ps[5][0:33, :]), 1)
                    q.wait_ge(eS.sem, eS.v)
                    dma(q, dS, ssm_in[s, :, :], ssm_sb[0:33, 0:TS])
                sc(_)
                def _(q, s=s):
                    q.wait_ge(dS.sem, dS.v)
                    eG.inc(q.collective_compute("AllReduce", ALU.add, replica_groups=GROUPS,
                                                ins=[ssm_in[s, :, :].opt()],
                                                outs=[ssm_out[s, :, :].opt()]), 1)
                gp(_)
                ar_done = eG.v
                def _(q, s=s, w=ar_done):
                    q.wait_ge(eG.sem, w)
                    dma(q, dG, ssm_bf[s, :, :], ssm_out[s, 1:33, :])
                gp(_)
                st2["bc_cast"] = dG.v

                def _(q, s=s, w=ar_done):
                    q.wait_ge(eG.sem, w)
                    dma(q, sdtr, dtrawb[:, :], ssm_out[s, 0:1, :].broadcast_to((128, TS)))
                sy(_)
                dtr_ld = sdtr.v
                eSd0, eVd0 = eS.v, eV.v
                def _(q, w=dtr_ld, eVd0=eVd0):
                    q.wait_ge(sdtr.sem, w)
                    for c in range(4):
                        if c >= 1:
                            q.wait_ge(eV.sem, eVd0 + c)
                        q.activation(out=tmpb[:, :], in_=dtrawb[:, :], func=AF.Abs,
                                     scale=wdt_sb[:, c:c+1], bias=bdt_sb[:, c:c+1])
                        q.activation(out=tmpb[:, :], in_=tmpb[:, :], func=AF.Exp, scale=-1.0)
                        q.activation(out=tmpb[:, :], in_=tmpb[:, :], func=AF.Ln, bias=1.0)
                        eS.inc(q.activation(out=dt4[:, c*TS:(c+1)*TS], in_=dtrawb[:, :],
                                            func=AF.Relu, scale=wdt_sb[:, c:c+1],
                                            bias=bdt_sb[:, c:c+1]), 1)
                sc(_)
                def _(q, eSd0=eSd0):
                    for c in range(4):
                        q.wait_ge(eS.sem, eSd0 + c + 1)
                        q.tensor_tensor(out=dt4[:, c*TS:(c+1)*TS], in0=dt4[:, c*TS:(c+1)*TS],
                                        in1=tmpb[:, :], op=ALU.add)
                        eV.inc(q.tensor_tensor(out=dtx4[:, c*TS:(c+1)*TS],
                                               in0=dt4[:, c*TS:(c+1)*TS],
                                               in1=xcsb[:, c*TS:(c+1)*TS],
                                               op=ALU.mult), 1)
                ve(_)

            emit_mhead(0)
            for s in range(NSUB):
                eS1, eV1 = eS.v, eV.v
                bc_ld = []
                def _(q, s=s, eV1=eV1, pv=prev_scan_eV1, wc=st2["bc_cast"]):
                    q.wait_ge(dG.sem, wc)
                    for n in range(DS):
                        slot = n % 2
                        if n >= 2:
                            q.wait_ge(eV.sem, eV1 + 4*(n-2) + 4)
                        elif pv is not None:
                            q.wait_ge(eV.sem, pv + 60 + 4*n)
                        dma(q, sbc[slot], Brep[:, slot*TS:(slot+1)*TS],
                            ssm_bf[s, n:n+1, :].broadcast_to((128, TS)))
                        dma(q, sbc[slot], Crep[:, slot*TS:(slot+1)*TS],
                            ssm_bf[s, 16+n:17+n, :].broadcast_to((128, TS)))
                        bc_ld.append(sbc[slot].v)
                sy(_)
                def _(q, eS1=eS1, eV1=eV1):
                    for g in range(4 * DS):
                        n, c = g // 4, g % 4
                        eslot = g % 2
                        if g >= 2:
                            q.wait_ge(eV.sem, eV1 + g - 1)
                        eS.inc(q.activation(out=Ebuf[:, eslot*TS:(eslot+1)*TS],
                                            in_=dt4[:, c*TS:(c+1)*TS],
                                            func=AF.Exp,
                                            scale=A_sb[:, c*DS+n:c*DS+n+1]), 1)
                sc(_)
                def _(q, eS1=eS1):
                    for g in range(4 * DS):
                        n, c = g // 4, g % 4
                        slot = n % 2
                        eslot = g % 2
                        q.wait_ge(eS.sem, eS1 + g + 1)
                        q.wait_ge(sbc[n % 2].sem, bc_ld[n])
                        q.tensor_tensor(out=ubuf[:, eslot*TS:(eslot+1)*TS],
                                        in0=dtx4[:, c*TS:(c+1)*TS],
                                        in1=Brep[:, slot*TS:(slot+1)*TS], op=ALU.mult)
                        q.tensor_tensor_scan(out=hbuf[:, :],
                                             data0=Ebuf[:, eslot*TS:(eslot+1)*TS],
                                             data1=ubuf[:, eslot*TS:(eslot+1)*TS],
                                             initial=state[:, c*DS+n:c*DS+n+1],
                                             op0=ALU.mult, op1=ALU.add)
                        if n == 0:
                            q.tensor_tensor(out=acc4[:, c*TS:(c+1)*TS], in0=hbuf[:, :],
                                            in1=Crep[:, slot*TS:(slot+1)*TS], op=ALU.mult)
                        else:
                            q.tensor_tensor(out=ytmp[:, :], in0=hbuf[:, :],
                                            in1=Crep[:, slot*TS:(slot+1)*TS], op=ALU.mult)
                            q.tensor_tensor(out=acc4[:, c*TS:(c+1)*TS],
                                            in0=acc4[:, c*TS:(c+1)*TS],
                                            in1=ytmp[:, :], op=ALU.add)
                        eV.inc(q.tensor_copy(out=state[:, c*DS+n:c*DS+n+1],
                                             in_=hbuf[:, TS-1:TS]), 1)
                    for c in range(4):
                        q.scalar_tensor_tensor(out=acc4[:, c*TS:(c+1)*TS],
                                               in0=xcsb[:, c*TS:(c+1)*TS],
                                               scalar=dparam_sb[:, c:c+1],
                                               in1=acc4[:, c*TS:(c+1)*TS],
                                               op0=ALU.mult, op1=ALU.add)
                ve(_)

                prev_scan_eV1 = eV1
                if s + 1 < NSUB:
                    emit_mhead(s + 1)

                eS2, eV2 = eS.v, eV.v
                zb2 = (s % 2) * 4 * TS
                def _(q, eV2=eV2, zb2=zb2):
                    for c in range(4):
                        q.wait_ge(eV.sem, eV2 + c)
                        eS.inc(q.activation(out=evsc[:, :],
                                            in_=zsb[:, zb2 + c*TS: zb2 + (c+1)*TS],
                                            func=AF.Silu), 1)
                sc(_)
                def _(q, eS2=eS2, pw_=st2["wo_mm"]):
                    q.wait_ge(eT.sem, pw_)
                    for c in range(4):
                        q.wait_ge(eS.sem, eS2 + c + 1)
                        eV.inc(q.tensor_tensor(out=ygsb[:, c*TS:(c+1)*TS],
                                               in0=acc4[:, c*TS:(c+1)*TS],
                                               in1=evsc[:, :], op=ALU.mult), 1)
                ve(_)

                eT2, eS3, dS0, eV3 = eT.v, eS.v, dS.v, eV.v
                def _(q, eT2=eT2, eS3=eS3, eV3=eV3):
                    q.wait_ge(eV.sem, eV3)
                    for tb in range(8):
                        if tb >= 2:
                            q.wait_ge(eS.sem, eS3 + tb - 1)
                        last = None
                        for fh in range(2):
                            for c in range(4):
                                last = q.matmul(out=ps[2*(tb % 2) + fh][0:128, :],
                                                lhsT=ygsb[:, c*TS + tb*128: c*TS + (tb+1)*128],
                                                rhs=woutb[:, c*D + fh*512: c*D + (fh+1)*512],
                                                start=(c == 0), stop=(c == 3))
                        eT.inc(last, 1)
                te(_)
                st2["wo_mm"] = eT.v
                swp0 = swp.v
                def _(q, s=s, eT2=eT2, eS3=eS3, swp0=swp0):
                    for tb in range(8):
                        q.wait_ge(eT.sem, eT2 + tb + 1)
                        if tb >= 1:
                            q.wait_ge(swp.sem, swp0 + 16*tb)
                        for fh in range(2):
                            ii = q.copy(out=evsb[:, fh*512:(fh+1)*512],
                                        in_=ps[2*(tb % 2) + fh][0:128, :])
                        eS.inc(ii, 1)
                        q.wait_ge(eS.sem, eS3 + tb + 1)
                        p = s * TS + tb * 128
                        r = (p % 2048) // 512
                        row = (p // 2048) * 4 * RPC + r * RPC + (p % 512)
                        dma(q, swp, wpart[row:row+128, :], evsb[:, 0:D])
                sc(_)
                if s % 2 == 1:
                    kc = s // 2
                    def _(q, kc=kc, w=swp.v):
                        q.wait_ge(swp.sem, w)
                        eG.inc(q.collective_compute(
                            "ReduceScatter", ALU.add, replica_groups=GROUPS,
                            ins=[wpart[kc*4*RPC:(kc+1)*4*RPC, :].opt()],
                            outs=[rsout[kc*RPC:(kc+1)*RPC, :].opt()]), 1)
                        rs_done_list.append(eG.v)
                    gp(_)
            barrier()

        ffn_phase(2)

        def _(q):
            q.wait_ge(dS.sem, dS.v)
            if DBG:
                q.wait_ge(dL.sem, dL.v)
                dma(q, dG, dbg_x1[:, :], x1_dram[:, :])
                dma(q, dG, dbg_xn2[:, :], xn2_slice[:, :])
                dma(q, dG, dbg_cag[:, :], cag2.rearrange("k r d -> (k r) d"))
                dma(q, dG, dbg_ssm[:, :, :], ssm_out[:, :, :])
                dma(q, dG, dbg_rs[:, :], rsout[:, :])
                dma(q, dG, dbg_wp[:, :], wpart[:, :])
                q.wait_ge(dG.sem, dG.v)
        gp(_)

    return nc


LAST_RES = None


def kernel(**inputs):
    x = np.asarray(inputs["L0_attn_x"], np.float32)
    W_in = np.asarray(inputs["L0_attn_W_in"], np.float32)
    gam = np.asarray(inputs["L0_attn_ln_gamma"], np.float32)
    bet = np.asarray(inputs["L0_attn_ln_beta"], np.float32)
    conv_w = np.asarray(inputs["L0_attn_conv_w"], np.float32)
    conv_b = np.asarray(inputs["L0_attn_conv_b"], np.float32)
    W_x = np.asarray(inputs["L0_attn_W_x"], np.float32)
    w_dt = np.asarray(inputs["L0_attn_w_dt"], np.float32)
    b_dt = np.asarray(inputs["L0_attn_b_dt"], np.float32)
    A_log = np.asarray(inputs["L0_attn_A_log"], np.float32)
    D_param = np.asarray(inputs["L0_attn_D_param"], np.float32)
    W_out = np.asarray(inputs["L0_attn_W_out"], np.float32)
    col_emb = np.asarray(inputs["L0_ffn_col_emb"], np.float32)
    gate_w1 = np.asarray(inputs["L0_ffn_gate_w1"], np.float32)
    gate_b1 = np.asarray(inputs["L0_ffn_gate_b1"], np.float32)
    gate_w2 = np.asarray(inputs["L0_ffn_gate_w2"], np.float32)
    gate_b2 = np.asarray(inputs["L0_ffn_gate_b2"], np.float32)
    W_cols = np.asarray(inputs["L0_ffn_W_cols"], np.float32)
    b_cols = np.asarray(inputs["L0_ffn_b_cols"], np.float32)
    proj_w = np.asarray(inputs["L0_ffn_proj_w"], np.float32)
    proj_b = np.asarray(inputs["L0_ffn_proj_b"], np.float32)
    ln_g = np.asarray(inputs["L0_ffn_ln_g"], np.float32)
    ln_b = np.asarray(inputs["L0_ffn_ln_b"], np.float32)

    nc = build_nc()

    C = np.ascontiguousarray
    in_maps = []
    for core in range(NCORE):
        b = core // 4
        qq = core % 4
        ch0 = qq * CHS
        xr = C(x[b].reshape(NCHK, 4, RPC, D)[:, qq].reshape(NROWS, D))
        win_slice = np.concatenate([W_in[ch0:ch0+CHS], W_in[DI+ch0:DI+ch0+CHS]], axis=0)
        m = dict(
            xrows=xr,
            win_t=C(win_slice.T),
            gam_m=C(gam.reshape(D, 1)),
            bet_m=C(bet.reshape(D, 1)),
            wout_t=C(W_out[:, ch0:ch0+CHS].T),
            wx_t=C(W_x[:, ch0:ch0+CHS].T),
            convw=C(conv_w[ch0:ch0+CHS]),
            convb=C(conv_b[ch0:ch0+CHS].reshape(CHS, 1)),
            dparam=C(D_param[ch0:ch0+CHS].reshape(CHS, 1)),
            wdt=C(w_dt[ch0:ch0+CHS].reshape(CHS, 1)),
            bdt=C(b_dt[ch0:ch0+CHS].reshape(CHS, 1)),
            alog=C(A_log[ch0:ch0+CHS]),
            gw1=gate_w1,
            gb1=C(gate_b1.reshape(DH, 1)),
            gw2=gate_w2,
            gb2=C(gate_b2.reshape(1, C16)),
            wcols_t=C(W_cols.transpose(1, 0, 2).reshape(D, CN)),
            bcols=C(b_cols.reshape(CN, 1)),
            projw=proj_w,
            projb=C(proj_b.reshape(1, D)),
            lng=C(ln_g.reshape(1, D)),
            lnb=C(ln_b.reshape(1, D)),
            colemb=col_emb,
            colembt=C(col_emb.T),
            ident=np.eye(128, dtype=np.float32),
        )
        in_maps.append(m)

    res = run_bass_kernel_spmd(nc, in_maps, core_ids=list(range(NCORE)))
    global LAST_RES
    LAST_RES = res

    out = np.empty((B, T, D), np.float32)
    for core in range(NCORE):
        b = core // 4
        qq = core % 4
        out[b].reshape(NCHK, 4, RPC, D)[:, qq] = \
            res.results[core]["outrows"].reshape(NCHK, RPC, D)
    return out


# version 24
# speedup vs baseline: 1.0088x; 1.0088x over previous
import numpy as np
from contextlib import ExitStack

import concourse.bass as bass
import concourse.mybir as mybir
from concourse.bass_utils import run_bass_kernel_spmd

F32 = mybir.dt.float32
BF16 = mybir.dt.bfloat16
AF = mybir.ActivationFunctionType
ALU = mybir.AluOpType
AX = mybir.AxisListType

B, T, D = 2, 16384, 1024
DI, DS, DC = 2048, 16, 4
C16, NN, DH = 16, 256, 512
CN = C16 * NN
CHS = 512
NCHK = 8
RPC = 512
NPT = 4
NSUB = 16
TS = 1024
EPS = 1e-5
GROUPS = [[0, 1, 2, 3], [4, 5, 6, 7]]
NROWS = NCHK * RPC
NCORE = 8
DBG = False
DBG_PHASE = 1


class Ctr:
    def __init__(self, stack, nc, name):
        self.sem = stack.enter_context(nc.semaphore(name))
        self.v = 0

    def inc(self, instr, n=1):
        instr.then_inc(self.sem, n)
        self.v += n

    def wait(self, eng, v=None):
        eng.wait_ge(self.sem, self.v if v is None else v)


def build_nc():
    nc = bass.Bass("TRN2", target_bir_lowering=False, debug=False)

    def din(name, shape):
        return nc.dram_tensor(name, shape, F32, kind="ExternalInput").ap()

    xrows = din("xrows", [NROWS, D])
    win_t = din("win_t", [D, 2 * CHS])
    gam_m = din("gam_m", [D, 1])
    bet_m = din("bet_m", [D, 1])
    wout_t = din("wout_t", [CHS, D])
    wx_t = din("wx_t", [CHS, 33])
    convw = din("convw", [CHS, DC])
    convb = din("convb", [CHS, 1])
    dparam = din("dparam", [CHS, 1])
    wdt = din("wdt", [CHS, 1])
    bdt = din("bdt", [CHS, 1])
    alog = din("alog", [CHS, DS])
    gw1 = din("gw1", [D, DH])
    gb1 = din("gb1", [DH, 1])
    gw2 = din("gw2", [DH, C16])
    gb2 = din("gb2", [1, C16])
    wcols_t = din("wcols_t", [D, CN])
    bcols = din("bcols", [CN, 1])
    projw = din("projw", [CN, D])
    projb = din("projb", [1, D])
    lng = din("lng", [1, D])
    lnb = din("lnb", [1, D])
    colemb = din("colemb", [C16, D])
    colembt = din("colembt", [D, C16])
    ident = din("ident", [128, 128])

    outrows = nc.dram_tensor("outrows", [NROWS, D], F32, kind="ExternalOutput").ap()
    dbgt = {}
    if DBG:
        for nm, cols, dt in [("u1b", NPT * D, BF16), ("st1", NPT * 12, F32),
                             ("U1", 8 * RPC, BF16), ("h1b", 4 * RPC, BF16),
                             ("wtok", NPT * C16, F32), ("lgt", NPT * C16, F32),
                             ("colsw", 32 * RPC, BF16), ("ybraw", NPT * D, F32),
                             ("yb2", NPT * D, F32), ("x0f", NPT * D, F32)]:
            dbgt[nm] = nc.dram_tensor("dbg_" + nm, [128, cols], dt,
                                      kind="ExternalOutput").ap()
        dbg_x1 = nc.dram_tensor("dbg_x1", [NROWS, D], F32, kind="ExternalOutput").ap()
        dbg_xn2 = nc.dram_tensor("dbg_xn2", [NROWS, D], BF16, kind="ExternalOutput").ap()
        dbg_cag = nc.dram_tensor("dbg_cag", [4 * NROWS, D], BF16, kind="ExternalOutput").ap()
        dbg_ssm = nc.dram_tensor("dbg_ssm", [NSUB, 33, TS], F32, kind="ExternalOutput").ap()
        dbg_rs = nc.dram_tensor("dbg_rs", [NROWS, D], F32, kind="ExternalOutput").ap()
        dbg_wp = nc.dram_tensor("dbg_wp", [4 * NROWS, D], F32, kind="ExternalOutput").ap()

    wcols_bf2 = nc.dram_tensor("wcols_bf2", [32 * 128, 8 * 128], BF16).ap()
    projw_bf = nc.dram_tensor("projw_bf", [CN, D], BF16).ap()
    u1b_dram = nc.dram_tensor("u1b_dram", [RPC, D], BF16).ap()
    xn2_slice = nc.dram_tensor("xn2_slice", [NROWS, D], BF16).ap()
    cag2 = nc.dram_tensor("cag2", [NCHK, 4 * RPC, D], BF16).ap()
    x1_dram = nc.dram_tensor("x1_dram", [NROWS, D], F32).ap()
    ssm_in = nc.dram_tensor("ssm_in", [NSUB, 33, TS], F32).ap()
    ssm_out = nc.dram_tensor("ssm_out", [NSUB, 33, TS], F32).ap()
    ssm_bf = nc.dram_tensor("ssm_bf", [NSUB, 32, TS], BF16).ap()
    winb_dram = nc.dram_tensor("winb_dram", [128, 8 * 1024], BF16).ap()
    woutb_dram = nc.dram_tensor("woutb_dram", [128, 4 * D + 4 * 33], BF16).ap()
    wpart = nc.dram_tensor("wpart", [4 * NROWS, D], BF16).ap()
    rsout = nc.dram_tensor("rsout", [NROWS, D], BF16).ap()
    wfm_dram = nc.dram_tensor("wfm_dram", [C16, RPC], BF16).ap()
    colr_dram = nc.dram_tensor("colr_dram", [C16, 1], F32).ap()

    with ExitStack() as perm:
        E = perm.enter_context
        block = E(nc.Block())

        def sy(fn): block.sync(fn)
        def ve(fn): block.vector(fn)
        def sc(fn): block.scalar(fn)
        def te(fn): block.tensor(fn)
        def gp(fn): block.gpsimd(fn)

        dL = Ctr(perm, nc, "dL")
        sx0 = Ctr(perm, nc, "sx0")
        su1 = Ctr(perm, nc, "su1")
        swc = [Ctr(perm, nc, f"swc{i}") for i in range(4)]
        spw = [Ctr(perm, nc, f"spw{i}") for i in range(4)]
        swr = [Ctr(perm, nc, f"swr{i}") for i in range(4)]
        sxs = [Ctr(perm, nc, "sxsA"), Ctr(perm, nc, "sxsB")]
        swp = Ctr(perm, nc, "swp")
        sxn = Ctr(perm, nc, "sxn")
        sdtr = Ctr(perm, nc, "sdtr")
        sbc = [Ctr(perm, nc, "sbcA"), Ctr(perm, nc, "sbcB")]
        dS = Ctr(perm, nc, "dS")
        dG = Ctr(perm, nc, "dG")
        eV = Ctr(perm, nc, "eV")
        eS = Ctr(perm, nc, "eS")
        eT = Ctr(perm, nc, "eT")
        eG = Ctr(perm, nc, "eG")
        bar = Ctr(perm, nc, "bar")

        def dma(q, ctr, out, in_, **kw):
            ctr.inc(q.dma_start(out=out, in_=in_, **kw), 16)
            return ctr.v

        def dump(name, sb, prod_ctr):
            if not DBG:
                return
            snap = prod_ctr.v
            def _(q, snap=snap, name=name, sb=sb, ctr=prod_ctr):
                q.wait_ge(ctr.sem, snap)
                dma(q, dS, dbgt[name][:, :], sb)
                q.wait_ge(dS.sem, dS.v)
            sy(_)
            barrier()

        def barrier():
            def _(q):
                for c in (dS, dL, dG, sxs[0], sxs[1], swp, sx0, su1, sxn):
                    q.wait_ge(c.sem, c.v)
            sy(_)
            for emit in (sy, ve, sc, te, gp):
                def _(q):
                    q.drain()
                    q.sem_inc(bar.sem, 1)
                emit(_)
            bar.v += 5
            tgt = bar.v
            for emit in (sy, ve, sc, te, gp):
                def _(q, tgt=tgt):
                    q.wait_ge(bar.sem, tgt)
                emit(_)

        convw_sb = E(nc.sbuf_tensor("convw_sb", [128, 4 * DC], F32))
        convb_sb = E(nc.sbuf_tensor("convb_sb", [128, 4], F32))
        dparam_sb = E(nc.sbuf_tensor("dparam_sb", [128, 4], F32))
        wdt_sb = E(nc.sbuf_tensor("wdt_sb", [128, 4], F32))
        bdt_sb = E(nc.sbuf_tensor("bdt_sb", [128, 4], F32))
        A_sb = E(nc.sbuf_tensor("A_sb", [128, 4 * DS], F32))
        gb1_sb = E(nc.sbuf_tensor("gb1_sb", [128, 4], F32))
        bcols_sb = E(nc.sbuf_tensor("bcols_sb", [128, 32], F32))
        cb_sb = E(nc.sbuf_tensor("cb_sb", [128, 8], F32))
        pbb = E(nc.sbuf_tensor("pbb", [128, D], F32))
        gls05 = E(nc.sbuf_tensor("gls05", [128, D], F32))
        bls05 = E(nc.sbuf_tensor("bls05", [128, D], F32))
        b2b = E(nc.sbuf_tensor("b2b", [128, C16], F32))
        colrb = E(nc.sbuf_tensor("colrb", [128, C16], F32))
        idsb = E(nc.sbuf_tensor("idsb", [128, 128], F32))
        g1b = E(nc.sbuf_tensor("g1b", [128, 8 * DH], BF16))
        g2b = E(nc.sbuf_tensor("g2b", [128, 4 * C16], BF16))
        cetb = E(nc.sbuf_tensor("cetb", [128, 8 * C16], BF16))
        stats3 = E(nc.sbuf_tensor("stats3", [128, NCHK * NPT * 2], F32))
        state = E(nc.sbuf_tensor("state", [128, 4 * DS], F32))
        gam_sb = E(nc.sbuf_tensor("gam_sb", [128, 8], F32))
        bet_sb = E(nc.sbuf_tensor("bet_sb", [128, 8], F32))
        epsc = E(nc.sbuf_tensor("epsc", [128, 1], F32))
        nc.const_aps.register(F32, EPS, epsc[:, 0:1])

        ps = [E(nc.psum_tensor(f"ps{i}", [128, 512], F32)) for i in range(8)]

        with ExitStack() as ph0:
            E0 = ph0.enter_context
            wscr = E0(nc.sbuf_tensor("wscr", [128, 8 * 1024], F32))
            sscr = E0(nc.sbuf_tensor("sscr", [128, 1024], F32))
            wbf = E0(nc.sbuf_tensor("wbf", [128, 8 * 1024], BF16))

            def _(q):
                for m in range(32):
                    dma(q, dG,
                        wcols_bf2[m*128:(m+1)*128, :].rearrange("p (g q) -> p g q", g=8),
                        wcols_t[:, m*128:(m+1)*128].rearrange("(g p) q -> p g q", p=128))
                dma(q, dG, projw_bf[:, :], projw[:, :])
            gp(_)

            def _(q):
                eV.inc(q.memset(epsc[:, :], EPS), 1)
            ve(_)

            def _(q):
                for c in range(4):
                    dma(q, dL, convw_sb[:, c*DC:(c+1)*DC], convw[c*128:(c+1)*128, :])
                    dma(q, dL, convb_sb[:, c:c+1], convb[c*128:(c+1)*128, :])
                    dma(q, dL, dparam_sb[:, c:c+1], dparam[c*128:(c+1)*128, :])
                    dma(q, dL, wdt_sb[:, c:c+1], wdt[c*128:(c+1)*128, :])
                    dma(q, dL, bdt_sb[:, c:c+1], bdt[c*128:(c+1)*128, :])
                    dma(q, dL, A_sb[:, c*DS:(c+1)*DS], alog[c*128:(c+1)*128, :])
                    dma(q, dL, gb1_sb[:, c:c+1], gb1[c*128:(c+1)*128, :])
                for m in range(32):
                    dma(q, dL, bcols_sb[:, m:m+1], bcols[m*128:(m+1)*128, :])
                dma(q, dL, pbb[:, :], projb[0:1, :].broadcast_to((128, D)))
                dma(q, dL, gls05[:, :], lng[0:1, :].broadcast_to((128, D)))
                dma(q, dL, bls05[:, :], lnb[0:1, :].broadcast_to((128, D)))
                dma(q, dL, b2b[:, :], gb2[0:1, :].broadcast_to((128, C16)))
                dma(q, dL, idsb[:, :], ident[:, :])
                for k in range(8):
                    dma(q, dL, gam_sb[:, k:k+1], gam_m[k*128:(k+1)*128, :])
                    dma(q, dL, bet_sb[:, k:k+1], bet_m[k*128:(k+1)*128, :])
                for k in range(8):
                    dma(q, dL, wscr[:, k*1024:(k+1)*1024], win_t[k*128:(k+1)*128, :])
                dma(q, dL, sscr[0:16, 0:1024], colemb[:, :])
            sy(_)
            ld_done = dL.v

            def _(q, w=ld_done):
                q.wait_ge(dL.sem, w)
                last = None
                for k in range(8):
                    for m in range(8):
                        last = q.matmul(out=ps[6][0:128, m:m+1],
                                        lhsT=wscr[:, k*1024 + m*128:k*1024 + (m+1)*128],
                                        rhs=bet_sb[:, k:k+1],
                                        start=(k == 0), stop=(k == 7))
                eT.inc(last, 1)
            te(_)
            cbeta_done = eT.v

            def _(q, w=ld_done):
                q.wait_ge(dL.sem, w)
                eV.inc(q.tensor_tensor(out=sscr[0:16, 0:1024], in0=sscr[0:16, 0:1024],
                                       in1=sscr[0:16, 0:1024], op=ALU.mult), 1)
            ve(_)
            def _(q):
                eV.wait(q)
                q.tensor_reduce(out=colrb[0:16, 0:1], in_=sscr[0:16, 0:1024], op=ALU.add,
                                axis=AX.X)
                q.drain()
                q.tensor_scalar_max(out=colrb[0:16, 0:1], in0=colrb[0:16, 0:1],
                                    scalar1=1e-24)
                eV.inc(q.drain(), 1)
            ve(_)
            def _(q):
                eV.wait(q)
                q.activation(out=colrb[0:16, 1:2], in_=colrb[0:16, 0:1], func=AF.Ln)
                q.drain()
                q.activation(out=colrb[0:16, 0:1], in_=colrb[0:16, 1:2], func=AF.Exp, scale=-0.5)
                q.drain()
                eS.inc(q.activation(out=A_sb[:, :], in_=A_sb[:, :], func=AF.Exp), 1)
            sc(_)
            def _(q):
                eS.wait(q)
                eV.inc(q.tensor_scalar_mul(out=A_sb[:, :], in0=A_sb[:, :], scalar1=-1.0), 1)
            ve(_)
            def _(q):
                eS.wait(q)
                dma(q, dS, colr_dram[:, :], colrb[0:16, 0:1])
            sc(_)
            def _(q):
                dS.wait(q)
                dma(q, dL, colrb[:, :],
                    colr_dram.rearrange("a b -> b a").broadcast_to((128, C16)))
            sy(_)

            def _(q, w=cbeta_done):
                q.wait_ge(eT.sem, w)
                eS.inc(q.copy(out=cb_sb[:, 0:8], in_=ps[6][0:128, 0:8]), 1)
            sc(_)
            def _(q, w=cbeta_done):
                q.wait_ge(eT.sem, w)
                for k in range(8):
                    q.tensor_scalar_mul(out=wscr[:, k*1024:(k+1)*1024],
                                        in0=wscr[:, k*1024:(k+1)*1024],
                                        scalar1=gam_sb[:, k:k+1])
                eV.inc(q.tensor_copy(out=wbf[:, :], in_=wscr[:, :]), 1)
            ve(_)
            win_cast = eV.v
            def _(q, w=win_cast):
                q.wait_ge(eV.sem, w)
                dma(q, dS, winb_dram[:, :], wbf[:, :])
            sc(_)
            winb_st = dS.v

            def _(q, w=win_cast):
                q.wait_ge(eV.sem, w)
                for c in range(4):
                    dma(q, dL, wscr[:, c*1024:(c+1)*1024], wout_t[c*128:(c+1)*128, :])
                for c in range(4):
                    dma(q, dL, sscr[:, c*33:(c+1)*33], wx_t[c*128:(c+1)*128, :])
            sy(_)
            w2 = dL.v
            def _(q, w=w2, ws=winb_st):
                q.wait_ge(dL.sem, w)
                q.wait_ge(dS.sem, ws)
                q.tensor_copy(out=wbf[:, 0:4*D], in_=wscr[:, 0:4*D])
                q.tensor_copy(out=wbf[:, 4*D:4*D + 4*33], in_=sscr[:, 0:4*33])
                q.tensor_scalar_mul(out=gls05[:, :], in0=gls05[:, :], scalar1=0.5)
                eV.inc(q.tensor_scalar_mul(out=bls05[:, :], in0=bls05[:, :], scalar1=0.5), 1)
            ve(_)
            w3v = eV.v
            def _(q, w=w3v):
                q.wait_ge(eV.sem, w)
                dma(q, dS, woutb_dram[:, :], wbf[:, 0:4*D + 4*33])
            sc(_)
            def _(q, w=w3v):
                q.wait_ge(eV.sem, w)
                for k in range(8):
                    dma(q, dL, wscr[:, k*512:(k+1)*512], gw1[k*128:(k+1)*128, :])
                for k in range(4):
                    dma(q, dL, sscr[:, k*C16:(k+1)*C16], gw2[k*128:(k+1)*128, :])
                for k in range(8):
                    dma(q, dL, sscr[:, 128 + k*C16:128 + (k+1)*C16],
                        colembt[k*128:(k+1)*128, :])
            sy(_)
            w4 = dL.v
            def _(q, w=w4):
                q.wait_ge(dL.sem, w)
                q.tensor_copy(out=g1b[:, :], in_=wscr[:, 0:8*DH])
                q.tensor_copy(out=g2b[:, :], in_=sscr[:, 0:4*C16])
                q.tensor_copy(out=cetb[:, :], in_=sscr[:, 128:128 + 8*C16])
                eV.inc(q.memset(state[:, :], 0.0), 1)
            ve(_)
            def _(q):
                dG.wait(q)
            sy(_)
            barrier()

        ag_done_list = []
        rs_done_list = []
        def ffn_phase(phase):
            with ExitStack() as ph:
                Ep = ph.enter_context
                sfx = f"_p{phase}"
                x0 = Ep(nc.sbuf_tensor("x0" + sfx, [128, 2 * NPT * D], F32))
                xex = Ep(nc.sbuf_tensor("xex" + sfx, [128, 2 * NPT * D], F32))
                yb = Ep(nc.sbuf_tensor("yb" + sfx, [128, NPT * D], F32))
                U1 = Ep(nc.sbuf_tensor("U1" + sfx, [128, 8 * RPC], BF16))
                u1b = Ep(nc.sbuf_tensor("u1b" + sfx, [128, 2 * NPT * D], BF16))
                h1b = Ep(nc.sbuf_tensor("h1b" + sfx, [128, 4 * RPC], BF16))
                colsw = Ep(nc.sbuf_tensor("colsw" + sfx, [128, 32 * RPC], BF16))
                colsg = Ep(nc.sbuf_tensor("colsg" + sfx, [128, 2 * RPC], BF16))
                wcpan = Ep(nc.sbuf_tensor("wcpan" + sfx, [128, 4 * 1024], BF16))
                pwwin = Ep(nc.sbuf_tensor("pwwin" + sfx, [128, 4 * D], BF16))
                wrep = Ep(nc.sbuf_tensor("wrep" + sfx, [128, 4 * RPC], BF16))
                wtok = Ep(nc.sbuf_tensor("wtok" + sfx, [128, NPT * C16], F32))
                lgt = Ep(nc.sbuf_tensor("lgt" + sfx, [128, NPT * C16], F32))
                sgt = Ep(nc.sbuf_tensor("sgt" + sfx, [128, NPT * C16], F32))
                wfm = Ep(nc.sbuf_tensor("wfm" + sfx, [C16, RPC], BF16))
                st1 = Ep(nc.sbuf_tensor("st1" + sfx, [128, 2 * NPT * 12], F32))
                rstg = Ep(nc.sbuf_tensor("rstg" + sfx, [128, NPT * D], BF16))

                def stc(P, pt, i):
                    o = (P * NPT + pt) * 12 + i
                    return st1[:, o:o+1]

                def xs(P, pt):
                    return x0[:, (P * NPT + pt)*D:(P * NPT + pt + 1)*D]

                def xe(P, pt):
                    return xex[:, (P * NPT + pt)*D:(P * NPT + pt + 1)*D]

                def us(P, pt):
                    return u1b[:, (P * NPT + pt)*D:(P * NPT + pt + 1)*D]

                st = dict(prev_xn2_store=None, prev_xn2_par=0,
                          prev_proj_mm=eT.v, prev_proj_evac=eS.v,
                          prev_colsw_last=eV.v, prev_cols_mm_last=eT.v,
                          prev_stores=dS.v, u1ld=0, u1st=0)

                def emit_head_load(j):
                    P = j % 2
                    r0 = j * RPC
                    def _(q, r0=r0, d=st["prev_stores"], j=j, P=P):
                        q.wait_ge(dS.sem, d)
                        if phase == 2:
                            q.wait_ge(eG.sem, rs_done_list[j])
                        if phase == 1:
                            dma(q, sx0,
                                x0[:, P*NPT*D:(P+1)*NPT*D].rearrange(
                                    "p (a d) -> p a d", a=NPT),
                                xrows[r0:r0+RPC, :].rearrange("(a p) d -> p a d", p=128))
                        else:
                            dma(q, sx0,
                                xex[:, P*NPT*D:(P+1)*NPT*D].rearrange(
                                    "p (a d) -> p a d", a=NPT),
                                x1_dram[r0:r0+RPC, :].rearrange("(a p) d -> p a d", p=128))
                            dma(q, sx0,
                                rstg[:, :].rearrange("p (a d) -> p a d", a=NPT),
                                rsout[r0:r0+RPC, :].rearrange("(a p) d -> p a d", p=128))
                    gp(_)
                    st["xload"] = sx0.v

                def emit_head_rest(j):
                    P = j % 2
                    xload = st["xload"]
                    if phase == 2:
                        def _(q, j=j, P=P, w=xload):
                            q.wait_ge(sx0.sem, w)
                            for pt in range(NPT):
                                scol = (j * NPT + pt) * 2
                                q.tensor_tensor(out=stc(P, pt, 11),
                                                in0=stats3[:, scol:scol+1],
                                                in1=stats3[:, scol+1:scol+2],
                                                op=ALU.mult)
                            q.drain()
                            for pt in range(NPT):
                                q.tensor_scalar_mul(out=stc(P, pt, 11),
                                                    in0=stc(P, pt, 11), scalar1=-1.0)
                            q.drain()
                            for pt in range(NPT):
                                scol = (j * NPT + pt) * 2
                                r3 = stats3[:, scol+1:scol+2]
                                q.scalar_tensor_tensor(out=xs(P, pt), in0=xe(P, pt),
                                                       scalar=r3,
                                                       in1=rstg[:, pt*D:(pt+1)*D],
                                                       op0=ALU.mult, op1=ALU.add)
                            for pt in range(NPT):
                                q.tensor_scalar(out=xs(P, pt), in0=xs(P, pt),
                                                scalar1=stc(P, pt, 11), scalar2=None,
                                                op0=ALU.add)
                            for pt in range(NPT):
                                q.tensor_tensor(out=xs(P, pt), in0=xs(P, pt),
                                                in1=xe(P, pt), op=ALU.add)
                            eV.inc(q.drain(), 1)
                        ve(_)

                    def _(q, w=xload, P=P):
                        q.wait_ge(sx0.sem, w)
                        eV.wait(q)
                        for pt in range(NPT):
                            ii = q.activation(out=xe(P, pt), in_=xs(P, pt),
                                              func=AF.Square)
                        eS.inc(ii, 1)
                    sc(_)
                    sq1 = eS.v
                    def _(q, w=xload, w2=sq1, P=P):
                        q.wait_ge(sx0.sem, w)
                        q.wait_ge(eS.sem, w2)
                        eV.wait(q)
                        for pt in range(NPT):
                            q.tensor_reduce(out=stc(P, pt, 0), in_=xs(P, pt),
                                            op=ALU.add, axis=AX.X)
                        for pt in range(NPT):
                            q.tensor_reduce(out=stc(P, pt, 1), in_=xe(P, pt),
                                            op=ALU.add, axis=AX.X)
                        for pt in range(NPT):
                            q.tensor_scalar_mul(out=stc(P, pt, 2), in0=stc(P, pt, 0),
                                                scalar1=1.0 / D)
                        for pt in range(NPT):
                            q.tensor_scalar_mul(out=stc(P, pt, 3), in0=stc(P, pt, 1),
                                                scalar1=1.0 / D)
                        q.drain()
                        for pt in range(NPT):
                            q.tensor_tensor(out=stc(P, pt, 11), in0=stc(P, pt, 2),
                                            in1=stc(P, pt, 2), op=ALU.mult)
                        q.drain()
                        for pt in range(NPT):
                            q.tensor_tensor(out=stc(P, pt, 3), in0=stc(P, pt, 3),
                                            in1=stc(P, pt, 11), op=ALU.subtract)
                        eV.inc(q.drain(), 1)
                    ve(_)
                    def _(q, P=P):
                        eV.wait(q)
                        for pt in range(NPT):
                            q.activation(out=stc(P, pt, 11), in_=stc(P, pt, 3),
                                         func=AF.Ln, bias=EPS)
                        q.drain()
                        for pt in range(NPT):
                            q.activation(out=stc(P, pt, 4), in_=stc(P, pt, 11),
                                         func=AF.Exp, scale=-0.5)
                        eS.inc(q.drain(), 1)
                    sc(_)
                    def _(q, P=P):
                        eS.wait(q)
                        for pt in range(NPT):
                            q.tensor_tensor(out=stc(P, pt, 5), in0=stc(P, pt, 2),
                                            in1=stc(P, pt, 4), op=ALU.mult)
                        for pt in range(NPT):
                            q.tensor_tensor(out=stc(P, pt, 6), in0=stc(P, pt, 4),
                                            in1=stc(P, pt, 4), op=ALU.mult)
                        q.drain()
                        for pt in range(NPT):
                            q.tensor_scalar_mul(out=stc(P, pt, 5), in0=stc(P, pt, 5),
                                                scalar1=-1.0)
                        for pt in range(NPT):
                            q.tensor_tensor(out=stc(P, pt, 6), in0=stc(P, pt, 6),
                                            in1=stc(P, pt, 3), op=ALU.mult)
                        q.drain()
                        for pt in range(NPT):
                            q.tensor_scalar_mul(out=stc(P, pt, 6), in0=stc(P, pt, 6),
                                                scalar1=float(D))
                        q.drain()
                        for pt in range(NPT):
                            q.tensor_scalar_max(out=stc(P, pt, 6), in0=stc(P, pt, 6),
                                                scalar1=1e-24)
                        eV.inc(q.drain(), 1)
                    ve(_)
                    def _(q, P=P, xw=st["prev_xn2_store"], xp=st["prev_xn2_par"]):
                        eV.wait(q)
                        if phase == 1 and xw is not None:
                            q.wait_ge(sxs[xp].sem, xw)
                        for pt in range(NPT):
                            q.activation(out=stc(P, pt, 11), in_=stc(P, pt, 6),
                                         func=AF.Ln)
                        q.drain()
                        for pt in range(NPT):
                            q.activation(out=stc(P, pt, 6), in_=stc(P, pt, 11),
                                         func=AF.Exp, scale=-0.5)
                        for pt in range(NPT):
                            q.activation(out=us(P, pt), in_=xs(P, pt),
                                         func=AF.Identity, scale=stc(P, pt, 4),
                                         bias=stc(P, pt, 5))
                        eS.inc(q.drain(), 1)
                    sc(_)

                    def _(q, P=P):
                        eS.wait(q)
                        dma(q, dS, u1b_dram.rearrange("(a p) d -> p a d", p=128),
                            u1b[:, P*NPT*D:(P+1)*NPT*D].rearrange("p (a d) -> p a d",
                                                                  a=NPT))
                    sc(_)
                    st["u1st"] = dS.v
                    def _(q, w=st["u1st"], pt_=st["prev_cols_mm_last"]):
                        q.wait_ge(dS.sem, w)
                        q.wait_ge(eT.sem, pt_)
                        for dt8 in range(8):
                            dma(q, su1, U1[:, dt8*RPC:(dt8+1)*RPC],
                                u1b_dram[:, dt8*128:(dt8+1)*128], transpose=True)
                    sy(_)
                    st["u1ld"] = su1.v

                def emit_gates(j):
                    P = j % 2
                    def _(q, w=st["u1ld"], pe=st["prev_proj_evac"]):
                        q.wait_ge(su1.sem, w)
                        q.wait_ge(eS.sem, pe)
                        last = None
                        for m4 in range(4):
                            for kk in range(8):
                                last = q.matmul(out=ps[m4][0:128, :],
                                                lhsT=g1b[:, kk*DH + m4*128: kk*DH + (m4+1)*128],
                                                rhs=U1[:, kk*RPC:(kk+1)*RPC],
                                                start=(kk == 0), stop=(kk == 7))
                        eT.inc(last, 1)
                    te(_)
                    g1t = eT.v
                    def _(q, w=g1t):
                        q.wait_ge(eT.sem, w)
                        for m4 in range(4):
                            ii = q.activation(out=h1b[:, m4*RPC:(m4+1)*RPC],
                                              in_=ps[m4][0:128, :],
                                              func=AF.Gelu, bias=gb1_sb[:, m4:m4+1])
                        eS.inc(ii, 1)
                    sc(_)

                    def _(q):
                        eS.wait(q)
                        last = None
                        for tb in range(NPT):
                            for k4 in range(4):
                                q.matmul(out=ps[tb][0:128, 0:C16],
                                         lhsT=h1b[:, k4*RPC + tb*128: k4*RPC + (tb+1)*128],
                                         rhs=g2b[:, k4*C16:(k4+1)*C16],
                                         start=(k4 == 0), stop=(k4 == 3))
                            for k8 in range(8):
                                last = q.matmul(out=ps[4+tb][0:128, 0:C16],
                                                lhsT=U1[:, k8*RPC + tb*128: k8*RPC + (tb+1)*128],
                                                rhs=cetb[:, k8*C16:(k8+1)*C16],
                                                start=(k8 == 0), stop=(k8 == 7))
                        eT.inc(last, 1)
                    te(_)
                    g2t = eT.v
                    def _(q, w=g2t, P=P):
                        q.wait_ge(eT.sem, w)
                        for tb in range(NPT):
                            q.tensor_tensor(out=sgt[:, tb*C16:(tb+1)*C16],
                                            in0=ps[tb][0:128, 0:C16], in1=b2b[:, :],
                                            op=ALU.add)
                        for tb in range(NPT):
                            q.tensor_tensor(out=lgt[:, tb*C16:(tb+1)*C16],
                                            in0=ps[4+tb][0:128, 0:C16], in1=colrb[:, :],
                                            op=ALU.mult)
                        q.drain()
                        for tb in range(NPT):
                            q.tensor_scalar_mul(out=lgt[:, tb*C16:(tb+1)*C16],
                                                in0=lgt[:, tb*C16:(tb+1)*C16],
                                                scalar1=stc(P, tb, 6))
                        eV.inc(q.drain(), 1)
                    ve(_)
                    def _(q):
                        eV.wait(q)
                        for tb in range(NPT):
                            q.activation(out=sgt[:, tb*C16:(tb+1)*C16],
                                         in_=sgt[:, tb*C16:(tb+1)*C16], func=AF.Sigmoid)
                        eS.inc(q.drain(), 1)
                    sc(_)
                    def _(q, P=P):
                        eS.wait(q)
                        for tb in range(NPT):
                            sl = lgt[:, tb*C16:(tb+1)*C16]
                            q.tensor_tensor(out=sl, in0=sl, in1=sgt[:, tb*C16:(tb+1)*C16],
                                            op=ALU.add)
                        q.drain()
                        for tb in range(NPT):
                            q.tensor_reduce(out=stc(P, tb, 11),
                                            in_=lgt[:, tb*C16:(tb+1)*C16],
                                            op=ALU.max, axis=AX.X)
                        q.drain()
                        for tb in range(NPT):
                            q.tensor_single_scalar(out=wtok[:, tb*C16:(tb+1)*C16],
                                                   in_=lgt[:, tb*C16:(tb+1)*C16],
                                                   scalar=stc(P, tb, 11), op=ALU.is_equal)
                        q.drain()
                        for tb in range(NPT):
                            q.tensor_scalar_mul(out=stc(P, tb, 11), in0=stc(P, tb, 11),
                                                scalar1=-1.0)
                        eV.inc(q.drain(), 1)
                    ve(_)
                    def _(q, P=P):
                        eV.wait(q)
                        for tb in range(NPT):
                            q.activation(out=sgt[:, tb*C16:(tb+1)*C16],
                                         in_=lgt[:, tb*C16:(tb+1)*C16],
                                         func=AF.Exp, bias=stc(P, tb, 11))
                        eS.inc(q.drain(), 1)
                    sc(_)
                    def _(q, P=P):
                        eS.wait(q)
                        for tb in range(NPT):
                            q.tensor_reduce(out=stc(P, tb, 11),
                                            in_=sgt[:, tb*C16:(tb+1)*C16],
                                            op=ALU.add, axis=AX.X)
                        q.drain()
                        for tb in range(NPT):
                            q.reciprocal(out=stc(P, tb, 11), in_=stc(P, tb, 11))
                        for tb in range(NPT):
                            q.tensor_tensor(out=wtok[:, tb*C16:(tb+1)*C16],
                                            in0=wtok[:, tb*C16:(tb+1)*C16],
                                            in1=sgt[:, tb*C16:(tb+1)*C16], op=ALU.mult)
                        q.drain()
                        for tb in range(NPT):
                            q.tensor_scalar_mul(out=wtok[:, tb*C16:(tb+1)*C16],
                                                in0=wtok[:, tb*C16:(tb+1)*C16],
                                                scalar1=stc(P, tb, 11))
                        eV.inc(q.drain(), 1)
                    ve(_)

                    def _(q):
                        eV.wait(q)
                        last = None
                        for tb in range(NPT):
                            last = q.transpose(out=ps[6][0:C16, tb*128:(tb+1)*128],
                                               in_=wtok[:, tb*C16:(tb+1)*C16],
                                               identity=idsb[:, :])
                        eT.inc(last, 1)
                    te(_)
                    wtr = eT.v
                    def _(q, w=wtr):
                        q.wait_ge(eT.sem, w)
                        eS.inc(q.copy(out=wfm[:, :], in_=ps[6][0:C16, 0:RPC]), 1)
                    sc(_)
                    def _(q):
                        eS.wait(q)
                        dma(q, dS, wfm_dram[:, :], wfm[:, :])
                    sc(_)
                    st["wfm_st"] = dS.v

                def emit_cols(j):
                    eS0, eV0 = eS.v, eV.v
                    st["eS0"], st["eV0"] = eS0, eV0
                    pan_ld, pan_used, wrep_ld = [], [], []

                    def _(q, w=st["wfm_st"], eV0=eV0, pv=st["prev_colsw_last"]):
                        q.wait_ge(dS.sem, w)
                        for c in range(C16):
                            if c >= 4:
                                q.wait_ge(eV.sem, eV0 + 2*(c-4) + 2)
                            else:
                                q.wait_ge(eV.sem, pv)
                            dma(q, swr[c % 4], wrep[:, (c % 4)*RPC:((c % 4)+1)*RPC],
                                wfm_dram[c:c+1, :].broadcast_to((128, RPC)))
                            wrep_ld.append(swr[c % 4].v)
                    gp(_)
                    def _(q, eT0=eT.v, pc=st["prev_cols_mm_last"]):
                        for m in range(32):
                            if m >= 4:
                                q.wait_ge(eT.sem, eT0 + m - 3)
                            else:
                                q.wait_ge(eT.sem, pc)
                            dma(q, swc[m % 4], wcpan[:, (m % 4)*1024:((m % 4)+1)*1024],
                                wcols_bf2[m*128:(m+1)*128, :])
                            pan_ld.append(swc[m % 4].v)
                    sy(_)
                    del _

                    def _(q):
                        for m in range(32):
                            q.wait_ge(swc[m % 4].sem, pan_ld[m])
                            if m >= 2:
                                q.wait_ge(eS.sem, eS0 + m - 1)
                            last = None
                            for kk in range(8):
                                last = q.matmul(
                                    out=ps[m % 2][0:128, :],
                                    lhsT=wcpan[:, (m % 4)*1024 + kk*128:
                                               (m % 4)*1024 + (kk+1)*128],
                                    rhs=U1[:, kk*RPC:(kk+1)*RPC],
                                    start=(kk == 0), stop=(kk == 7))
                            eT.inc(last, 1)
                            pan_used.append(eT.v)
                    te(_)
                    def _(q):
                        for m in range(32):
                            q.wait_ge(eT.sem, pan_used[m])
                            if m >= 2:
                                q.wait_ge(eV.sem, eV0 + m - 1)
                            eS.inc(q.activation(out=colsg[:, (m % 2)*RPC:((m % 2)+1)*RPC],
                                                in_=ps[m % 2][0:128, :], func=AF.Gelu,
                                                bias=bcols_sb[:, m:m+1]), 1)
                    sc(_)
                    def _(q):
                        for m in range(32):
                            c = m // 2
                            q.wait_ge(eS.sem, eS0 + m + 1)
                            q.wait_ge(swr[c % 4].sem, wrep_ld[c])
                            eV.inc(q.tensor_tensor(out=colsw[:, m*RPC:(m+1)*RPC],
                                                   in0=colsg[:, (m % 2)*RPC:((m % 2)+1)*RPC],
                                                   in1=wrep[:, (c % 4)*RPC:((c % 4)+1)*RPC],
                                                   op=ALU.mult), 1)
                    ve(_)
                    st["colsw_last"] = eV.v
                    st["prev_cols_mm_last"] = eT.v

                def emit_proj(j):
                    pw_ld = []
                    eT1 = eT.v
                    def _(q, ppm=st["prev_proj_mm"]):
                        for kk in range(32):
                            if kk >= 4:
                                q.wait_ge(eT.sem, eT1 + kk - 3)
                            else:
                                q.wait_ge(eT.sem, ppm)
                            dma(q, spw[kk % 4], pwwin[:, (kk % 4)*D:((kk % 4)+1)*D],
                                projw_bf[kk*128:(kk+1)*128, :])
                            pw_ld.append(spw[kk % 4].v)
                    sy(_)
                    def _(q, eS0=st["eS0"], eV0=st["eV0"]):
                        for kk in range(32):
                            q.wait_ge(spw[kk % 4].sem, pw_ld[kk])
                            q.wait_ge(eV.sem, eV0 + kk + 1)
                            if kk == 0:
                                q.wait_ge(eS.sem, eS0 + 32)
                            last = None
                            for tb in range(NPT):
                                for fh in range(2):
                                    last = q.matmul(
                                        out=ps[tb*2 + fh][0:128, :],
                                        lhsT=colsw[:, kk*RPC + tb*128: kk*RPC + (tb+1)*128],
                                        rhs=pwwin[:, (kk % 4)*D + fh*512:
                                                  (kk % 4)*D + (fh+1)*512],
                                        start=(kk == 0), stop=(kk == 31))
                            eT.inc(last, 1)
                    te(_)
                    st["prev_proj_mm"] = eT.v

                def emit_evac(j):
                    def _(q, tv=st["prev_proj_mm"]):
                        q.wait_ge(eT.sem, tv)
                        for tb in range(NPT):
                            for fh in range(2):
                                ii = q.copy(out=yb[:, tb*D + fh*512: tb*D + (fh+1)*512],
                                            in_=ps[tb*2 + fh][0:128, :])
                        eS.inc(ii, 1)
                    sc(_)
                    st["prev_proj_evac"] = eS.v

                def emit_tail(j):
                    P = j % 2
                    r0 = j * RPC
                    k = j
                    def _(q, P=P):
                        eS.wait(q)
                        for pt in range(NPT):
                            ysl = yb[:, pt*D:(pt+1)*D]
                            q.tensor_tensor(out=ysl, in0=ysl, in1=pbb[:, :], op=ALU.add)
                            q.scalar_tensor_tensor(out=ysl, in0=xs(P, pt),
                                                   scalar=stc(P, pt, 4), in1=ysl,
                                                   op0=ALU.mult, op1=ALU.add)
                            q.tensor_scalar(out=ysl, in0=ysl, scalar1=stc(P, pt, 5),
                                            scalar2=None, op0=ALU.add)
                        for pt in range(NPT):
                            q.tensor_reduce(out=stc(P, pt, 7), in_=yb[:, pt*D:(pt+1)*D],
                                            op=ALU.add, axis=AX.X)
                        for pt in range(NPT):
                            q.tensor_tensor(out=xe(P, pt), in0=yb[:, pt*D:(pt+1)*D],
                                            in1=yb[:, pt*D:(pt+1)*D], op=ALU.mult)
                        for pt in range(NPT):
                            q.tensor_reduce(out=stc(P, pt, 8), in_=xe(P, pt),
                                            op=ALU.add, axis=AX.X)
                        for pt in range(NPT):
                            q.tensor_scalar_mul(out=stc(P, pt, 7), in0=stc(P, pt, 7),
                                                scalar1=1.0 / D)
                        for pt in range(NPT):
                            q.tensor_scalar_mul(out=stc(P, pt, 8), in0=stc(P, pt, 8),
                                                scalar1=1.0 / D)
                        q.drain()
                        for pt in range(NPT):
                            q.tensor_tensor(out=stc(P, pt, 11), in0=stc(P, pt, 7),
                                            in1=stc(P, pt, 7), op=ALU.mult)
                        q.drain()
                        for pt in range(NPT):
                            q.tensor_tensor(out=stc(P, pt, 8), in0=stc(P, pt, 8),
                                            in1=stc(P, pt, 11), op=ALU.subtract)
                        eV.inc(q.drain(), 1)
                    ve(_)
                    def _(q, P=P):
                        eV.wait(q)
                        for pt in range(NPT):
                            q.activation(out=stc(P, pt, 11), in_=stc(P, pt, 8),
                                         func=AF.Ln, bias=EPS)
                        q.drain()
                        for pt in range(NPT):
                            q.activation(out=stc(P, pt, 9), in_=stc(P, pt, 11),
                                         func=AF.Exp, scale=-0.5)
                        eS.inc(q.drain(), 1)
                    sc(_)
                    def _(q, P=P):
                        eS.wait(q)
                        for pt in range(NPT):
                            q.tensor_tensor(out=stc(P, pt, 10), in0=stc(P, pt, 7),
                                            in1=stc(P, pt, 9), op=ALU.mult)
                        q.drain()
                        for pt in range(NPT):
                            q.tensor_scalar_mul(out=stc(P, pt, 10), in0=stc(P, pt, 10),
                                                scalar1=-1.0)
                        eV.inc(q.drain(), 1)
                    ve(_)
                    def _(q, P=P):
                        eV.wait(q)
                        for pt in range(NPT):
                            ii = q.activation(out=xe(P, pt), in_=yb[:, pt*D:(pt+1)*D],
                                              func=AF.Identity, scale=stc(P, pt, 9),
                                              bias=stc(P, pt, 10))
                        eS.inc(ii, 1)
                    sc(_)
                    def _(q, P=P):
                        eS.wait(q)
                        for pt in range(NPT):
                            tl = xe(P, pt)
                            q.tensor_tensor(out=tl, in0=tl, in1=gls05[:, :], op=ALU.mult)
                            q.tensor_tensor(out=tl, in0=tl, in1=bls05[:, :], op=ALU.add)
                            ii = q.tensor_tensor(out=xs(P, pt), in0=xs(P, pt), in1=tl,
                                                 op=ALU.add)
                        eV.inc(ii, 1)
                    ve(_)

                    st["prev_colsw_last"] = st["colsw_last"]

                    if phase == 1:
                        def _(q, k=k, P=P):
                            eV.wait(q)
                            for pt in range(NPT):
                                q.tensor_reduce(out=stc(P, pt, 0), in_=xs(P, pt),
                                                op=ALU.add, axis=AX.X)
                            for pt in range(NPT):
                                q.tensor_tensor(out=xe(P, pt), in0=xs(P, pt),
                                                in1=xs(P, pt), op=ALU.mult)
                            for pt in range(NPT):
                                q.tensor_reduce(out=stc(P, pt, 1), in_=xe(P, pt),
                                                op=ALU.add, axis=AX.X)
                            for pt in range(NPT):
                                scol = (k * NPT + pt) * 2
                                q.tensor_scalar_mul(out=stats3[:, scol:scol+1],
                                                    in0=stc(P, pt, 0), scalar1=1.0 / D)
                            for pt in range(NPT):
                                q.tensor_scalar_mul(out=stc(P, pt, 3), in0=stc(P, pt, 1),
                                                    scalar1=1.0 / D)
                            q.drain()
                            for pt in range(NPT):
                                scol = (k * NPT + pt) * 2
                                q.tensor_tensor(out=stc(P, pt, 11),
                                                in0=stats3[:, scol:scol+1],
                                                in1=stats3[:, scol:scol+1], op=ALU.mult)
                            q.drain()
                            for pt in range(NPT):
                                q.tensor_tensor(out=stc(P, pt, 3), in0=stc(P, pt, 3),
                                                in1=stc(P, pt, 11), op=ALU.subtract)
                            eV.inc(q.drain(), 1)
                        ve(_)
                        def _(q, k=k, P=P):
                            eV.wait(q)
                            for pt in range(NPT):
                                q.activation(out=stc(P, pt, 11), in_=stc(P, pt, 3),
                                             func=AF.Ln, bias=EPS)
                            q.drain()
                            for pt in range(NPT):
                                scol = (k * NPT + pt) * 2
                                q.activation(out=stats3[:, scol+1:scol+2],
                                             in_=stc(P, pt, 11), func=AF.Exp, scale=-0.5)
                            eS.inc(q.drain(), 1)
                        sc(_)
                        def _(q, k=k, P=P):
                            eS.wait(q)
                            for pt in range(NPT):
                                scol = (k * NPT + pt) * 2
                                q.tensor_tensor(out=stc(P, pt, 11),
                                                in0=stats3[:, scol+1:scol+2],
                                                in1=stats3[:, scol+1:scol+2], op=ALU.mult)
                            q.drain()
                            for pt in range(NPT):
                                q.tensor_tensor(out=stc(P, pt, 11), in0=stc(P, pt, 11),
                                                in1=stc(P, pt, 3), op=ALU.mult)
                            eV.inc(q.drain(), 1)
                        ve(_)
                        def _(q, P=P):
                            eV.wait(q)
                            for pt in range(NPT):
                                q.activation(out=stc(P, pt, 11), in_=stc(P, pt, 11),
                                             func=AF.Ln, bias=EPS)
                            q.drain()
                            for pt in range(NPT):
                                q.activation(out=stc(P, pt, 11), in_=stc(P, pt, 11),
                                             func=AF.Exp, scale=-0.5)
                            eS.inc(q.drain(), 1)
                        sc(_)
                        def _(q, k=k, P=P):
                            eS.wait(q)
                            for pt in range(NPT):
                                scol = (k * NPT + pt) * 2
                                q.tensor_tensor(out=stc(P, pt, 11), in0=stc(P, pt, 11),
                                                in1=stats3[:, scol+1:scol+2], op=ALU.mult)
                            q.drain()
                            for pt in range(NPT):
                                scol = (k * NPT + pt) * 2
                                q.tensor_tensor(out=stc(P, pt, 10),
                                                in0=stats3[:, scol:scol+1],
                                                in1=stc(P, pt, 11), op=ALU.mult)
                            q.drain()
                            for pt in range(NPT):
                                q.tensor_scalar_mul(out=stc(P, pt, 10), in0=stc(P, pt, 10),
                                                    scalar1=-1.0)
                            eV.inc(q.drain(), 1)
                        ve(_)
                        def _(q, P=P):
                            eV.wait(q)
                            for pt in range(NPT):
                                ii = q.activation(out=us(P, pt), in_=xs(P, pt),
                                                  func=AF.Identity, scale=stc(P, pt, 11),
                                                  bias=stc(P, pt, 10))
                            eS.inc(ii, 1)
                        sc(_)
                        def _(q, r0=r0, k=k, P=P):
                            eS.wait(q)
                            dma(q, sxs[k % 2],
                                xn2_slice[r0:r0+RPC, :].rearrange("(a p) d -> p a d", p=128),
                                u1b[:, P*NPT*D:(P+1)*NPT*D].rearrange("p (a d) -> p a d",
                                                                      a=NPT))
                            dma(q, dS, x1_dram[r0:r0+RPC, :].rearrange("(a p) d -> p a d",
                                                                       p=128),
                                x0[:, P*NPT*D:(P+1)*NPT*D].rearrange("p (a d) -> p a d",
                                                                     a=NPT))
                        sc(_)
                        st["prev_stores"] = dS.v
                        st["prev_xn2_store"] = sxs[k % 2].v
                        st["prev_xn2_par"] = k % 2
                        def _(q, k=k, w=st["prev_xn2_store"]):
                            q.wait_ge(sxs[k % 2].sem, w)
                            eG.inc(q.collective_compute(
                                "AllGather", ALU.bypass, replica_groups=GROUPS,
                                ins=[xn2_slice[k*RPC:(k+1)*RPC, :].opt()],
                                outs=[cag2[k, :, :].opt()]), 1)
                            ag_done_list.append(eG.v)
                        gp(_)
                    else:
                        def _(q, r0=r0, P=P):
                            eV.wait(q)
                            dma(q, dS,
                                outrows[r0:r0+RPC, :].rearrange("(a p) d -> p a d", p=128),
                                x0[:, P*NPT*D:(P+1)*NPT*D].rearrange("p (a d) -> p a d",
                                                                     a=NPT))
                        sc(_)
                        st["prev_stores"] = dS.v

                emit_head_load(0)
                emit_head_rest(0)
                emit_gates(0)
                emit_cols(0)
                for k in range(NCHK):
                    if k + 1 < NCHK:
                        emit_head_load(k + 1)
                    emit_proj(k)
                    if k + 1 < NCHK:
                        emit_head_rest(k + 1)
                    emit_evac(k)
                    if k + 1 < NCHK:
                        emit_gates(k + 1)
                        emit_cols(k + 1)
                    emit_tail(k)
                barrier()

        ffn_phase(1)

        with ExitStack() as ph:
            Em = ph.enter_context
            Xn2 = Em(nc.sbuf_tensor("Xn2", [128, 8 * TS], BF16))
            xbwin = Em(nc.sbuf_tensor("xbwin", [128, TS + 3], F32))
            halo = Em(nc.sbuf_tensor("halo", [128, 12], F32))
            zsb = Em(nc.sbuf_tensor("zsb", [128, 8 * TS], BF16))
            xcsb = Em(nc.sbuf_tensor("xcsb", [128, 4 * TS], F32))
            xcbw = Em(nc.sbuf_tensor("xcbw", [128, TS], BF16))
            dt4 = Em(nc.sbuf_tensor("dt4", [128, 4 * TS], F32))
            dtx4 = Em(nc.sbuf_tensor("dtx4", [128, 4 * TS], BF16))
            dtrawb = Em(nc.sbuf_tensor("dtrawb", [128, TS], F32))
            Ebuf = Em(nc.sbuf_tensor("Ebuf", [128, 2 * TS], F32))
            ubuf = Em(nc.sbuf_tensor("ubuf", [128, 2 * TS], BF16))
            hbuf = Em(nc.sbuf_tensor("hbuf", [128, TS], BF16))
            ytmp = Em(nc.sbuf_tensor("ytmp", [128, TS], BF16))
            tmpb = Em(nc.sbuf_tensor("tmpb", [128, TS], F32))
            acc4 = Em(nc.sbuf_tensor("acc4", [128, 4 * TS], F32))
            Brep = Em(nc.sbuf_tensor("Brep", [128, 2 * TS], BF16))
            Crep = Em(nc.sbuf_tensor("Crep", [128, 2 * TS], BF16))
            ygsb = Em(nc.sbuf_tensor("ygsb", [128, 4 * TS], BF16))
            ssm_sb = Em(nc.sbuf_tensor("ssm_sb", [128, TS], F32))
            evsb = Em(nc.sbuf_tensor("evsb", [128, D], BF16))
            winb = Em(nc.sbuf_tensor("winb", [128, 8 * 1024], BF16))
            woutb = Em(nc.sbuf_tensor("woutb", [128, 4 * D], BF16))
            wxb = Em(nc.sbuf_tensor("wxb", [128, 4 * 33], BF16))
            evsc = Em(nc.sbuf_tensor("evsc", [128, TS], F32))

            def _(q):
                q.memset(halo[:, :], 0.0)
                eV.inc(q.memset(state[:, :], 0.0), 1)
            ve(_)

            def _(q):
                dma(q, dL, winb[:, :], winb_dram[:, :])
                dma(q, dL, woutb[:, :], woutb_dram[:, 0:4*D])
                dma(q, dL, wxb[:, :], woutb_dram[:, 4*D:4*D + 4*33])
            sy(_)
            mwl = dL.v

            st2 = dict(win_done=eT.v, mh_evac=eS.v, wo_mm=eT.v, bc_cast=0)
            prev_scan_eV1 = None

            def emit_mhead(s):
                kch = s // 2
                hh = s % 2
                zb = (s % 2) * 4 * TS

                def _(q, kch=kch, hh=hh, a=st2["win_done"], g=ag_done_list[kch]):
                    q.wait_ge(eG.sem, g)
                    q.wait_ge(eT.sem, a)
                    for rb in range(2):
                        row0 = (2*hh + rb) * RPC
                        for dt8 in range(8):
                            dma(q, sxn, Xn2[:, dt8*TS + rb*512: dt8*TS + (rb+1)*512],
                                cag2[kch, row0:row0+512, dt8*128:(dt8+1)*128], transpose=True)
                sy(_)
                xn_ld = sxn.v

                eT0, eS0, eV0 = eT.v, eS.v, eV.v
                def winpos(m):
                    return m + 1 if m < 4 else 2*m - 2
                def wxpos(c):
                    return 5 + 2*c
                def evacpos(m):
                    return 2*m + 1 if m < 4 else m + 5

                def _(q, w=xn_ld, eT0=eT0, eS0=eS0, eV0=eV0, pw_=st2["mh_evac"], s=s):
                    q.wait_ge(sxn.sem, w)
                    if s == 0:
                        q.wait_ge(dL.sem, mwl)
                    order = [("w", 0), ("w", 1), ("w", 2), ("w", 3),
                             ("x", 0), ("w", 4), ("x", 1), ("w", 5),
                             ("x", 2), ("w", 6), ("x", 3), ("w", 7)]
                    for kind, i in order:
                        if kind == "w":
                            m = i
                            if m == 0:
                                q.wait_ge(eS.sem, pw_)
                            if m >= 1:
                                q.wait_ge(eS.sem, eS0 + evacpos(m-1))
                            last = None
                            for half in range(2):
                                for k8 in range(8):
                                    last = q.matmul(
                                        out=ps[6 + half][0:128, :],
                                        lhsT=winb[:, k8*1024 + m*128: k8*1024 + (m+1)*128],
                                        rhs=Xn2[:, k8*TS + half*512: k8*TS + (half+1)*512],
                                        start=(k8 == 0), stop=(k8 == 7))
                            eT.inc(last, 1)
                        else:
                            c = i
                            q.wait_ge(eV.sem, eV0 + 2*c + 2)
                            last = None
                            for half in range(2):
                                last = q.matmul(out=ps[4 + half][0:33, :],
                                                lhsT=wxb[:, c*33:(c+1)*33],
                                                rhs=xcbw[:, half*512:(half+1)*512],
                                                start=(c == 0), stop=(c == 3))
                            eT.inc(last, 1)
                te(_)
                st2["win_done"] = eT.v
                def _(q, eT0=eT0, eV0=eV0, zb=zb):
                    for m in range(8):
                        q.wait_ge(eT.sem, eT0 + winpos(m))
                        if 1 <= m < 4:
                            q.wait_ge(eV.sem, eV0 + 2*(m-1) + 1)
                        for half in range(2):
                            if m < 4:
                                ii = q.activation(out=xbwin[:, 3 + half*512: 3 + (half+1)*512],
                                                  in_=ps[6 + half][0:128, :], func=AF.Identity,
                                                  bias=cb_sb[:, m:m+1])
                            else:
                                ii = q.activation(out=zsb[:, zb + (m-4)*TS + half*512:
                                                          zb + (m-4)*TS + (half+1)*512],
                                                  in_=ps[6 + half][0:128, :], func=AF.Identity,
                                                  bias=cb_sb[:, m:m+1])
                        eS.inc(ii, 1)
                        if m < 4:
                            c = m
                            q.wait_ge(eV.sem, eV0 + 2*c + 1)
                            eS.inc(q.activation(out=xcsb[:, c*TS:(c+1)*TS],
                                                in_=xcsb[:, c*TS:(c+1)*TS], func=AF.Silu), 1)
                sc(_)
                st2["mh_evac"] = eS.v
                def _(q, eS0=eS0, eT0=eT0):
                    for c in range(4):
                        q.wait_ge(eS.sem, eS0 + 2*c + 1)
                        q.tensor_copy(out=xbwin[:, 0:3], in_=halo[:, c*3:(c+1)*3])
                        q.tensor_scalar(out=xcsb[:, c*TS:(c+1)*TS], in0=xbwin[:, 3:3+TS],
                                        scalar1=convw_sb[:, c*DC+3:c*DC+4],
                                        scalar2=convb_sb[:, c:c+1],
                                        op0=ALU.mult, op1=ALU.add)
                        for kk in (2, 1, 0):
                            q.scalar_tensor_tensor(out=xcsb[:, c*TS:(c+1)*TS],
                                                   in0=xbwin[:, kk:kk+TS],
                                                   scalar=convw_sb[:, c*DC+kk:c*DC+kk+1],
                                                   in1=xcsb[:, c*TS:(c+1)*TS],
                                                   op0=ALU.mult, op1=ALU.add)
                        eV.inc(q.tensor_copy(out=halo[:, c*3:(c+1)*3],
                                             in_=xbwin[:, TS:TS+3]), 1)
                        q.wait_ge(eS.sem, eS0 + 2*c + 2)
                        if c >= 1:
                            q.wait_ge(eT.sem, eT0 + wxpos(c-1))
                        eV.inc(q.tensor_copy(out=xcbw[:, :], in_=xcsb[:, c*TS:(c+1)*TS]), 1)
                ve(_)

                def _(q):
                    eT.wait(q)
                    q.copy(out=ssm_sb[0:33, 0:512], in_=ps[4][0:33, :])
                    eS.inc(q.copy(out=ssm_sb[0:33, 512:1024], in_=ps[5][0:33, :]), 1)
                    q.wait_ge(eS.sem, eS.v)
                    dma(q, dS, ssm_in[s, :, :], ssm_sb[0:33, 0:TS])
                sc(_)
                def _(q, s=s):
                    q.wait_ge(dS.sem, dS.v)
                    eG.inc(q.collective_compute("AllReduce", ALU.add, replica_groups=GROUPS,
                                                ins=[ssm_in[s, :, :].opt()],
                                                outs=[ssm_out[s, :, :].opt()]), 1)
                gp(_)
                ar_done = eG.v
                def _(q, s=s, w=ar_done):
                    q.wait_ge(eG.sem, w)
                    dma(q, dG, ssm_bf[s, :, :], ssm_out[s, 1:33, :])
                gp(_)
                st2["bc_cast"] = dG.v

                def _(q, s=s, w=ar_done):
                    q.wait_ge(eG.sem, w)
                    dma(q, sdtr, dtrawb[:, :], ssm_out[s, 0:1, :].broadcast_to((128, TS)))
                sy(_)
                dtr_ld = sdtr.v
                eSd0, eVd0 = eS.v, eV.v
                def _(q, w=dtr_ld, eVd0=eVd0):
                    q.wait_ge(sdtr.sem, w)
                    for c in range(4):
                        if c >= 1:
                            q.wait_ge(eV.sem, eVd0 + c)
                        q.activation(out=tmpb[:, :], in_=dtrawb[:, :], func=AF.Abs,
                                     scale=wdt_sb[:, c:c+1], bias=bdt_sb[:, c:c+1])
                        q.activation(out=tmpb[:, :], in_=tmpb[:, :], func=AF.Exp, scale=-1.0)
                        q.activation(out=tmpb[:, :], in_=tmpb[:, :], func=AF.Ln, bias=1.0)
                        eS.inc(q.activation(out=dt4[:, c*TS:(c+1)*TS], in_=dtrawb[:, :],
                                            func=AF.Relu, scale=wdt_sb[:, c:c+1],
                                            bias=bdt_sb[:, c:c+1]), 1)
                sc(_)
                def _(q, eSd0=eSd0):
                    for c in range(4):
                        q.wait_ge(eS.sem, eSd0 + c + 1)
                        q.tensor_tensor(out=dt4[:, c*TS:(c+1)*TS], in0=dt4[:, c*TS:(c+1)*TS],
                                        in1=tmpb[:, :], op=ALU.add)
                        eV.inc(q.tensor_tensor(out=dtx4[:, c*TS:(c+1)*TS],
                                               in0=dt4[:, c*TS:(c+1)*TS],
                                               in1=xcsb[:, c*TS:(c+1)*TS],
                                               op=ALU.mult), 1)
                ve(_)

            emit_mhead(0)
            for s in range(NSUB):
                eS1, eV1 = eS.v, eV.v
                bc_ld = []
                def _(q, s=s, eV1=eV1, pv=prev_scan_eV1, wc=st2["bc_cast"]):
                    q.wait_ge(dG.sem, wc)
                    for n in range(DS):
                        slot = n % 2
                        if n >= 2:
                            q.wait_ge(eV.sem, eV1 + 4*(n-2) + 4)
                        elif pv is not None:
                            q.wait_ge(eV.sem, pv + 60 + 4*n)
                        dma(q, sbc[slot], Brep[:, slot*TS:(slot+1)*TS],
                            ssm_bf[s, n:n+1, :].broadcast_to((128, TS)))
                        dma(q, sbc[slot], Crep[:, slot*TS:(slot+1)*TS],
                            ssm_bf[s, 16+n:17+n, :].broadcast_to((128, TS)))
                        bc_ld.append(sbc[slot].v)
                sy(_)
                def _(q, eS1=eS1, eV1=eV1):
                    for g in range(4 * DS):
                        n, c = g // 4, g % 4
                        eslot = g % 2
                        if g >= 2:
                            q.wait_ge(eV.sem, eV1 + g - 1)
                        eS.inc(q.activation(out=Ebuf[:, eslot*TS:(eslot+1)*TS],
                                            in_=dt4[:, c*TS:(c+1)*TS],
                                            func=AF.Exp,
                                            scale=A_sb[:, c*DS+n:c*DS+n+1]), 1)
                sc(_)
                def _(q, eS1=eS1):
                    for g in range(4 * DS):
                        n, c = g // 4, g % 4
                        slot = n % 2
                        eslot = g % 2
                        q.wait_ge(eS.sem, eS1 + g + 1)
                        q.wait_ge(sbc[n % 2].sem, bc_ld[n])
                        q.tensor_tensor(out=ubuf[:, eslot*TS:(eslot+1)*TS],
                                        in0=dtx4[:, c*TS:(c+1)*TS],
                                        in1=Brep[:, slot*TS:(slot+1)*TS], op=ALU.mult)
                        q.tensor_tensor_scan(out=hbuf[:, :],
                                             data0=Ebuf[:, eslot*TS:(eslot+1)*TS],
                                             data1=ubuf[:, eslot*TS:(eslot+1)*TS],
                                             initial=state[:, c*DS+n:c*DS+n+1],
                                             op0=ALU.mult, op1=ALU.add)
                        if n == 0:
                            q.tensor_tensor(out=acc4[:, c*TS:(c+1)*TS], in0=hbuf[:, :],
                                            in1=Crep[:, slot*TS:(slot+1)*TS], op=ALU.mult)
                        else:
                            q.tensor_tensor(out=ytmp[:, :], in0=hbuf[:, :],
                                            in1=Crep[:, slot*TS:(slot+1)*TS], op=ALU.mult)
                            q.tensor_tensor(out=acc4[:, c*TS:(c+1)*TS],
                                            in0=acc4[:, c*TS:(c+1)*TS],
                                            in1=ytmp[:, :], op=ALU.add)
                        eV.inc(q.tensor_copy(out=state[:, c*DS+n:c*DS+n+1],
                                             in_=hbuf[:, TS-1:TS]), 1)
                    for c in range(4):
                        q.scalar_tensor_tensor(out=acc4[:, c*TS:(c+1)*TS],
                                               in0=xcsb[:, c*TS:(c+1)*TS],
                                               scalar=dparam_sb[:, c:c+1],
                                               in1=acc4[:, c*TS:(c+1)*TS],
                                               op0=ALU.mult, op1=ALU.add)
                ve(_)

                prev_scan_eV1 = eV1
                if s + 1 < NSUB:
                    emit_mhead(s + 1)

                eS2, eV2 = eS.v, eV.v
                zb2 = (s % 2) * 4 * TS
                def _(q, eV2=eV2, zb2=zb2):
                    for c in range(4):
                        q.wait_ge(eV.sem, eV2 + c)
                        eS.inc(q.activation(out=evsc[:, :],
                                            in_=zsb[:, zb2 + c*TS: zb2 + (c+1)*TS],
                                            func=AF.Silu), 1)
                sc(_)
                def _(q, eS2=eS2, pw_=st2["wo_mm"]):
                    q.wait_ge(eT.sem, pw_)
                    for c in range(4):
                        q.wait_ge(eS.sem, eS2 + c + 1)
                        eV.inc(q.tensor_tensor(out=ygsb[:, c*TS:(c+1)*TS],
                                               in0=acc4[:, c*TS:(c+1)*TS],
                                               in1=evsc[:, :], op=ALU.mult), 1)
                ve(_)

                eT2, eS3, dS0, eV3 = eT.v, eS.v, dS.v, eV.v
                def _(q, eT2=eT2, eS3=eS3, eV3=eV3):
                    q.wait_ge(eV.sem, eV3)
                    for tb in range(8):
                        if tb >= 2:
                            q.wait_ge(eS.sem, eS3 + tb - 1)
                        last = None
                        for fh in range(2):
                            for c in range(4):
                                last = q.matmul(out=ps[2*(tb % 2) + fh][0:128, :],
                                                lhsT=ygsb[:, c*TS + tb*128: c*TS + (tb+1)*128],
                                                rhs=woutb[:, c*D + fh*512: c*D + (fh+1)*512],
                                                start=(c == 0), stop=(c == 3))
                        eT.inc(last, 1)
                te(_)
                st2["wo_mm"] = eT.v
                swp0 = swp.v
                def _(q, s=s, eT2=eT2, eS3=eS3, swp0=swp0):
                    for tb in range(8):
                        q.wait_ge(eT.sem, eT2 + tb + 1)
                        if tb >= 1:
                            q.wait_ge(swp.sem, swp0 + 16*tb)
                        for fh in range(2):
                            ii = q.copy(out=evsb[:, fh*512:(fh+1)*512],
                                        in_=ps[2*(tb % 2) + fh][0:128, :])
                        eS.inc(ii, 1)
                        q.wait_ge(eS.sem, eS3 + tb + 1)
                        p = s * TS + tb * 128
                        r = (p % 2048) // 512
                        row = (p // 2048) * 4 * RPC + r * RPC + (p % 512)
                        dma(q, swp, wpart[row:row+128, :], evsb[:, 0:D])
                sc(_)
                if s % 2 == 1:
                    kc = s // 2
                    def _(q, kc=kc, w=swp.v):
                        q.wait_ge(swp.sem, w)
                        eG.inc(q.collective_compute(
                            "ReduceScatter", ALU.add, replica_groups=GROUPS,
                            ins=[wpart[kc*4*RPC:(kc+1)*4*RPC, :].opt()],
                            outs=[rsout[kc*RPC:(kc+1)*RPC, :].opt()]), 1)
                        rs_done_list.append(eG.v)
                    gp(_)
            barrier()

        ffn_phase(2)

        def _(q):
            q.wait_ge(dS.sem, dS.v)
            if DBG:
                q.wait_ge(dL.sem, dL.v)
                dma(q, dG, dbg_x1[:, :], x1_dram[:, :])
                dma(q, dG, dbg_xn2[:, :], xn2_slice[:, :])
                dma(q, dG, dbg_cag[:, :], cag2.rearrange("k r d -> (k r) d"))
                dma(q, dG, dbg_ssm[:, :, :], ssm_out[:, :, :])
                dma(q, dG, dbg_rs[:, :], rsout[:, :])
                dma(q, dG, dbg_wp[:, :], wpart[:, :])
                q.wait_ge(dG.sem, dG.v)
        gp(_)

    return nc


LAST_RES = None


def kernel(**inputs):
    x = np.asarray(inputs["L0_attn_x"], np.float32)
    W_in = np.asarray(inputs["L0_attn_W_in"], np.float32)
    gam = np.asarray(inputs["L0_attn_ln_gamma"], np.float32)
    bet = np.asarray(inputs["L0_attn_ln_beta"], np.float32)
    conv_w = np.asarray(inputs["L0_attn_conv_w"], np.float32)
    conv_b = np.asarray(inputs["L0_attn_conv_b"], np.float32)
    W_x = np.asarray(inputs["L0_attn_W_x"], np.float32)
    w_dt = np.asarray(inputs["L0_attn_w_dt"], np.float32)
    b_dt = np.asarray(inputs["L0_attn_b_dt"], np.float32)
    A_log = np.asarray(inputs["L0_attn_A_log"], np.float32)
    D_param = np.asarray(inputs["L0_attn_D_param"], np.float32)
    W_out = np.asarray(inputs["L0_attn_W_out"], np.float32)
    col_emb = np.asarray(inputs["L0_ffn_col_emb"], np.float32)
    gate_w1 = np.asarray(inputs["L0_ffn_gate_w1"], np.float32)
    gate_b1 = np.asarray(inputs["L0_ffn_gate_b1"], np.float32)
    gate_w2 = np.asarray(inputs["L0_ffn_gate_w2"], np.float32)
    gate_b2 = np.asarray(inputs["L0_ffn_gate_b2"], np.float32)
    W_cols = np.asarray(inputs["L0_ffn_W_cols"], np.float32)
    b_cols = np.asarray(inputs["L0_ffn_b_cols"], np.float32)
    proj_w = np.asarray(inputs["L0_ffn_proj_w"], np.float32)
    proj_b = np.asarray(inputs["L0_ffn_proj_b"], np.float32)
    ln_g = np.asarray(inputs["L0_ffn_ln_g"], np.float32)
    ln_b = np.asarray(inputs["L0_ffn_ln_b"], np.float32)

    nc = build_nc()

    C = np.ascontiguousarray
    in_maps = []
    for core in range(NCORE):
        b = core // 4
        qq = core % 4
        ch0 = qq * CHS
        xr = C(x[b].reshape(NCHK, 4, RPC, D)[:, qq].reshape(NROWS, D))
        win_slice = np.concatenate([W_in[ch0:ch0+CHS], W_in[DI+ch0:DI+ch0+CHS]], axis=0)
        m = dict(
            xrows=xr,
            win_t=C(win_slice.T),
            gam_m=C(gam.reshape(D, 1)),
            bet_m=C(bet.reshape(D, 1)),
            wout_t=C(W_out[:, ch0:ch0+CHS].T),
            wx_t=C(W_x[:, ch0:ch0+CHS].T),
            convw=C(conv_w[ch0:ch0+CHS]),
            convb=C(conv_b[ch0:ch0+CHS].reshape(CHS, 1)),
            dparam=C(D_param[ch0:ch0+CHS].reshape(CHS, 1)),
            wdt=C(w_dt[ch0:ch0+CHS].reshape(CHS, 1)),
            bdt=C(b_dt[ch0:ch0+CHS].reshape(CHS, 1)),
            alog=C(A_log[ch0:ch0+CHS]),
            gw1=gate_w1,
            gb1=C(gate_b1.reshape(DH, 1)),
            gw2=gate_w2,
            gb2=C(gate_b2.reshape(1, C16)),
            wcols_t=C(W_cols.transpose(1, 0, 2).reshape(D, CN)),
            bcols=C(b_cols.reshape(CN, 1)),
            projw=proj_w,
            projb=C(proj_b.reshape(1, D)),
            lng=C(ln_g.reshape(1, D)),
            lnb=C(ln_b.reshape(1, D)),
            colemb=col_emb,
            colembt=C(col_emb.T),
            ident=np.eye(128, dtype=np.float32),
        )
        in_maps.append(m)

    res = run_bass_kernel_spmd(nc, in_maps, core_ids=list(range(NCORE)))
    global LAST_RES
    LAST_RES = res

    out = np.empty((B, T, D), np.float32)
    for core in range(NCORE):
        b = core // 4
        qq = core % 4
        out[b].reshape(NCHK, 4, RPC, D)[:, qq] = \
            res.results[core]["outrows"].reshape(NCHK, RPC, D)
    return out


# version 25
# speedup vs baseline: 1.0108x; 1.0019x over previous
import numpy as np
from contextlib import ExitStack

import concourse.bass as bass
import concourse.mybir as mybir
from concourse.bass_utils import run_bass_kernel_spmd

F32 = mybir.dt.float32
BF16 = mybir.dt.bfloat16
AF = mybir.ActivationFunctionType
ALU = mybir.AluOpType
AX = mybir.AxisListType

B, T, D = 2, 16384, 1024
DI, DS, DC = 2048, 16, 4
C16, NN, DH = 16, 256, 512
CN = C16 * NN
CHS = 512
NCHK = 8
RPC = 512
NPT = 4
NSUB = 16
TS = 1024
EPS = 1e-5
GROUPS = [[0, 1, 2, 3], [4, 5, 6, 7]]
NROWS = NCHK * RPC
NCORE = 8
DBG = False
DBG_PHASE = 1


class Ctr:
    def __init__(self, stack, nc, name):
        self.sem = stack.enter_context(nc.semaphore(name))
        self.v = 0

    def inc(self, instr, n=1):
        instr.then_inc(self.sem, n)
        self.v += n

    def wait(self, eng, v=None):
        eng.wait_ge(self.sem, self.v if v is None else v)


def build_nc():
    nc = bass.Bass("TRN2", target_bir_lowering=False, debug=False)

    def din(name, shape):
        return nc.dram_tensor(name, shape, F32, kind="ExternalInput").ap()

    xrows = din("xrows", [NROWS, D])
    win_t = din("win_t", [D, 2 * CHS])
    gam_m = din("gam_m", [D, 1])
    bet_m = din("bet_m", [D, 1])
    wout_t = din("wout_t", [CHS, D])
    wx_t = din("wx_t", [CHS, 33])
    convw = din("convw", [CHS, DC])
    convb = din("convb", [CHS, 1])
    dparam = din("dparam", [CHS, 1])
    wdt = din("wdt", [CHS, 1])
    bdt = din("bdt", [CHS, 1])
    alog = din("alog", [CHS, DS])
    gw1 = din("gw1", [D, DH])
    gb1 = din("gb1", [DH, 1])
    gw2 = din("gw2", [DH, C16])
    gb2 = din("gb2", [1, C16])
    wcols_t = din("wcols_t", [D, CN])
    bcols = din("bcols", [CN, 1])
    projw = din("projw", [CN, D])
    projb = din("projb", [1, D])
    lng = din("lng", [1, D])
    lnb = din("lnb", [1, D])
    colemb = din("colemb", [C16, D])
    colembt = din("colembt", [D, C16])
    ident = din("ident", [128, 128])

    outrows = nc.dram_tensor("outrows", [NROWS, D], F32, kind="ExternalOutput").ap()
    dbgt = {}
    if DBG:
        for nm, cols, dt in [("u1b", NPT * D, BF16), ("st1", NPT * 12, F32),
                             ("U1", 8 * RPC, BF16), ("h1b", 4 * RPC, BF16),
                             ("wtok", NPT * C16, F32), ("lgt", NPT * C16, F32),
                             ("colsw", 32 * RPC, BF16), ("ybraw", NPT * D, F32),
                             ("yb2", NPT * D, F32), ("x0f", NPT * D, F32)]:
            dbgt[nm] = nc.dram_tensor("dbg_" + nm, [128, cols], dt,
                                      kind="ExternalOutput").ap()
        dbg_x1 = nc.dram_tensor("dbg_x1", [NROWS, D], F32, kind="ExternalOutput").ap()
        dbg_xn2 = nc.dram_tensor("dbg_xn2", [NROWS, D], BF16, kind="ExternalOutput").ap()
        dbg_cag = nc.dram_tensor("dbg_cag", [4 * NROWS, D], BF16, kind="ExternalOutput").ap()
        dbg_ssm = nc.dram_tensor("dbg_ssm", [NSUB, 33, TS], F32, kind="ExternalOutput").ap()
        dbg_rs = nc.dram_tensor("dbg_rs", [NROWS, D], F32, kind="ExternalOutput").ap()
        dbg_wp = nc.dram_tensor("dbg_wp", [4 * NROWS, D], F32, kind="ExternalOutput").ap()

    wcols_bf2 = nc.dram_tensor("wcols_bf2", [32 * 128, 8 * 128], BF16).ap()
    projw_bf = nc.dram_tensor("projw_bf", [CN, D], BF16).ap()
    u1b_dram = nc.dram_tensor("u1b_dram", [RPC, D], BF16).ap()
    xn2_slice = nc.dram_tensor("xn2_slice", [NROWS, D], BF16).ap()
    cag2 = nc.dram_tensor("cag2", [NCHK, 4 * RPC, D], BF16).ap()
    x1_dram = nc.dram_tensor("x1_dram", [NROWS, D], F32).ap()
    ssm_in = nc.dram_tensor("ssm_in", [NSUB, 33, TS], F32).ap()
    ssm_out = nc.dram_tensor("ssm_out", [NSUB, 33, TS], F32).ap()
    ssm_bf = nc.dram_tensor("ssm_bf", [NSUB, 32, TS], BF16).ap()
    winb_dram = nc.dram_tensor("winb_dram", [128, 8 * 1024], BF16).ap()
    woutb_dram = nc.dram_tensor("woutb_dram", [128, 4 * D + 4 * 33], BF16).ap()
    wpart = nc.dram_tensor("wpart", [4 * NROWS, D], BF16).ap()
    rsout = nc.dram_tensor("rsout", [NROWS, D], BF16).ap()
    wfm_dram = nc.dram_tensor("wfm_dram", [C16, RPC], BF16).ap()
    colr_dram = nc.dram_tensor("colr_dram", [C16, 1], F32).ap()

    with ExitStack() as perm:
        E = perm.enter_context
        block = E(nc.Block())

        def sy(fn): block.sync(fn)
        def ve(fn): block.vector(fn)
        def sc(fn): block.scalar(fn)
        def te(fn): block.tensor(fn)
        def gp(fn): block.gpsimd(fn)

        dL = Ctr(perm, nc, "dL")
        sx0 = Ctr(perm, nc, "sx0")
        su1 = Ctr(perm, nc, "su1")
        swc = [Ctr(perm, nc, f"swc{i}") for i in range(4)]
        spw = [Ctr(perm, nc, f"spw{i}") for i in range(4)]
        swr = [Ctr(perm, nc, f"swr{i}") for i in range(4)]
        sxs = [Ctr(perm, nc, "sxsA"), Ctr(perm, nc, "sxsB")]
        swp = Ctr(perm, nc, "swp")
        sxn = Ctr(perm, nc, "sxn")
        sdtr = Ctr(perm, nc, "sdtr")
        sbc = [Ctr(perm, nc, "sbcA"), Ctr(perm, nc, "sbcB")]
        dS = Ctr(perm, nc, "dS")
        dG = Ctr(perm, nc, "dG")
        eV = Ctr(perm, nc, "eV")
        eS = Ctr(perm, nc, "eS")
        eT = Ctr(perm, nc, "eT")
        eG = Ctr(perm, nc, "eG")
        bar = Ctr(perm, nc, "bar")

        def dma(q, ctr, out, in_, **kw):
            ctr.inc(q.dma_start(out=out, in_=in_, **kw), 16)
            return ctr.v

        def dump(name, sb, prod_ctr):
            if not DBG:
                return
            snap = prod_ctr.v
            def _(q, snap=snap, name=name, sb=sb, ctr=prod_ctr):
                q.wait_ge(ctr.sem, snap)
                dma(q, dS, dbgt[name][:, :], sb)
                q.wait_ge(dS.sem, dS.v)
            sy(_)
            barrier()

        def barrier():
            def _(q):
                for c in (dS, dL, dG, sxs[0], sxs[1], swp, sx0, su1, sxn):
                    q.wait_ge(c.sem, c.v)
            sy(_)
            for emit in (sy, ve, sc, te, gp):
                def _(q):
                    q.drain()
                    q.sem_inc(bar.sem, 1)
                emit(_)
            bar.v += 5
            tgt = bar.v
            for emit in (sy, ve, sc, te, gp):
                def _(q, tgt=tgt):
                    q.wait_ge(bar.sem, tgt)
                emit(_)

        convw_sb = E(nc.sbuf_tensor("convw_sb", [128, 4 * DC], F32))
        convb_sb = E(nc.sbuf_tensor("convb_sb", [128, 4], F32))
        dparam_sb = E(nc.sbuf_tensor("dparam_sb", [128, 4], F32))
        wdt_sb = E(nc.sbuf_tensor("wdt_sb", [128, 4], F32))
        bdt_sb = E(nc.sbuf_tensor("bdt_sb", [128, 4], F32))
        A_sb = E(nc.sbuf_tensor("A_sb", [128, 4 * DS], F32))
        gb1_sb = E(nc.sbuf_tensor("gb1_sb", [128, 4], F32))
        bcols_sb = E(nc.sbuf_tensor("bcols_sb", [128, 32], F32))
        cb_sb = E(nc.sbuf_tensor("cb_sb", [128, 8], F32))
        pbb = E(nc.sbuf_tensor("pbb", [128, D], F32))
        gls05 = E(nc.sbuf_tensor("gls05", [128, D], F32))
        bls05 = E(nc.sbuf_tensor("bls05", [128, D], F32))
        b2b = E(nc.sbuf_tensor("b2b", [128, C16], F32))
        colrb = E(nc.sbuf_tensor("colrb", [128, C16], F32))
        idsb = E(nc.sbuf_tensor("idsb", [128, 128], F32))
        g1b = E(nc.sbuf_tensor("g1b", [128, 8 * DH], BF16))
        g2b = E(nc.sbuf_tensor("g2b", [128, 4 * C16], BF16))
        cetb = E(nc.sbuf_tensor("cetb", [128, 8 * C16], BF16))
        stats3 = E(nc.sbuf_tensor("stats3", [128, NCHK * NPT * 2], F32))
        state = E(nc.sbuf_tensor("state", [128, 4 * DS], F32))
        gam_sb = E(nc.sbuf_tensor("gam_sb", [128, 8], F32))
        bet_sb = E(nc.sbuf_tensor("bet_sb", [128, 8], F32))
        epsc = E(nc.sbuf_tensor("epsc", [128, 1], F32))
        nc.const_aps.register(F32, EPS, epsc[:, 0:1])

        ps = [E(nc.psum_tensor(f"ps{i}", [128, 512], F32)) for i in range(8)]

        with ExitStack() as ph0:
            E0 = ph0.enter_context
            wscr = E0(nc.sbuf_tensor("wscr", [128, 8 * 1024], F32))
            sscr = E0(nc.sbuf_tensor("sscr", [128, 1024], F32))
            wbf = E0(nc.sbuf_tensor("wbf", [128, 8 * 1024], BF16))

            def _(q):
                for m in range(32):
                    dma(q, dG,
                        wcols_bf2[m*128:(m+1)*128, :].rearrange("p (g q) -> p g q", g=8),
                        wcols_t[:, m*128:(m+1)*128].rearrange("(g p) q -> p g q", p=128))
                dma(q, dG, projw_bf[:, :], projw[:, :])
            gp(_)

            def _(q):
                eV.inc(q.memset(epsc[:, :], EPS), 1)
            ve(_)

            def _(q):
                for c in range(4):
                    dma(q, dL, convw_sb[:, c*DC:(c+1)*DC], convw[c*128:(c+1)*128, :])
                    dma(q, dL, convb_sb[:, c:c+1], convb[c*128:(c+1)*128, :])
                    dma(q, dL, dparam_sb[:, c:c+1], dparam[c*128:(c+1)*128, :])
                    dma(q, dL, wdt_sb[:, c:c+1], wdt[c*128:(c+1)*128, :])
                    dma(q, dL, bdt_sb[:, c:c+1], bdt[c*128:(c+1)*128, :])
                    dma(q, dL, A_sb[:, c*DS:(c+1)*DS], alog[c*128:(c+1)*128, :])
                    dma(q, dL, gb1_sb[:, c:c+1], gb1[c*128:(c+1)*128, :])
                for m in range(32):
                    dma(q, dL, bcols_sb[:, m:m+1], bcols[m*128:(m+1)*128, :])
                dma(q, dL, pbb[:, :], projb[0:1, :].broadcast_to((128, D)))
                dma(q, dL, gls05[:, :], lng[0:1, :].broadcast_to((128, D)))
                dma(q, dL, bls05[:, :], lnb[0:1, :].broadcast_to((128, D)))
                dma(q, dL, b2b[:, :], gb2[0:1, :].broadcast_to((128, C16)))
                dma(q, dL, idsb[:, :], ident[:, :])
                for k in range(8):
                    dma(q, dL, gam_sb[:, k:k+1], gam_m[k*128:(k+1)*128, :])
                    dma(q, dL, bet_sb[:, k:k+1], bet_m[k*128:(k+1)*128, :])
                for k in range(8):
                    dma(q, dL, wscr[:, k*1024:(k+1)*1024], win_t[k*128:(k+1)*128, :])
                dma(q, dL, sscr[0:16, 0:1024], colemb[:, :])
            sy(_)
            ld_done = dL.v

            def _(q, w=ld_done):
                q.wait_ge(dL.sem, w)
                last = None
                for k in range(8):
                    for m in range(8):
                        last = q.matmul(out=ps[6][0:128, m:m+1],
                                        lhsT=wscr[:, k*1024 + m*128:k*1024 + (m+1)*128],
                                        rhs=bet_sb[:, k:k+1],
                                        start=(k == 0), stop=(k == 7))
                eT.inc(last, 1)
            te(_)
            cbeta_done = eT.v

            def _(q, w=ld_done):
                q.wait_ge(dL.sem, w)
                eV.inc(q.tensor_tensor(out=sscr[0:16, 0:1024], in0=sscr[0:16, 0:1024],
                                       in1=sscr[0:16, 0:1024], op=ALU.mult), 1)
            ve(_)
            def _(q):
                eV.wait(q)
                q.tensor_reduce(out=colrb[0:16, 0:1], in_=sscr[0:16, 0:1024], op=ALU.add,
                                axis=AX.X)
                q.drain()
                q.tensor_scalar_max(out=colrb[0:16, 0:1], in0=colrb[0:16, 0:1],
                                    scalar1=1e-24)
                eV.inc(q.drain(), 1)
            ve(_)
            def _(q):
                eV.wait(q)
                q.activation(out=colrb[0:16, 1:2], in_=colrb[0:16, 0:1], func=AF.Ln)
                q.drain()
                q.activation(out=colrb[0:16, 0:1], in_=colrb[0:16, 1:2], func=AF.Exp, scale=-0.5)
                q.drain()
                eS.inc(q.activation(out=A_sb[:, :], in_=A_sb[:, :], func=AF.Exp), 1)
            sc(_)
            def _(q):
                eS.wait(q)
                eV.inc(q.tensor_scalar_mul(out=A_sb[:, :], in0=A_sb[:, :], scalar1=-1.0), 1)
            ve(_)
            def _(q):
                eS.wait(q)
                dma(q, dS, colr_dram[:, :], colrb[0:16, 0:1])
            sc(_)
            def _(q):
                dS.wait(q)
                dma(q, dL, colrb[:, :],
                    colr_dram.rearrange("a b -> b a").broadcast_to((128, C16)))
            sy(_)

            def _(q, w=cbeta_done):
                q.wait_ge(eT.sem, w)
                eS.inc(q.copy(out=cb_sb[:, 0:8], in_=ps[6][0:128, 0:8]), 1)
            sc(_)
            def _(q, w=cbeta_done):
                q.wait_ge(eT.sem, w)
                for k in range(8):
                    q.tensor_scalar_mul(out=wscr[:, k*1024:(k+1)*1024],
                                        in0=wscr[:, k*1024:(k+1)*1024],
                                        scalar1=gam_sb[:, k:k+1])
                eV.inc(q.tensor_copy(out=wbf[:, :], in_=wscr[:, :]), 1)
            ve(_)
            win_cast = eV.v
            def _(q, w=win_cast):
                q.wait_ge(eV.sem, w)
                dma(q, dS, winb_dram[:, :], wbf[:, :])
            sc(_)
            winb_st = dS.v

            def _(q, w=win_cast):
                q.wait_ge(eV.sem, w)
                for c in range(4):
                    dma(q, dL, wscr[:, c*1024:(c+1)*1024], wout_t[c*128:(c+1)*128, :])
                for c in range(4):
                    dma(q, dL, sscr[:, c*33:(c+1)*33], wx_t[c*128:(c+1)*128, :])
            sy(_)
            w2 = dL.v
            def _(q, w=w2, ws=winb_st):
                q.wait_ge(dL.sem, w)
                q.wait_ge(dS.sem, ws)
                q.tensor_copy(out=wbf[:, 0:4*D], in_=wscr[:, 0:4*D])
                q.tensor_copy(out=wbf[:, 4*D:4*D + 4*33], in_=sscr[:, 0:4*33])
                q.tensor_scalar_mul(out=gls05[:, :], in0=gls05[:, :], scalar1=0.5)
                eV.inc(q.tensor_scalar_mul(out=bls05[:, :], in0=bls05[:, :], scalar1=0.5), 1)
            ve(_)
            w3v = eV.v
            def _(q, w=w3v):
                q.wait_ge(eV.sem, w)
                dma(q, dS, woutb_dram[:, :], wbf[:, 0:4*D + 4*33])
            sc(_)
            def _(q, w=w3v):
                q.wait_ge(eV.sem, w)
                for k in range(8):
                    dma(q, dL, wscr[:, k*512:(k+1)*512], gw1[k*128:(k+1)*128, :])
                for k in range(4):
                    dma(q, dL, sscr[:, k*C16:(k+1)*C16], gw2[k*128:(k+1)*128, :])
                for k in range(8):
                    dma(q, dL, sscr[:, 128 + k*C16:128 + (k+1)*C16],
                        colembt[k*128:(k+1)*128, :])
            sy(_)
            w4 = dL.v
            def _(q, w=w4):
                q.wait_ge(dL.sem, w)
                q.tensor_copy(out=g1b[:, :], in_=wscr[:, 0:8*DH])
                q.tensor_copy(out=g2b[:, :], in_=sscr[:, 0:4*C16])
                q.tensor_copy(out=cetb[:, :], in_=sscr[:, 128:128 + 8*C16])
                eV.inc(q.memset(state[:, :], 0.0), 1)
            ve(_)
            def _(q):
                dG.wait(q)
            sy(_)
            barrier()

        ag_done_list = []
        rs_done_list = []
        def ffn_phase(phase):
            with ExitStack() as ph:
                Ep = ph.enter_context
                sfx = f"_p{phase}"
                x0 = Ep(nc.sbuf_tensor("x0" + sfx, [128, 2 * NPT * D], F32))
                xex = Ep(nc.sbuf_tensor("xex" + sfx, [128, 2 * NPT * D], F32))
                yb = Ep(nc.sbuf_tensor("yb" + sfx, [128, NPT * D], F32))
                U1 = Ep(nc.sbuf_tensor("U1" + sfx, [128, 8 * RPC], BF16))
                u1b = Ep(nc.sbuf_tensor("u1b" + sfx, [128, 2 * NPT * D], BF16))
                h1b = Ep(nc.sbuf_tensor("h1b" + sfx, [128, 4 * RPC], BF16))
                colsw = Ep(nc.sbuf_tensor("colsw" + sfx, [128, 32 * RPC], BF16))
                colsg = Ep(nc.sbuf_tensor("colsg" + sfx, [128, 2 * RPC], BF16))
                wcpan = Ep(nc.sbuf_tensor("wcpan" + sfx, [128, 4 * 1024], BF16))
                pwwin = Ep(nc.sbuf_tensor("pwwin" + sfx, [128, 4 * D], BF16))
                wrep = Ep(nc.sbuf_tensor("wrep" + sfx, [128, 4 * RPC], BF16))
                wtok = Ep(nc.sbuf_tensor("wtok" + sfx, [128, NPT * C16], F32))
                lgt = Ep(nc.sbuf_tensor("lgt" + sfx, [128, NPT * C16], F32))
                sgt = Ep(nc.sbuf_tensor("sgt" + sfx, [128, NPT * C16], F32))
                wfm = Ep(nc.sbuf_tensor("wfm" + sfx, [C16, RPC], BF16))
                st1 = Ep(nc.sbuf_tensor("st1" + sfx, [128, 2 * NPT * 12], F32))
                rstg = Ep(nc.sbuf_tensor("rstg" + sfx, [128, NPT * D], BF16))

                def stc(P, pt, i):
                    o = (P * NPT + pt) * 12 + i
                    return st1[:, o:o+1]

                def xs(P, pt):
                    return x0[:, (P * NPT + pt)*D:(P * NPT + pt + 1)*D]

                def xe(P, pt):
                    return xex[:, (P * NPT + pt)*D:(P * NPT + pt + 1)*D]

                def us(P, pt):
                    return u1b[:, (P * NPT + pt)*D:(P * NPT + pt + 1)*D]

                st = dict(prev_xn2_store=None, prev_xn2_par=0,
                          prev_proj_mm=eT.v, prev_proj_evac=eS.v,
                          prev_colsw_last=eV.v, prev_cols_mm_last=eT.v,
                          prev_stores=dS.v, u1ld=0, u1st=0)

                def emit_head_load(j):
                    P = j % 2
                    r0 = j * RPC
                    def _(q, r0=r0, d=st["prev_stores"], j=j, P=P):
                        q.wait_ge(dS.sem, d)
                        if phase == 2:
                            q.wait_ge(eG.sem, rs_done_list[j])
                        if phase == 1:
                            dma(q, sx0,
                                x0[:, P*NPT*D:(P+1)*NPT*D].rearrange(
                                    "p (a d) -> p a d", a=NPT),
                                xrows[r0:r0+RPC, :].rearrange("(a p) d -> p a d", p=128))
                        else:
                            dma(q, sx0,
                                xex[:, P*NPT*D:(P+1)*NPT*D].rearrange(
                                    "p (a d) -> p a d", a=NPT),
                                x1_dram[r0:r0+RPC, :].rearrange("(a p) d -> p a d", p=128))
                            dma(q, sx0,
                                rstg[:, :].rearrange("p (a d) -> p a d", a=NPT),
                                rsout[r0:r0+RPC, :].rearrange("(a p) d -> p a d", p=128))
                    gp(_)
                    st["xload"] = sx0.v

                def emit_head_rest(j):
                    P = j % 2
                    xload = st["xload"]
                    if phase == 2:
                        def _(q, j=j, P=P, w=xload):
                            q.wait_ge(sx0.sem, w)
                            for pt in range(NPT):
                                scol = (j * NPT + pt) * 2
                                q.tensor_tensor(out=stc(P, pt, 11),
                                                in0=stats3[:, scol:scol+1],
                                                in1=stats3[:, scol+1:scol+2],
                                                op=ALU.mult)
                            q.drain()
                            for pt in range(NPT):
                                q.tensor_scalar_mul(out=stc(P, pt, 11),
                                                    in0=stc(P, pt, 11), scalar1=-1.0)
                            q.drain()
                            for pt in range(NPT):
                                scol = (j * NPT + pt) * 2
                                r3 = stats3[:, scol+1:scol+2]
                                q.scalar_tensor_tensor(out=xs(P, pt), in0=xe(P, pt),
                                                       scalar=r3,
                                                       in1=rstg[:, pt*D:(pt+1)*D],
                                                       op0=ALU.mult, op1=ALU.add)
                            for pt in range(NPT):
                                q.tensor_scalar(out=xs(P, pt), in0=xs(P, pt),
                                                scalar1=stc(P, pt, 11), scalar2=None,
                                                op0=ALU.add)
                            for pt in range(NPT):
                                q.tensor_tensor(out=xs(P, pt), in0=xs(P, pt),
                                                in1=xe(P, pt), op=ALU.add)
                            eV.inc(q.drain(), 1)
                        ve(_)

                    def _(q, w=xload, P=P):
                        q.wait_ge(sx0.sem, w)
                        eV.wait(q)
                        for pt in range(NPT):
                            q.activation(out=xe(P, pt), in_=xs(P, pt),
                                         func=AF.Square, accum_out=stc(P, pt, 1))
                        for pt in range(NPT):
                            ii = q.activation(out=xe(P, pt), in_=xs(P, pt),
                                              func=AF.Identity, accum_out=stc(P, pt, 0))
                        eS.inc(ii, 1)
                    sc(_)
                    sq1 = eS.v
                    def _(q, w=xload, w2=sq1, P=P):
                        q.wait_ge(sx0.sem, w)
                        q.wait_ge(eS.sem, w2)
                        eV.wait(q)
                        for pt in range(NPT):
                            q.tensor_scalar_mul(out=stc(P, pt, 2), in0=stc(P, pt, 0),
                                                scalar1=1.0 / D)
                        for pt in range(NPT):
                            q.tensor_scalar_mul(out=stc(P, pt, 3), in0=stc(P, pt, 1),
                                                scalar1=1.0 / D)
                        q.drain()
                        for pt in range(NPT):
                            q.tensor_tensor(out=stc(P, pt, 11), in0=stc(P, pt, 2),
                                            in1=stc(P, pt, 2), op=ALU.mult)
                        q.drain()
                        for pt in range(NPT):
                            q.tensor_tensor(out=stc(P, pt, 3), in0=stc(P, pt, 3),
                                            in1=stc(P, pt, 11), op=ALU.subtract)
                        eV.inc(q.drain(), 1)
                    ve(_)
                    def _(q, P=P):
                        eV.wait(q)
                        for pt in range(NPT):
                            q.activation(out=stc(P, pt, 11), in_=stc(P, pt, 3),
                                         func=AF.Ln, bias=EPS)
                        q.drain()
                        for pt in range(NPT):
                            q.activation(out=stc(P, pt, 4), in_=stc(P, pt, 11),
                                         func=AF.Exp, scale=-0.5)
                        eS.inc(q.drain(), 1)
                    sc(_)
                    def _(q, P=P):
                        eS.wait(q)
                        for pt in range(NPT):
                            q.tensor_tensor(out=stc(P, pt, 5), in0=stc(P, pt, 2),
                                            in1=stc(P, pt, 4), op=ALU.mult)
                        for pt in range(NPT):
                            q.tensor_tensor(out=stc(P, pt, 6), in0=stc(P, pt, 4),
                                            in1=stc(P, pt, 4), op=ALU.mult)
                        q.drain()
                        for pt in range(NPT):
                            q.tensor_scalar_mul(out=stc(P, pt, 5), in0=stc(P, pt, 5),
                                                scalar1=-1.0)
                        for pt in range(NPT):
                            q.tensor_tensor(out=stc(P, pt, 6), in0=stc(P, pt, 6),
                                            in1=stc(P, pt, 3), op=ALU.mult)
                        q.drain()
                        for pt in range(NPT):
                            q.tensor_scalar_mul(out=stc(P, pt, 6), in0=stc(P, pt, 6),
                                                scalar1=float(D))
                        q.drain()
                        for pt in range(NPT):
                            q.tensor_scalar_max(out=stc(P, pt, 6), in0=stc(P, pt, 6),
                                                scalar1=1e-24)
                        eV.inc(q.drain(), 1)
                    ve(_)
                    def _(q, P=P, xw=st["prev_xn2_store"], xp=st["prev_xn2_par"]):
                        eV.wait(q)
                        if phase == 1 and xw is not None:
                            q.wait_ge(sxs[xp].sem, xw)
                        for pt in range(NPT):
                            q.activation(out=stc(P, pt, 11), in_=stc(P, pt, 6),
                                         func=AF.Ln)
                        q.drain()
                        for pt in range(NPT):
                            q.activation(out=stc(P, pt, 6), in_=stc(P, pt, 11),
                                         func=AF.Exp, scale=-0.5)
                        for pt in range(NPT):
                            q.activation(out=us(P, pt), in_=xs(P, pt),
                                         func=AF.Identity, scale=stc(P, pt, 4),
                                         bias=stc(P, pt, 5))
                        eS.inc(q.drain(), 1)
                    sc(_)

                    def _(q, P=P):
                        eS.wait(q)
                        dma(q, dS, u1b_dram.rearrange("(a p) d -> p a d", p=128),
                            u1b[:, P*NPT*D:(P+1)*NPT*D].rearrange("p (a d) -> p a d",
                                                                  a=NPT))
                    sc(_)
                    st["u1st"] = dS.v
                    def _(q, w=st["u1st"], pt_=st["prev_cols_mm_last"]):
                        q.wait_ge(dS.sem, w)
                        q.wait_ge(eT.sem, pt_)
                        for dt8 in range(8):
                            dma(q, su1, U1[:, dt8*RPC:(dt8+1)*RPC],
                                u1b_dram[:, dt8*128:(dt8+1)*128], transpose=True)
                    sy(_)
                    st["u1ld"] = su1.v

                def emit_gates(j):
                    P = j % 2
                    def _(q, w=st["u1ld"], pe=st["prev_proj_evac"]):
                        q.wait_ge(su1.sem, w)
                        q.wait_ge(eS.sem, pe)
                        last = None
                        for m4 in range(4):
                            for kk in range(8):
                                last = q.matmul(out=ps[m4][0:128, :],
                                                lhsT=g1b[:, kk*DH + m4*128: kk*DH + (m4+1)*128],
                                                rhs=U1[:, kk*RPC:(kk+1)*RPC],
                                                start=(kk == 0), stop=(kk == 7))
                        eT.inc(last, 1)
                    te(_)
                    g1t = eT.v
                    def _(q, w=g1t):
                        q.wait_ge(eT.sem, w)
                        for m4 in range(4):
                            ii = q.activation(out=h1b[:, m4*RPC:(m4+1)*RPC],
                                              in_=ps[m4][0:128, :],
                                              func=AF.Gelu, bias=gb1_sb[:, m4:m4+1])
                        eS.inc(ii, 1)
                    sc(_)

                    def _(q):
                        eS.wait(q)
                        last = None
                        for tb in range(NPT):
                            for k4 in range(4):
                                q.matmul(out=ps[tb][0:128, 0:C16],
                                         lhsT=h1b[:, k4*RPC + tb*128: k4*RPC + (tb+1)*128],
                                         rhs=g2b[:, k4*C16:(k4+1)*C16],
                                         start=(k4 == 0), stop=(k4 == 3))
                            for k8 in range(8):
                                last = q.matmul(out=ps[4+tb][0:128, 0:C16],
                                                lhsT=U1[:, k8*RPC + tb*128: k8*RPC + (tb+1)*128],
                                                rhs=cetb[:, k8*C16:(k8+1)*C16],
                                                start=(k8 == 0), stop=(k8 == 7))
                        eT.inc(last, 1)
                    te(_)
                    g2t = eT.v
                    def _(q, w=g2t, P=P):
                        q.wait_ge(eT.sem, w)
                        for tb in range(NPT):
                            q.tensor_tensor(out=sgt[:, tb*C16:(tb+1)*C16],
                                            in0=ps[tb][0:128, 0:C16], in1=b2b[:, :],
                                            op=ALU.add)
                        for tb in range(NPT):
                            q.tensor_tensor(out=lgt[:, tb*C16:(tb+1)*C16],
                                            in0=ps[4+tb][0:128, 0:C16], in1=colrb[:, :],
                                            op=ALU.mult)
                        q.drain()
                        for tb in range(NPT):
                            q.tensor_scalar_mul(out=lgt[:, tb*C16:(tb+1)*C16],
                                                in0=lgt[:, tb*C16:(tb+1)*C16],
                                                scalar1=stc(P, tb, 6))
                        eV.inc(q.drain(), 1)
                    ve(_)
                    def _(q):
                        eV.wait(q)
                        for tb in range(NPT):
                            q.activation(out=sgt[:, tb*C16:(tb+1)*C16],
                                         in_=sgt[:, tb*C16:(tb+1)*C16], func=AF.Sigmoid)
                        eS.inc(q.drain(), 1)
                    sc(_)
                    def _(q, P=P):
                        eS.wait(q)
                        for tb in range(NPT):
                            sl = lgt[:, tb*C16:(tb+1)*C16]
                            q.tensor_tensor(out=sl, in0=sl, in1=sgt[:, tb*C16:(tb+1)*C16],
                                            op=ALU.add)
                        q.drain()
                        for tb in range(NPT):
                            q.tensor_reduce(out=stc(P, tb, 11),
                                            in_=lgt[:, tb*C16:(tb+1)*C16],
                                            op=ALU.max, axis=AX.X)
                        q.drain()
                        for tb in range(NPT):
                            q.tensor_single_scalar(out=wtok[:, tb*C16:(tb+1)*C16],
                                                   in_=lgt[:, tb*C16:(tb+1)*C16],
                                                   scalar=stc(P, tb, 11), op=ALU.is_equal)
                        q.drain()
                        for tb in range(NPT):
                            q.tensor_scalar_mul(out=stc(P, tb, 11), in0=stc(P, tb, 11),
                                                scalar1=-1.0)
                        eV.inc(q.drain(), 1)
                    ve(_)
                    def _(q, P=P):
                        eV.wait(q)
                        for tb in range(NPT):
                            q.activation(out=sgt[:, tb*C16:(tb+1)*C16],
                                         in_=lgt[:, tb*C16:(tb+1)*C16],
                                         func=AF.Exp, bias=stc(P, tb, 11))
                        eS.inc(q.drain(), 1)
                    sc(_)
                    def _(q, P=P):
                        eS.wait(q)
                        for tb in range(NPT):
                            q.tensor_reduce(out=stc(P, tb, 11),
                                            in_=sgt[:, tb*C16:(tb+1)*C16],
                                            op=ALU.add, axis=AX.X)
                        q.drain()
                        for tb in range(NPT):
                            q.reciprocal(out=stc(P, tb, 11), in_=stc(P, tb, 11))
                        for tb in range(NPT):
                            q.tensor_tensor(out=wtok[:, tb*C16:(tb+1)*C16],
                                            in0=wtok[:, tb*C16:(tb+1)*C16],
                                            in1=sgt[:, tb*C16:(tb+1)*C16], op=ALU.mult)
                        q.drain()
                        for tb in range(NPT):
                            q.tensor_scalar_mul(out=wtok[:, tb*C16:(tb+1)*C16],
                                                in0=wtok[:, tb*C16:(tb+1)*C16],
                                                scalar1=stc(P, tb, 11))
                        eV.inc(q.drain(), 1)
                    ve(_)

                    def _(q):
                        eV.wait(q)
                        last = None
                        for tb in range(NPT):
                            last = q.transpose(out=ps[6][0:C16, tb*128:(tb+1)*128],
                                               in_=wtok[:, tb*C16:(tb+1)*C16],
                                               identity=idsb[:, :])
                        eT.inc(last, 1)
                    te(_)
                    wtr = eT.v
                    def _(q, w=wtr):
                        q.wait_ge(eT.sem, w)
                        eS.inc(q.copy(out=wfm[:, :], in_=ps[6][0:C16, 0:RPC]), 1)
                    sc(_)
                    def _(q):
                        eS.wait(q)
                        dma(q, dS, wfm_dram[:, :], wfm[:, :])
                    sc(_)
                    st["wfm_st"] = dS.v

                def emit_cols(j):
                    eS0, eV0 = eS.v, eV.v
                    st["eS0"], st["eV0"] = eS0, eV0
                    pan_ld, pan_used, wrep_ld = [], [], []

                    def _(q, w=st["wfm_st"], eV0=eV0, pv=st["prev_colsw_last"]):
                        q.wait_ge(dS.sem, w)
                        for c in range(C16):
                            if c >= 4:
                                q.wait_ge(eV.sem, eV0 + 2*(c-4) + 2)
                            else:
                                q.wait_ge(eV.sem, pv)
                            dma(q, swr[c % 4], wrep[:, (c % 4)*RPC:((c % 4)+1)*RPC],
                                wfm_dram[c:c+1, :].broadcast_to((128, RPC)))
                            wrep_ld.append(swr[c % 4].v)
                    gp(_)
                    def _(q, eT0=eT.v, pc=st["prev_cols_mm_last"]):
                        for m in range(32):
                            if m >= 4:
                                q.wait_ge(eT.sem, eT0 + m - 3)
                            else:
                                q.wait_ge(eT.sem, pc)
                            dma(q, swc[m % 4], wcpan[:, (m % 4)*1024:((m % 4)+1)*1024],
                                wcols_bf2[m*128:(m+1)*128, :])
                            pan_ld.append(swc[m % 4].v)
                    sy(_)
                    del _

                    def _(q):
                        for m in range(32):
                            q.wait_ge(swc[m % 4].sem, pan_ld[m])
                            if m >= 2:
                                q.wait_ge(eS.sem, eS0 + m - 1)
                            last = None
                            for kk in range(8):
                                last = q.matmul(
                                    out=ps[m % 2][0:128, :],
                                    lhsT=wcpan[:, (m % 4)*1024 + kk*128:
                                               (m % 4)*1024 + (kk+1)*128],
                                    rhs=U1[:, kk*RPC:(kk+1)*RPC],
                                    start=(kk == 0), stop=(kk == 7))
                            eT.inc(last, 1)
                            pan_used.append(eT.v)
                    te(_)
                    def _(q):
                        for m in range(32):
                            q.wait_ge(eT.sem, pan_used[m])
                            if m >= 2:
                                q.wait_ge(eV.sem, eV0 + m - 1)
                            eS.inc(q.activation(out=colsg[:, (m % 2)*RPC:((m % 2)+1)*RPC],
                                                in_=ps[m % 2][0:128, :], func=AF.Gelu,
                                                bias=bcols_sb[:, m:m+1]), 1)
                    sc(_)
                    def _(q):
                        for m in range(32):
                            c = m // 2
                            q.wait_ge(eS.sem, eS0 + m + 1)
                            q.wait_ge(swr[c % 4].sem, wrep_ld[c])
                            eV.inc(q.tensor_tensor(out=colsw[:, m*RPC:(m+1)*RPC],
                                                   in0=colsg[:, (m % 2)*RPC:((m % 2)+1)*RPC],
                                                   in1=wrep[:, (c % 4)*RPC:((c % 4)+1)*RPC],
                                                   op=ALU.mult), 1)
                    ve(_)
                    st["colsw_last"] = eV.v
                    st["prev_cols_mm_last"] = eT.v

                def emit_proj(j):
                    pw_ld = []
                    eT1 = eT.v
                    def _(q, ppm=st["prev_proj_mm"]):
                        for kk in range(32):
                            if kk >= 4:
                                q.wait_ge(eT.sem, eT1 + kk - 3)
                            else:
                                q.wait_ge(eT.sem, ppm)
                            dma(q, spw[kk % 4], pwwin[:, (kk % 4)*D:((kk % 4)+1)*D],
                                projw_bf[kk*128:(kk+1)*128, :])
                            pw_ld.append(spw[kk % 4].v)
                    sy(_)
                    def _(q, eS0=st["eS0"], eV0=st["eV0"]):
                        for kk in range(32):
                            q.wait_ge(spw[kk % 4].sem, pw_ld[kk])
                            q.wait_ge(eV.sem, eV0 + kk + 1)
                            if kk == 0:
                                q.wait_ge(eS.sem, eS0 + 32)
                            last = None
                            for tb in range(NPT):
                                for fh in range(2):
                                    last = q.matmul(
                                        out=ps[tb*2 + fh][0:128, :],
                                        lhsT=colsw[:, kk*RPC + tb*128: kk*RPC + (tb+1)*128],
                                        rhs=pwwin[:, (kk % 4)*D + fh*512:
                                                  (kk % 4)*D + (fh+1)*512],
                                        start=(kk == 0), stop=(kk == 31))
                            eT.inc(last, 1)
                    te(_)
                    st["prev_proj_mm"] = eT.v

                def emit_evac(j):
                    def _(q, tv=st["prev_proj_mm"]):
                        q.wait_ge(eT.sem, tv)
                        for tb in range(NPT):
                            for fh in range(2):
                                ii = q.copy(out=yb[:, tb*D + fh*512: tb*D + (fh+1)*512],
                                            in_=ps[tb*2 + fh][0:128, :])
                        eS.inc(ii, 1)
                    sc(_)
                    st["prev_proj_evac"] = eS.v

                def emit_tail(j):
                    P = j % 2
                    r0 = j * RPC
                    k = j
                    def _(q, P=P):
                        eS.wait(q)
                        for pt in range(NPT):
                            ysl = yb[:, pt*D:(pt+1)*D]
                            q.tensor_tensor(out=ysl, in0=ysl, in1=pbb[:, :], op=ALU.add)
                            q.scalar_tensor_tensor(out=ysl, in0=xs(P, pt),
                                                   scalar=stc(P, pt, 4), in1=ysl,
                                                   op0=ALU.mult, op1=ALU.add)
                            q.tensor_scalar(out=ysl, in0=ysl, scalar1=stc(P, pt, 5),
                                            scalar2=None, op0=ALU.add)
                        eV.inc(q.drain(), 1)
                    ve(_)
                    def _(q, P=P):
                        eV.wait(q)
                        for pt in range(NPT):
                            q.activation(out=xe(P, pt), in_=yb[:, pt*D:(pt+1)*D],
                                         func=AF.Square, accum_out=stc(P, pt, 8))
                        for pt in range(NPT):
                            ii = q.activation(out=xe(P, pt), in_=yb[:, pt*D:(pt+1)*D],
                                              func=AF.Identity, accum_out=stc(P, pt, 7))
                        eS.inc(ii, 1)
                    sc(_)
                    def _(q, P=P):
                        eS.wait(q)
                        for pt in range(NPT):
                            q.tensor_scalar_mul(out=stc(P, pt, 7), in0=stc(P, pt, 7),
                                                scalar1=1.0 / D)
                        for pt in range(NPT):
                            q.tensor_scalar_mul(out=stc(P, pt, 8), in0=stc(P, pt, 8),
                                                scalar1=1.0 / D)
                        q.drain()
                        for pt in range(NPT):
                            q.tensor_tensor(out=stc(P, pt, 11), in0=stc(P, pt, 7),
                                            in1=stc(P, pt, 7), op=ALU.mult)
                        q.drain()
                        for pt in range(NPT):
                            q.tensor_tensor(out=stc(P, pt, 8), in0=stc(P, pt, 8),
                                            in1=stc(P, pt, 11), op=ALU.subtract)
                        eV.inc(q.drain(), 1)
                    ve(_)
                    def _(q, P=P):
                        eV.wait(q)
                        for pt in range(NPT):
                            q.activation(out=stc(P, pt, 11), in_=stc(P, pt, 8),
                                         func=AF.Ln, bias=EPS)
                        q.drain()
                        for pt in range(NPT):
                            q.activation(out=stc(P, pt, 9), in_=stc(P, pt, 11),
                                         func=AF.Exp, scale=-0.5)
                        eS.inc(q.drain(), 1)
                    sc(_)
                    def _(q, P=P):
                        eS.wait(q)
                        for pt in range(NPT):
                            q.tensor_tensor(out=stc(P, pt, 10), in0=stc(P, pt, 7),
                                            in1=stc(P, pt, 9), op=ALU.mult)
                        q.drain()
                        for pt in range(NPT):
                            q.tensor_scalar_mul(out=stc(P, pt, 10), in0=stc(P, pt, 10),
                                                scalar1=-1.0)
                        eV.inc(q.drain(), 1)
                    ve(_)
                    def _(q, P=P):
                        eV.wait(q)
                        for pt in range(NPT):
                            ii = q.activation(out=xe(P, pt), in_=yb[:, pt*D:(pt+1)*D],
                                              func=AF.Identity, scale=stc(P, pt, 9),
                                              bias=stc(P, pt, 10))
                        eS.inc(ii, 1)
                    sc(_)
                    def _(q, P=P):
                        eS.wait(q)
                        for pt in range(NPT):
                            tl = xe(P, pt)
                            q.tensor_tensor(out=tl, in0=tl, in1=gls05[:, :], op=ALU.mult)
                            q.tensor_tensor(out=tl, in0=tl, in1=bls05[:, :], op=ALU.add)
                            ii = q.tensor_tensor(out=xs(P, pt), in0=xs(P, pt), in1=tl,
                                                 op=ALU.add)
                        eV.inc(ii, 1)
                    ve(_)

                    st["prev_colsw_last"] = st["colsw_last"]

                    if phase == 1:
                        def _(q, P=P):
                            eV.wait(q)
                            for pt in range(NPT):
                                q.activation(out=xe(P, pt), in_=xs(P, pt),
                                             func=AF.Square, accum_out=stc(P, pt, 1))
                            for pt in range(NPT):
                                ii = q.activation(out=xe(P, pt), in_=xs(P, pt),
                                                  func=AF.Identity,
                                                  accum_out=stc(P, pt, 0))
                            eS.inc(ii, 1)
                        sc(_)
                        def _(q, k=k, P=P):
                            eS.wait(q)
                            for pt in range(NPT):
                                scol = (k * NPT + pt) * 2
                                q.tensor_scalar_mul(out=stats3[:, scol:scol+1],
                                                    in0=stc(P, pt, 0), scalar1=1.0 / D)
                            for pt in range(NPT):
                                q.tensor_scalar_mul(out=stc(P, pt, 3), in0=stc(P, pt, 1),
                                                    scalar1=1.0 / D)
                            q.drain()
                            for pt in range(NPT):
                                scol = (k * NPT + pt) * 2
                                q.tensor_tensor(out=stc(P, pt, 11),
                                                in0=stats3[:, scol:scol+1],
                                                in1=stats3[:, scol:scol+1], op=ALU.mult)
                            q.drain()
                            for pt in range(NPT):
                                q.tensor_tensor(out=stc(P, pt, 3), in0=stc(P, pt, 3),
                                                in1=stc(P, pt, 11), op=ALU.subtract)
                            eV.inc(q.drain(), 1)
                        ve(_)
                        def _(q, k=k, P=P):
                            eV.wait(q)
                            for pt in range(NPT):
                                q.activation(out=stc(P, pt, 11), in_=stc(P, pt, 3),
                                             func=AF.Ln, bias=EPS)
                            q.drain()
                            for pt in range(NPT):
                                scol = (k * NPT + pt) * 2
                                q.activation(out=stats3[:, scol+1:scol+2],
                                             in_=stc(P, pt, 11), func=AF.Exp, scale=-0.5)
                            eS.inc(q.drain(), 1)
                        sc(_)
                        def _(q, k=k, P=P):
                            eS.wait(q)
                            for pt in range(NPT):
                                scol = (k * NPT + pt) * 2
                                q.tensor_tensor(out=stc(P, pt, 11),
                                                in0=stats3[:, scol+1:scol+2],
                                                in1=stats3[:, scol+1:scol+2], op=ALU.mult)
                            q.drain()
                            for pt in range(NPT):
                                q.tensor_tensor(out=stc(P, pt, 11), in0=stc(P, pt, 11),
                                                in1=stc(P, pt, 3), op=ALU.mult)
                            eV.inc(q.drain(), 1)
                        ve(_)
                        def _(q, P=P):
                            eV.wait(q)
                            for pt in range(NPT):
                                q.activation(out=stc(P, pt, 11), in_=stc(P, pt, 11),
                                             func=AF.Ln, bias=EPS)
                            q.drain()
                            for pt in range(NPT):
                                q.activation(out=stc(P, pt, 11), in_=stc(P, pt, 11),
                                             func=AF.Exp, scale=-0.5)
                            eS.inc(q.drain(), 1)
                        sc(_)
                        def _(q, k=k, P=P):
                            eS.wait(q)
                            for pt in range(NPT):
                                scol = (k * NPT + pt) * 2
                                q.tensor_tensor(out=stc(P, pt, 11), in0=stc(P, pt, 11),
                                                in1=stats3[:, scol+1:scol+2], op=ALU.mult)
                            q.drain()
                            for pt in range(NPT):
                                scol = (k * NPT + pt) * 2
                                q.tensor_tensor(out=stc(P, pt, 10),
                                                in0=stats3[:, scol:scol+1],
                                                in1=stc(P, pt, 11), op=ALU.mult)
                            q.drain()
                            for pt in range(NPT):
                                q.tensor_scalar_mul(out=stc(P, pt, 10), in0=stc(P, pt, 10),
                                                    scalar1=-1.0)
                            eV.inc(q.drain(), 1)
                        ve(_)
                        def _(q, P=P):
                            eV.wait(q)
                            for pt in range(NPT):
                                ii = q.activation(out=us(P, pt), in_=xs(P, pt),
                                                  func=AF.Identity, scale=stc(P, pt, 11),
                                                  bias=stc(P, pt, 10))
                            eS.inc(ii, 1)
                        sc(_)
                        def _(q, r0=r0, k=k, P=P):
                            eS.wait(q)
                            dma(q, sxs[k % 2],
                                xn2_slice[r0:r0+RPC, :].rearrange("(a p) d -> p a d", p=128),
                                u1b[:, P*NPT*D:(P+1)*NPT*D].rearrange("p (a d) -> p a d",
                                                                      a=NPT))
                            dma(q, dS, x1_dram[r0:r0+RPC, :].rearrange("(a p) d -> p a d",
                                                                       p=128),
                                x0[:, P*NPT*D:(P+1)*NPT*D].rearrange("p (a d) -> p a d",
                                                                     a=NPT))
                        sc(_)
                        st["prev_stores"] = dS.v
                        st["prev_xn2_store"] = sxs[k % 2].v
                        st["prev_xn2_par"] = k % 2
                        def _(q, k=k, w=st["prev_xn2_store"]):
                            q.wait_ge(sxs[k % 2].sem, w)
                            eG.inc(q.collective_compute(
                                "AllGather", ALU.bypass, replica_groups=GROUPS,
                                ins=[xn2_slice[k*RPC:(k+1)*RPC, :].opt()],
                                outs=[cag2[k, :, :].opt()]), 1)
                            ag_done_list.append(eG.v)
                        gp(_)
                    else:
                        def _(q, r0=r0, P=P):
                            eV.wait(q)
                            dma(q, dS,
                                outrows[r0:r0+RPC, :].rearrange("(a p) d -> p a d", p=128),
                                x0[:, P*NPT*D:(P+1)*NPT*D].rearrange("p (a d) -> p a d",
                                                                     a=NPT))
                        sc(_)
                        st["prev_stores"] = dS.v

                emit_head_load(0)
                emit_head_rest(0)
                emit_gates(0)
                emit_cols(0)
                for k in range(NCHK):
                    if k + 1 < NCHK:
                        emit_head_load(k + 1)
                    emit_proj(k)
                    if k + 1 < NCHK:
                        emit_head_rest(k + 1)
                    emit_evac(k)
                    if k + 1 < NCHK:
                        emit_gates(k + 1)
                        emit_cols(k + 1)
                    emit_tail(k)
                barrier()

        ffn_phase(1)

        with ExitStack() as ph:
            Em = ph.enter_context
            Xn2 = Em(nc.sbuf_tensor("Xn2", [128, 8 * TS], BF16))
            xbwin = Em(nc.sbuf_tensor("xbwin", [128, TS + 3], F32))
            halo = Em(nc.sbuf_tensor("halo", [128, 12], F32))
            zsb = Em(nc.sbuf_tensor("zsb", [128, 8 * TS], BF16))
            xcsb = Em(nc.sbuf_tensor("xcsb", [128, 4 * TS], F32))
            xcbw = Em(nc.sbuf_tensor("xcbw", [128, TS], BF16))
            dt4 = Em(nc.sbuf_tensor("dt4", [128, 4 * TS], F32))
            dtx4 = Em(nc.sbuf_tensor("dtx4", [128, 4 * TS], BF16))
            dtrawb = Em(nc.sbuf_tensor("dtrawb", [128, TS], F32))
            Ebuf = Em(nc.sbuf_tensor("Ebuf", [128, 2 * TS], F32))
            ubuf = Em(nc.sbuf_tensor("ubuf", [128, 2 * TS], BF16))
            hbuf = Em(nc.sbuf_tensor("hbuf", [128, TS], BF16))
            ytmp = Em(nc.sbuf_tensor("ytmp", [128, TS], BF16))
            tmpb = Em(nc.sbuf_tensor("tmpb", [128, TS], F32))
            acc4 = Em(nc.sbuf_tensor("acc4", [128, 4 * TS], F32))
            Brep = Em(nc.sbuf_tensor("Brep", [128, 2 * TS], BF16))
            Crep = Em(nc.sbuf_tensor("Crep", [128, 2 * TS], BF16))
            ygsb = Em(nc.sbuf_tensor("ygsb", [128, 4 * TS], BF16))
            ssm_sb = Em(nc.sbuf_tensor("ssm_sb", [128, TS], F32))
            evsb = Em(nc.sbuf_tensor("evsb", [128, D], BF16))
            winb = Em(nc.sbuf_tensor("winb", [128, 8 * 1024], BF16))
            woutb = Em(nc.sbuf_tensor("woutb", [128, 4 * D], BF16))
            wxb = Em(nc.sbuf_tensor("wxb", [128, 4 * 33], BF16))
            evsc = Em(nc.sbuf_tensor("evsc", [128, TS], F32))

            def _(q):
                q.memset(halo[:, :], 0.0)
                eV.inc(q.memset(state[:, :], 0.0), 1)
            ve(_)

            def _(q):
                dma(q, dL, winb[:, :], winb_dram[:, :])
                dma(q, dL, woutb[:, :], woutb_dram[:, 0:4*D])
                dma(q, dL, wxb[:, :], woutb_dram[:, 4*D:4*D + 4*33])
            sy(_)
            mwl = dL.v

            st2 = dict(win_done=eT.v, mh_evac=eS.v, wo_mm=eT.v, bc_cast=0)
            prev_scan_eV1 = None

            def emit_mhead(s):
                kch = s // 2
                hh = s % 2
                zb = (s % 2) * 4 * TS

                def _(q, kch=kch, hh=hh, a=st2["win_done"], g=ag_done_list[kch]):
                    q.wait_ge(eG.sem, g)
                    q.wait_ge(eT.sem, a)
                    for rb in range(2):
                        row0 = (2*hh + rb) * RPC
                        for dt8 in range(8):
                            dma(q, sxn, Xn2[:, dt8*TS + rb*512: dt8*TS + (rb+1)*512],
                                cag2[kch, row0:row0+512, dt8*128:(dt8+1)*128], transpose=True)
                sy(_)
                xn_ld = sxn.v

                eT0, eS0, eV0 = eT.v, eS.v, eV.v
                def winpos(m):
                    return m + 1 if m < 4 else 2*m - 2
                def wxpos(c):
                    return 5 + 2*c
                def evacpos(m):
                    return 2*m + 1 if m < 4 else m + 5

                def _(q, w=xn_ld, eT0=eT0, eS0=eS0, eV0=eV0, pw_=st2["mh_evac"], s=s):
                    q.wait_ge(sxn.sem, w)
                    if s == 0:
                        q.wait_ge(dL.sem, mwl)
                    order = [("w", 0), ("w", 1), ("w", 2), ("w", 3),
                             ("x", 0), ("w", 4), ("x", 1), ("w", 5),
                             ("x", 2), ("w", 6), ("x", 3), ("w", 7)]
                    for kind, i in order:
                        if kind == "w":
                            m = i
                            if m == 0:
                                q.wait_ge(eS.sem, pw_)
                            if m >= 1:
                                q.wait_ge(eS.sem, eS0 + evacpos(m-1))
                            last = None
                            for half in range(2):
                                for k8 in range(8):
                                    last = q.matmul(
                                        out=ps[6 + half][0:128, :],
                                        lhsT=winb[:, k8*1024 + m*128: k8*1024 + (m+1)*128],
                                        rhs=Xn2[:, k8*TS + half*512: k8*TS + (half+1)*512],
                                        start=(k8 == 0), stop=(k8 == 7))
                            eT.inc(last, 1)
                        else:
                            c = i
                            q.wait_ge(eV.sem, eV0 + 2*c + 2)
                            last = None
                            for half in range(2):
                                last = q.matmul(out=ps[4 + half][0:33, :],
                                                lhsT=wxb[:, c*33:(c+1)*33],
                                                rhs=xcbw[:, half*512:(half+1)*512],
                                                start=(c == 0), stop=(c == 3))
                            eT.inc(last, 1)
                te(_)
                st2["win_done"] = eT.v
                def _(q, eT0=eT0, eV0=eV0, zb=zb):
                    for m in range(8):
                        q.wait_ge(eT.sem, eT0 + winpos(m))
                        if 1 <= m < 4:
                            q.wait_ge(eV.sem, eV0 + 2*(m-1) + 1)
                        for half in range(2):
                            if m < 4:
                                ii = q.activation(out=xbwin[:, 3 + half*512: 3 + (half+1)*512],
                                                  in_=ps[6 + half][0:128, :], func=AF.Identity,
                                                  bias=cb_sb[:, m:m+1])
                            else:
                                ii = q.activation(out=zsb[:, zb + (m-4)*TS + half*512:
                                                          zb + (m-4)*TS + (half+1)*512],
                                                  in_=ps[6 + half][0:128, :], func=AF.Identity,
                                                  bias=cb_sb[:, m:m+1])
                        eS.inc(ii, 1)
                        if m < 4:
                            c = m
                            q.wait_ge(eV.sem, eV0 + 2*c + 1)
                            eS.inc(q.activation(out=xcsb[:, c*TS:(c+1)*TS],
                                                in_=xcsb[:, c*TS:(c+1)*TS], func=AF.Silu), 1)
                sc(_)
                st2["mh_evac"] = eS.v
                def _(q, eS0=eS0, eT0=eT0):
                    for c in range(4):
                        q.wait_ge(eS.sem, eS0 + 2*c + 1)
                        q.tensor_copy(out=xbwin[:, 0:3], in_=halo[:, c*3:(c+1)*3])
                        q.tensor_scalar(out=xcsb[:, c*TS:(c+1)*TS], in0=xbwin[:, 3:3+TS],
                                        scalar1=convw_sb[:, c*DC+3:c*DC+4],
                                        scalar2=convb_sb[:, c:c+1],
                                        op0=ALU.mult, op1=ALU.add)
                        for kk in (2, 1, 0):
                            q.scalar_tensor_tensor(out=xcsb[:, c*TS:(c+1)*TS],
                                                   in0=xbwin[:, kk:kk+TS],
                                                   scalar=convw_sb[:, c*DC+kk:c*DC+kk+1],
                                                   in1=xcsb[:, c*TS:(c+1)*TS],
                                                   op0=ALU.mult, op1=ALU.add)
                        eV.inc(q.tensor_copy(out=halo[:, c*3:(c+1)*3],
                                             in_=xbwin[:, TS:TS+3]), 1)
                        q.wait_ge(eS.sem, eS0 + 2*c + 2)
                        if c >= 1:
                            q.wait_ge(eT.sem, eT0 + wxpos(c-1))
                        eV.inc(q.tensor_copy(out=xcbw[:, :], in_=xcsb[:, c*TS:(c+1)*TS]), 1)
                ve(_)

                def _(q):
                    eT.wait(q)
                    q.copy(out=ssm_sb[0:33, 0:512], in_=ps[4][0:33, :])
                    eS.inc(q.copy(out=ssm_sb[0:33, 512:1024], in_=ps[5][0:33, :]), 1)
                    q.wait_ge(eS.sem, eS.v)
                    dma(q, dS, ssm_in[s, :, :], ssm_sb[0:33, 0:TS])
                sc(_)
                def _(q, s=s):
                    q.wait_ge(dS.sem, dS.v)
                    eG.inc(q.collective_compute("AllReduce", ALU.add, replica_groups=GROUPS,
                                                ins=[ssm_in[s, :, :].opt()],
                                                outs=[ssm_out[s, :, :].opt()]), 1)
                gp(_)
                ar_done = eG.v
                def _(q, s=s, w=ar_done):
                    q.wait_ge(eG.sem, w)
                    dma(q, dG, ssm_bf[s, :, :], ssm_out[s, 1:33, :])
                gp(_)
                st2["bc_cast"] = dG.v

                def _(q, s=s, w=ar_done):
                    q.wait_ge(eG.sem, w)
                    dma(q, sdtr, dtrawb[:, :], ssm_out[s, 0:1, :].broadcast_to((128, TS)))
                sy(_)
                dtr_ld = sdtr.v
                eSd0, eVd0 = eS.v, eV.v
                def _(q, w=dtr_ld, eVd0=eVd0):
                    q.wait_ge(sdtr.sem, w)
                    for c in range(4):
                        if c >= 1:
                            q.wait_ge(eV.sem, eVd0 + c)
                        q.activation(out=tmpb[:, :], in_=dtrawb[:, :], func=AF.Abs,
                                     scale=wdt_sb[:, c:c+1], bias=bdt_sb[:, c:c+1])
                        q.activation(out=tmpb[:, :], in_=tmpb[:, :], func=AF.Exp, scale=-1.0)
                        q.activation(out=tmpb[:, :], in_=tmpb[:, :], func=AF.Ln, bias=1.0)
                        eS.inc(q.activation(out=dt4[:, c*TS:(c+1)*TS], in_=dtrawb[:, :],
                                            func=AF.Relu, scale=wdt_sb[:, c:c+1],
                                            bias=bdt_sb[:, c:c+1]), 1)
                sc(_)
                def _(q, eSd0=eSd0):
                    for c in range(4):
                        q.wait_ge(eS.sem, eSd0 + c + 1)
                        q.tensor_tensor(out=dt4[:, c*TS:(c+1)*TS], in0=dt4[:, c*TS:(c+1)*TS],
                                        in1=tmpb[:, :], op=ALU.add)
                        eV.inc(q.tensor_tensor(out=dtx4[:, c*TS:(c+1)*TS],
                                               in0=dt4[:, c*TS:(c+1)*TS],
                                               in1=xcsb[:, c*TS:(c+1)*TS],
                                               op=ALU.mult), 1)
                ve(_)

            emit_mhead(0)
            for s in range(NSUB):
                eS1, eV1 = eS.v, eV.v
                bc_ld = []
                def _(q, s=s, eV1=eV1, pv=prev_scan_eV1, wc=st2["bc_cast"]):
                    q.wait_ge(dG.sem, wc)
                    for n in range(DS):
                        slot = n % 2
                        if n >= 2:
                            q.wait_ge(eV.sem, eV1 + 4*(n-2) + 4)
                        elif pv is not None:
                            q.wait_ge(eV.sem, pv + 60 + 4*n)
                        dma(q, sbc[slot], Brep[:, slot*TS:(slot+1)*TS],
                            ssm_bf[s, n:n+1, :].broadcast_to((128, TS)))
                        dma(q, sbc[slot], Crep[:, slot*TS:(slot+1)*TS],
                            ssm_bf[s, 16+n:17+n, :].broadcast_to((128, TS)))
                        bc_ld.append(sbc[slot].v)
                sy(_)
                def _(q, eS1=eS1, eV1=eV1):
                    for g in range(4 * DS):
                        n, c = g // 4, g % 4
                        eslot = g % 2
                        if g >= 2:
                            q.wait_ge(eV.sem, eV1 + g - 1)
                        eS.inc(q.activation(out=Ebuf[:, eslot*TS:(eslot+1)*TS],
                                            in_=dt4[:, c*TS:(c+1)*TS],
                                            func=AF.Exp,
                                            scale=A_sb[:, c*DS+n:c*DS+n+1]), 1)
                sc(_)
                def _(q, eS1=eS1):
                    for g in range(4 * DS):
                        n, c = g // 4, g % 4
                        slot = n % 2
                        eslot = g % 2
                        q.wait_ge(eS.sem, eS1 + g + 1)
                        q.wait_ge(sbc[n % 2].sem, bc_ld[n])
                        q.tensor_tensor(out=ubuf[:, eslot*TS:(eslot+1)*TS],
                                        in0=dtx4[:, c*TS:(c+1)*TS],
                                        in1=Brep[:, slot*TS:(slot+1)*TS], op=ALU.mult)
                        q.tensor_tensor_scan(out=hbuf[:, :],
                                             data0=Ebuf[:, eslot*TS:(eslot+1)*TS],
                                             data1=ubuf[:, eslot*TS:(eslot+1)*TS],
                                             initial=state[:, c*DS+n:c*DS+n+1],
                                             op0=ALU.mult, op1=ALU.add)
                        if n == 0:
                            q.tensor_tensor(out=acc4[:, c*TS:(c+1)*TS], in0=hbuf[:, :],
                                            in1=Crep[:, slot*TS:(slot+1)*TS], op=ALU.mult)
                        else:
                            q.tensor_tensor(out=ytmp[:, :], in0=hbuf[:, :],
                                            in1=Crep[:, slot*TS:(slot+1)*TS], op=ALU.mult)
                            q.tensor_tensor(out=acc4[:, c*TS:(c+1)*TS],
                                            in0=acc4[:, c*TS:(c+1)*TS],
                                            in1=ytmp[:, :], op=ALU.add)
                        eV.inc(q.tensor_copy(out=state[:, c*DS+n:c*DS+n+1],
                                             in_=hbuf[:, TS-1:TS]), 1)
                    for c in range(4):
                        q.scalar_tensor_tensor(out=acc4[:, c*TS:(c+1)*TS],
                                               in0=xcsb[:, c*TS:(c+1)*TS],
                                               scalar=dparam_sb[:, c:c+1],
                                               in1=acc4[:, c*TS:(c+1)*TS],
                                               op0=ALU.mult, op1=ALU.add)
                ve(_)

                prev_scan_eV1 = eV1
                if s + 1 < NSUB:
                    emit_mhead(s + 1)

                eS2, eV2 = eS.v, eV.v
                zb2 = (s % 2) * 4 * TS
                def _(q, eV2=eV2, zb2=zb2):
                    for c in range(4):
                        q.wait_ge(eV.sem, eV2 + c)
                        eS.inc(q.activation(out=evsc[:, :],
                                            in_=zsb[:, zb2 + c*TS: zb2 + (c+1)*TS],
                                            func=AF.Silu), 1)
                sc(_)
                def _(q, eS2=eS2, pw_=st2["wo_mm"]):
                    q.wait_ge(eT.sem, pw_)
                    for c in range(4):
                        q.wait_ge(eS.sem, eS2 + c + 1)
                        eV.inc(q.tensor_tensor(out=ygsb[:, c*TS:(c+1)*TS],
                                               in0=acc4[:, c*TS:(c+1)*TS],
                                               in1=evsc[:, :], op=ALU.mult), 1)
                ve(_)

                eT2, eS3, dS0, eV3 = eT.v, eS.v, dS.v, eV.v
                def _(q, eT2=eT2, eS3=eS3, eV3=eV3):
                    q.wait_ge(eV.sem, eV3)
                    for tb in range(8):
                        if tb >= 2:
                            q.wait_ge(eS.sem, eS3 + tb - 1)
                        last = None
                        for fh in range(2):
                            for c in range(4):
                                last = q.matmul(out=ps[2*(tb % 2) + fh][0:128, :],
                                                lhsT=ygsb[:, c*TS + tb*128: c*TS + (tb+1)*128],
                                                rhs=woutb[:, c*D + fh*512: c*D + (fh+1)*512],
                                                start=(c == 0), stop=(c == 3))
                        eT.inc(last, 1)
                te(_)
                st2["wo_mm"] = eT.v
                swp0 = swp.v
                def _(q, s=s, eT2=eT2, eS3=eS3, swp0=swp0):
                    for tb in range(8):
                        q.wait_ge(eT.sem, eT2 + tb + 1)
                        if tb >= 1:
                            q.wait_ge(swp.sem, swp0 + 16*tb)
                        for fh in range(2):
                            ii = q.copy(out=evsb[:, fh*512:(fh+1)*512],
                                        in_=ps[2*(tb % 2) + fh][0:128, :])
                        eS.inc(ii, 1)
                        q.wait_ge(eS.sem, eS3 + tb + 1)
                        p = s * TS + tb * 128
                        r = (p % 2048) // 512
                        row = (p // 2048) * 4 * RPC + r * RPC + (p % 512)
                        dma(q, swp, wpart[row:row+128, :], evsb[:, 0:D])
                sc(_)
                if s % 2 == 1:
                    kc = s // 2
                    def _(q, kc=kc, w=swp.v):
                        q.wait_ge(swp.sem, w)
                        eG.inc(q.collective_compute(
                            "ReduceScatter", ALU.add, replica_groups=GROUPS,
                            ins=[wpart[kc*4*RPC:(kc+1)*4*RPC, :].opt()],
                            outs=[rsout[kc*RPC:(kc+1)*RPC, :].opt()]), 1)
                        rs_done_list.append(eG.v)
                    gp(_)
            barrier()

        ffn_phase(2)

        def _(q):
            q.wait_ge(dS.sem, dS.v)
            if DBG:
                q.wait_ge(dL.sem, dL.v)
                dma(q, dG, dbg_x1[:, :], x1_dram[:, :])
                dma(q, dG, dbg_xn2[:, :], xn2_slice[:, :])
                dma(q, dG, dbg_cag[:, :], cag2.rearrange("k r d -> (k r) d"))
                dma(q, dG, dbg_ssm[:, :, :], ssm_out[:, :, :])
                dma(q, dG, dbg_rs[:, :], rsout[:, :])
                dma(q, dG, dbg_wp[:, :], wpart[:, :])
                q.wait_ge(dG.sem, dG.v)
        gp(_)

    return nc


LAST_RES = None


def kernel(**inputs):
    x = np.asarray(inputs["L0_attn_x"], np.float32)
    W_in = np.asarray(inputs["L0_attn_W_in"], np.float32)
    gam = np.asarray(inputs["L0_attn_ln_gamma"], np.float32)
    bet = np.asarray(inputs["L0_attn_ln_beta"], np.float32)
    conv_w = np.asarray(inputs["L0_attn_conv_w"], np.float32)
    conv_b = np.asarray(inputs["L0_attn_conv_b"], np.float32)
    W_x = np.asarray(inputs["L0_attn_W_x"], np.float32)
    w_dt = np.asarray(inputs["L0_attn_w_dt"], np.float32)
    b_dt = np.asarray(inputs["L0_attn_b_dt"], np.float32)
    A_log = np.asarray(inputs["L0_attn_A_log"], np.float32)
    D_param = np.asarray(inputs["L0_attn_D_param"], np.float32)
    W_out = np.asarray(inputs["L0_attn_W_out"], np.float32)
    col_emb = np.asarray(inputs["L0_ffn_col_emb"], np.float32)
    gate_w1 = np.asarray(inputs["L0_ffn_gate_w1"], np.float32)
    gate_b1 = np.asarray(inputs["L0_ffn_gate_b1"], np.float32)
    gate_w2 = np.asarray(inputs["L0_ffn_gate_w2"], np.float32)
    gate_b2 = np.asarray(inputs["L0_ffn_gate_b2"], np.float32)
    W_cols = np.asarray(inputs["L0_ffn_W_cols"], np.float32)
    b_cols = np.asarray(inputs["L0_ffn_b_cols"], np.float32)
    proj_w = np.asarray(inputs["L0_ffn_proj_w"], np.float32)
    proj_b = np.asarray(inputs["L0_ffn_proj_b"], np.float32)
    ln_g = np.asarray(inputs["L0_ffn_ln_g"], np.float32)
    ln_b = np.asarray(inputs["L0_ffn_ln_b"], np.float32)

    nc = build_nc()

    C = np.ascontiguousarray
    in_maps = []
    for core in range(NCORE):
        b = core // 4
        qq = core % 4
        ch0 = qq * CHS
        xr = C(x[b].reshape(NCHK, 4, RPC, D)[:, qq].reshape(NROWS, D))
        win_slice = np.concatenate([W_in[ch0:ch0+CHS], W_in[DI+ch0:DI+ch0+CHS]], axis=0)
        m = dict(
            xrows=xr,
            win_t=C(win_slice.T),
            gam_m=C(gam.reshape(D, 1)),
            bet_m=C(bet.reshape(D, 1)),
            wout_t=C(W_out[:, ch0:ch0+CHS].T),
            wx_t=C(W_x[:, ch0:ch0+CHS].T),
            convw=C(conv_w[ch0:ch0+CHS]),
            convb=C(conv_b[ch0:ch0+CHS].reshape(CHS, 1)),
            dparam=C(D_param[ch0:ch0+CHS].reshape(CHS, 1)),
            wdt=C(w_dt[ch0:ch0+CHS].reshape(CHS, 1)),
            bdt=C(b_dt[ch0:ch0+CHS].reshape(CHS, 1)),
            alog=C(A_log[ch0:ch0+CHS]),
            gw1=gate_w1,
            gb1=C(gate_b1.reshape(DH, 1)),
            gw2=gate_w2,
            gb2=C(gate_b2.reshape(1, C16)),
            wcols_t=C(W_cols.transpose(1, 0, 2).reshape(D, CN)),
            bcols=C(b_cols.reshape(CN, 1)),
            projw=proj_w,
            projb=C(proj_b.reshape(1, D)),
            lng=C(ln_g.reshape(1, D)),
            lnb=C(ln_b.reshape(1, D)),
            colemb=col_emb,
            colembt=C(col_emb.T),
            ident=np.eye(128, dtype=np.float32),
        )
        in_maps.append(m)

    res = run_bass_kernel_spmd(nc, in_maps, core_ids=list(range(NCORE)))
    global LAST_RES
    LAST_RES = res

    out = np.empty((B, T, D), np.float32)
    for core in range(NCORE):
        b = core // 4
        qq = core % 4
        out[b].reshape(NCHK, 4, RPC, D)[:, qq] = \
            res.results[core]["outrows"].reshape(NCHK, RPC, D)
    return out


# version 26 (final)
# speedup vs baseline: 1.0226x; 1.0117x over previous
import numpy as np
from contextlib import ExitStack

import concourse.bass as bass
import concourse.mybir as mybir
from concourse.bass_utils import run_bass_kernel_spmd

F32 = mybir.dt.float32
BF16 = mybir.dt.bfloat16
AF = mybir.ActivationFunctionType
ALU = mybir.AluOpType
AX = mybir.AxisListType

B, T, D = 2, 16384, 1024
DI, DS, DC = 2048, 16, 4
C16, NN, DH = 16, 256, 512
CN = C16 * NN
CHS = 512
NCHK = 8
RPC = 512
NPT = 4
NSUB = 16
TS = 1024
EPS = 1e-5
GROUPS = [[0, 1, 2, 3], [4, 5, 6, 7]]
NROWS = NCHK * RPC
NCORE = 8
DBG = False
DBG_PHASE = 1


class Ctr:
    def __init__(self, stack, nc, name):
        self.sem = stack.enter_context(nc.semaphore(name))
        self.v = 0

    def inc(self, instr, n=1):
        instr.then_inc(self.sem, n)
        self.v += n

    def wait(self, eng, v=None):
        eng.wait_ge(self.sem, self.v if v is None else v)


def build_nc():
    nc = bass.Bass("TRN2", target_bir_lowering=False, debug=False)

    def din(name, shape):
        return nc.dram_tensor(name, shape, F32, kind="ExternalInput").ap()

    xrows = din("xrows", [NROWS, D])
    win_t = din("win_t", [D, 2 * CHS])
    gam_m = din("gam_m", [D, 1])
    bet_m = din("bet_m", [D, 1])
    wout_t = din("wout_t", [CHS, D])
    wx_t = din("wx_t", [CHS, 33])
    convw = din("convw", [CHS, DC])
    convb = din("convb", [CHS, 1])
    dparam = din("dparam", [CHS, 1])
    wdt = din("wdt", [CHS, 1])
    bdt = din("bdt", [CHS, 1])
    alog = din("alog", [CHS, DS])
    gw1 = din("gw1", [D, DH])
    gb1 = din("gb1", [DH, 1])
    gw2 = din("gw2", [DH, C16])
    gb2 = din("gb2", [1, C16])
    wcols_t = din("wcols_t", [D, CN])
    bcols = din("bcols", [CN, 1])
    projw = din("projw", [CN, D])
    projb = din("projb", [1, D])
    lng = din("lng", [1, D])
    lnb = din("lnb", [1, D])
    colemb = din("colemb", [C16, D])
    colembt = din("colembt", [D, C16])
    ident = din("ident", [128, 128])

    outrows = nc.dram_tensor("outrows", [NROWS, D], F32, kind="ExternalOutput").ap()
    dbgt = {}
    if DBG:
        for nm, cols, dt in [("u1b", NPT * D, BF16), ("st1", NPT * 12, F32),
                             ("U1", 8 * RPC, BF16), ("h1b", 4 * RPC, BF16),
                             ("wtok", NPT * C16, F32), ("lgt", NPT * C16, F32),
                             ("colsw", 32 * RPC, BF16), ("ybraw", NPT * D, F32),
                             ("yb2", NPT * D, F32), ("x0f", NPT * D, F32)]:
            dbgt[nm] = nc.dram_tensor("dbg_" + nm, [128, cols], dt,
                                      kind="ExternalOutput").ap()
        dbg_x1 = nc.dram_tensor("dbg_x1", [NROWS, D], F32, kind="ExternalOutput").ap()
        dbg_xn2 = nc.dram_tensor("dbg_xn2", [NROWS, D], BF16, kind="ExternalOutput").ap()
        dbg_cag = nc.dram_tensor("dbg_cag", [4 * NROWS, D], BF16, kind="ExternalOutput").ap()
        dbg_ssm = nc.dram_tensor("dbg_ssm", [NSUB, 33, TS], F32, kind="ExternalOutput").ap()
        dbg_rs = nc.dram_tensor("dbg_rs", [NROWS, D], F32, kind="ExternalOutput").ap()
        dbg_wp = nc.dram_tensor("dbg_wp", [4 * NROWS, D], F32, kind="ExternalOutput").ap()

    wcols_bf2 = nc.dram_tensor("wcols_bf2", [32 * 128, 8 * 128], BF16).ap()
    projw_bf = nc.dram_tensor("projw_bf", [CN, D], BF16).ap()
    u1b_dram = nc.dram_tensor("u1b_dram", [RPC, D], BF16).ap()
    xn2_slice = nc.dram_tensor("xn2_slice", [NROWS, D], BF16).ap()
    cag2 = nc.dram_tensor("cag2", [NCHK, 4 * RPC, D], BF16).ap()
    x1_dram = nc.dram_tensor("x1_dram", [NROWS, D], F32).ap()
    ssm_in = nc.dram_tensor("ssm_in", [NSUB, 33, TS], F32).ap()
    ssm_out = nc.dram_tensor("ssm_out", [NSUB, 33, TS], F32).ap()
    ssm_bf = nc.dram_tensor("ssm_bf", [NSUB, 32, TS], BF16).ap()
    winb_dram = nc.dram_tensor("winb_dram", [128, 8 * 1024], BF16).ap()
    woutb_dram = nc.dram_tensor("woutb_dram", [128, 4 * D + 4 * 33], BF16).ap()
    wpart = nc.dram_tensor("wpart", [4 * NROWS, D], BF16).ap()
    rsout = nc.dram_tensor("rsout", [NROWS, D], BF16).ap()
    wfm_dram = nc.dram_tensor("wfm_dram", [C16, RPC], BF16).ap()
    colr_dram = nc.dram_tensor("colr_dram", [C16, 1], F32).ap()

    with ExitStack() as perm:
        E = perm.enter_context
        block = E(nc.Block())

        def sy(fn): block.sync(fn)
        def ve(fn): block.vector(fn)
        def sc(fn): block.scalar(fn)
        def te(fn): block.tensor(fn)
        def gp(fn): block.gpsimd(fn)

        dL = Ctr(perm, nc, "dL")
        sx0 = Ctr(perm, nc, "sx0")
        su1 = Ctr(perm, nc, "su1")
        swc = [Ctr(perm, nc, f"swc{i}") for i in range(4)]
        spw = [Ctr(perm, nc, f"spw{i}") for i in range(4)]
        swr = [Ctr(perm, nc, f"swr{i}") for i in range(4)]
        sxs = [Ctr(perm, nc, "sxsA"), Ctr(perm, nc, "sxsB")]
        swp = Ctr(perm, nc, "swp")
        sxn = Ctr(perm, nc, "sxn")
        sdtr = Ctr(perm, nc, "sdtr")
        sbc = [Ctr(perm, nc, "sbcA"), Ctr(perm, nc, "sbcB")]
        dS = Ctr(perm, nc, "dS")
        dG = Ctr(perm, nc, "dG")
        eV = Ctr(perm, nc, "eV")
        eS = Ctr(perm, nc, "eS")
        eT = Ctr(perm, nc, "eT")
        eG = Ctr(perm, nc, "eG")
        bar = Ctr(perm, nc, "bar")

        def dma(q, ctr, out, in_, **kw):
            ctr.inc(q.dma_start(out=out, in_=in_, **kw), 16)
            return ctr.v

        def dump(name, sb, prod_ctr):
            if not DBG:
                return
            snap = prod_ctr.v
            def _(q, snap=snap, name=name, sb=sb, ctr=prod_ctr):
                q.wait_ge(ctr.sem, snap)
                dma(q, dS, dbgt[name][:, :], sb)
                q.wait_ge(dS.sem, dS.v)
            sy(_)
            barrier()

        def barrier():
            def _(q):
                for c in (dS, dL, dG, sxs[0], sxs[1], swp, sx0, su1, sxn):
                    q.wait_ge(c.sem, c.v)
            sy(_)
            for emit in (sy, ve, sc, te, gp):
                def _(q):
                    q.drain()
                    q.sem_inc(bar.sem, 1)
                emit(_)
            bar.v += 5
            tgt = bar.v
            for emit in (sy, ve, sc, te, gp):
                def _(q, tgt=tgt):
                    q.wait_ge(bar.sem, tgt)
                emit(_)

        convw_sb = E(nc.sbuf_tensor("convw_sb", [128, 4 * DC], F32))
        convb_sb = E(nc.sbuf_tensor("convb_sb", [128, 4], F32))
        dparam_sb = E(nc.sbuf_tensor("dparam_sb", [128, 4], F32))
        wdt_sb = E(nc.sbuf_tensor("wdt_sb", [128, 4], F32))
        bdt_sb = E(nc.sbuf_tensor("bdt_sb", [128, 4], F32))
        A_sb = E(nc.sbuf_tensor("A_sb", [128, 4 * DS], F32))
        gb1_sb = E(nc.sbuf_tensor("gb1_sb", [128, 4], F32))
        bcols_sb = E(nc.sbuf_tensor("bcols_sb", [128, 32], F32))
        cb_sb = E(nc.sbuf_tensor("cb_sb", [128, 8], F32))
        pbb = E(nc.sbuf_tensor("pbb", [128, D], F32))
        gls05 = E(nc.sbuf_tensor("gls05", [128, D], F32))
        bls05 = E(nc.sbuf_tensor("bls05", [128, D], F32))
        b2b = E(nc.sbuf_tensor("b2b", [128, C16], F32))
        colrb = E(nc.sbuf_tensor("colrb", [128, C16], F32))
        idsb = E(nc.sbuf_tensor("idsb", [128, 128], F32))
        g1b = E(nc.sbuf_tensor("g1b", [128, 8 * DH], BF16))
        g2b = E(nc.sbuf_tensor("g2b", [128, 4 * C16], BF16))
        cetb = E(nc.sbuf_tensor("cetb", [128, 8 * C16], BF16))
        stats3 = E(nc.sbuf_tensor("stats3", [128, NCHK * NPT * 2], F32))
        state = E(nc.sbuf_tensor("state", [128, 4 * DS], F32))
        gam_sb = E(nc.sbuf_tensor("gam_sb", [128, 8], F32))
        bet_sb = E(nc.sbuf_tensor("bet_sb", [128, 8], F32))
        epsc = E(nc.sbuf_tensor("epsc", [128, 1], F32))
        nc.const_aps.register(F32, EPS, epsc[:, 0:1])

        ps = [E(nc.psum_tensor(f"ps{i}", [128, 512], F32)) for i in range(8)]

        with ExitStack() as ph0:
            E0 = ph0.enter_context
            wscr = E0(nc.sbuf_tensor("wscr", [128, 8 * 1024], F32))
            sscr = E0(nc.sbuf_tensor("sscr", [128, 1024], F32))
            wbf = E0(nc.sbuf_tensor("wbf", [128, 8 * 1024], BF16))

            def _(q):
                for m in range(32):
                    dma(q, dG,
                        wcols_bf2[m*128:(m+1)*128, :].rearrange("p (g q) -> p g q", g=8),
                        wcols_t[:, m*128:(m+1)*128].rearrange("(g p) q -> p g q", p=128))
                dma(q, dG, projw_bf[:, :], projw[:, :])
            gp(_)

            def _(q):
                eV.inc(q.memset(epsc[:, :], EPS), 1)
            ve(_)

            def _(q):
                for c in range(4):
                    dma(q, dL, convw_sb[:, c*DC:(c+1)*DC], convw[c*128:(c+1)*128, :])
                    dma(q, dL, convb_sb[:, c:c+1], convb[c*128:(c+1)*128, :])
                    dma(q, dL, dparam_sb[:, c:c+1], dparam[c*128:(c+1)*128, :])
                    dma(q, dL, wdt_sb[:, c:c+1], wdt[c*128:(c+1)*128, :])
                    dma(q, dL, bdt_sb[:, c:c+1], bdt[c*128:(c+1)*128, :])
                    dma(q, dL, A_sb[:, c*DS:(c+1)*DS], alog[c*128:(c+1)*128, :])
                    dma(q, dL, gb1_sb[:, c:c+1], gb1[c*128:(c+1)*128, :])
                for m in range(32):
                    dma(q, dL, bcols_sb[:, m:m+1], bcols[m*128:(m+1)*128, :])
                dma(q, dL, pbb[:, :], projb[0:1, :].broadcast_to((128, D)))
                dma(q, dL, gls05[:, :], lng[0:1, :].broadcast_to((128, D)))
                dma(q, dL, bls05[:, :], lnb[0:1, :].broadcast_to((128, D)))
                dma(q, dL, b2b[:, :], gb2[0:1, :].broadcast_to((128, C16)))
                dma(q, dL, idsb[:, :], ident[:, :])
                for k in range(8):
                    dma(q, dL, gam_sb[:, k:k+1], gam_m[k*128:(k+1)*128, :])
                    dma(q, dL, bet_sb[:, k:k+1], bet_m[k*128:(k+1)*128, :])
                for k in range(8):
                    dma(q, dL, wscr[:, k*1024:(k+1)*1024], win_t[k*128:(k+1)*128, :])
                dma(q, dL, sscr[0:16, 0:1024], colemb[:, :])
            sy(_)
            ld_done = dL.v

            def _(q, w=ld_done):
                q.wait_ge(dL.sem, w)
                last = None
                for m in range(8):
                    for k in range(8):
                        last = q.matmul(out=ps[6][0:128, m:m+1],
                                        lhsT=wscr[:, k*1024 + m*128:k*1024 + (m+1)*128],
                                        rhs=bet_sb[:, k:k+1],
                                        start=(k == 0), stop=(k == 7))
                eT.inc(last, 1)
            te(_)
            cbeta_done = eT.v

            def _(q, w=ld_done):
                q.wait_ge(dL.sem, w)
                eV.inc(q.tensor_tensor(out=sscr[0:16, 0:1024], in0=sscr[0:16, 0:1024],
                                       in1=sscr[0:16, 0:1024], op=ALU.mult), 1)
            ve(_)
            def _(q):
                eV.wait(q)
                q.tensor_reduce(out=colrb[0:16, 0:1], in_=sscr[0:16, 0:1024], op=ALU.add,
                                axis=AX.X)
                q.drain()
                q.tensor_scalar_max(out=colrb[0:16, 0:1], in0=colrb[0:16, 0:1],
                                    scalar1=1e-24)
                eV.inc(q.drain(), 1)
            ve(_)
            def _(q):
                eV.wait(q)
                q.activation(out=colrb[0:16, 1:2], in_=colrb[0:16, 0:1], func=AF.Ln)
                q.drain()
                q.activation(out=colrb[0:16, 0:1], in_=colrb[0:16, 1:2], func=AF.Exp, scale=-0.5)
                q.drain()
                eS.inc(q.activation(out=A_sb[:, :], in_=A_sb[:, :], func=AF.Exp), 1)
            sc(_)
            def _(q):
                eS.wait(q)
                eV.inc(q.tensor_scalar_mul(out=A_sb[:, :], in0=A_sb[:, :], scalar1=-1.0), 1)
            ve(_)
            def _(q):
                eS.wait(q)
                dma(q, dS, colr_dram[:, :], colrb[0:16, 0:1])
            sc(_)
            def _(q):
                dS.wait(q)
                dma(q, dL, colrb[:, :],
                    colr_dram.rearrange("a b -> b a").broadcast_to((128, C16)))
            sy(_)

            def _(q, w=cbeta_done):
                q.wait_ge(eT.sem, w)
                eS.inc(q.copy(out=cb_sb[:, 0:8], in_=ps[6][0:128, 0:8]), 1)
            sc(_)
            def _(q, w=cbeta_done):
                q.wait_ge(eT.sem, w)
                for k in range(8):
                    q.tensor_scalar_mul(out=wscr[:, k*1024:(k+1)*1024],
                                        in0=wscr[:, k*1024:(k+1)*1024],
                                        scalar1=gam_sb[:, k:k+1])
                eV.inc(q.tensor_copy(out=wbf[:, :], in_=wscr[:, :]), 1)
            ve(_)
            win_cast = eV.v
            def _(q, w=win_cast):
                q.wait_ge(eV.sem, w)
                dma(q, dS, winb_dram[:, :], wbf[:, :])
            sc(_)
            winb_st = dS.v

            def _(q, w=win_cast):
                q.wait_ge(eV.sem, w)
                for c in range(4):
                    dma(q, dL, wscr[:, c*1024:(c+1)*1024], wout_t[c*128:(c+1)*128, :])
                for c in range(4):
                    dma(q, dL, sscr[:, c*33:(c+1)*33], wx_t[c*128:(c+1)*128, :])
            sy(_)
            w2 = dL.v
            def _(q, w=w2, ws=winb_st):
                q.wait_ge(dL.sem, w)
                q.wait_ge(dS.sem, ws)
                q.tensor_copy(out=wbf[:, 0:4*D], in_=wscr[:, 0:4*D])
                q.tensor_copy(out=wbf[:, 4*D:4*D + 4*33], in_=sscr[:, 0:4*33])
                q.tensor_scalar_mul(out=gls05[:, :], in0=gls05[:, :], scalar1=0.5)
                eV.inc(q.tensor_scalar_mul(out=bls05[:, :], in0=bls05[:, :], scalar1=0.5), 1)
            ve(_)
            w3v = eV.v
            def _(q, w=w3v):
                q.wait_ge(eV.sem, w)
                dma(q, dS, woutb_dram[:, :], wbf[:, 0:4*D + 4*33])
            sc(_)
            def _(q, w=w3v):
                q.wait_ge(eV.sem, w)
                for k in range(8):
                    dma(q, dL, wscr[:, k*512:(k+1)*512], gw1[k*128:(k+1)*128, :])
                for k in range(4):
                    dma(q, dL, sscr[:, k*C16:(k+1)*C16], gw2[k*128:(k+1)*128, :])
                for k in range(8):
                    dma(q, dL, sscr[:, 128 + k*C16:128 + (k+1)*C16],
                        colembt[k*128:(k+1)*128, :])
            sy(_)
            w4 = dL.v
            def _(q, w=w4):
                q.wait_ge(dL.sem, w)
                q.tensor_copy(out=g1b[:, :], in_=wscr[:, 0:8*DH])
                q.tensor_copy(out=g2b[:, :], in_=sscr[:, 0:4*C16])
                q.tensor_copy(out=cetb[:, :], in_=sscr[:, 128:128 + 8*C16])
                eV.inc(q.memset(state[:, :], 0.0), 1)
            ve(_)
            def _(q):
                dG.wait(q)
            sy(_)
            barrier()

        ag_done_list = []
        rs_done_list = []
        def ffn_phase(phase):
            with ExitStack() as ph:
                Ep = ph.enter_context
                sfx = f"_p{phase}"
                x0 = Ep(nc.sbuf_tensor("x0" + sfx, [128, 2 * NPT * D], F32))
                xex = Ep(nc.sbuf_tensor("xex" + sfx, [128, 2 * NPT * D], F32))
                yb = Ep(nc.sbuf_tensor("yb" + sfx, [128, NPT * D], F32))
                U1 = Ep(nc.sbuf_tensor("U1" + sfx, [128, 8 * RPC], BF16))
                u1b = Ep(nc.sbuf_tensor("u1b" + sfx, [128, 2 * NPT * D], BF16))
                h1b = Ep(nc.sbuf_tensor("h1b" + sfx, [128, 4 * RPC], BF16))
                colsw = Ep(nc.sbuf_tensor("colsw" + sfx, [128, 32 * RPC], BF16))
                colsg = Ep(nc.sbuf_tensor("colsg" + sfx, [128, 2 * RPC], BF16))
                wcpan = Ep(nc.sbuf_tensor("wcpan" + sfx, [128, 4 * 1024], BF16))
                pwwin = Ep(nc.sbuf_tensor("pwwin" + sfx, [128, 4 * D], BF16))
                wrep = Ep(nc.sbuf_tensor("wrep" + sfx, [128, 4 * RPC], BF16))
                wtok = Ep(nc.sbuf_tensor("wtok" + sfx, [128, NPT * C16], F32))
                lgt = Ep(nc.sbuf_tensor("lgt" + sfx, [128, NPT * C16], F32))
                sgt = Ep(nc.sbuf_tensor("sgt" + sfx, [128, NPT * C16], F32))
                wfm = Ep(nc.sbuf_tensor("wfm" + sfx, [C16, RPC], BF16))
                st1 = Ep(nc.sbuf_tensor("st1" + sfx, [128, 2 * NPT * 12], F32))
                rstg = Ep(nc.sbuf_tensor("rstg" + sfx, [128, NPT * D], BF16))

                def stc(P, pt, i):
                    o = (P * NPT + pt) * 12 + i
                    return st1[:, o:o+1]

                def xs(P, pt):
                    return x0[:, (P * NPT + pt)*D:(P * NPT + pt + 1)*D]

                def xe(P, pt):
                    return xex[:, (P * NPT + pt)*D:(P * NPT + pt + 1)*D]

                def us(P, pt):
                    return u1b[:, (P * NPT + pt)*D:(P * NPT + pt + 1)*D]

                st = dict(prev_xn2_store=None, prev_xn2_par=0,
                          prev_proj_mm=eT.v, prev_proj_evac=eS.v,
                          prev_colsw_last=eV.v, prev_cols_mm_last=eT.v,
                          prev_stores=dS.v, u1ld=0, u1st=0)

                def emit_head_load(j):
                    P = j % 2
                    r0 = j * RPC
                    def _(q, r0=r0, d=st["prev_stores"], j=j, P=P):
                        q.wait_ge(dS.sem, d)
                        if phase == 2:
                            q.wait_ge(eG.sem, rs_done_list[j])
                        if phase == 1:
                            dma(q, sx0,
                                x0[:, P*NPT*D:(P+1)*NPT*D].rearrange(
                                    "p (a d) -> p a d", a=NPT),
                                xrows[r0:r0+RPC, :].rearrange("(a p) d -> p a d", p=128))
                        else:
                            dma(q, sx0,
                                xex[:, P*NPT*D:(P+1)*NPT*D].rearrange(
                                    "p (a d) -> p a d", a=NPT),
                                x1_dram[r0:r0+RPC, :].rearrange("(a p) d -> p a d", p=128))
                            dma(q, sx0,
                                rstg[:, :].rearrange("p (a d) -> p a d", a=NPT),
                                rsout[r0:r0+RPC, :].rearrange("(a p) d -> p a d", p=128))
                    gp(_)
                    st["xload"] = sx0.v

                def emit_head_rest(j):
                    P = j % 2
                    xload = st["xload"]
                    if phase == 2:
                        def _(q, j=j, P=P, w=xload):
                            q.wait_ge(sx0.sem, w)
                            for pt in range(NPT):
                                scol = (j * NPT + pt) * 2
                                q.tensor_tensor(out=stc(P, pt, 11),
                                                in0=stats3[:, scol:scol+1],
                                                in1=stats3[:, scol+1:scol+2],
                                                op=ALU.mult)
                            q.drain()
                            for pt in range(NPT):
                                q.tensor_scalar_mul(out=stc(P, pt, 11),
                                                    in0=stc(P, pt, 11), scalar1=-1.0)
                            q.drain()
                            for pt in range(NPT):
                                scol = (j * NPT + pt) * 2
                                r3 = stats3[:, scol+1:scol+2]
                                q.scalar_tensor_tensor(out=xs(P, pt), in0=xe(P, pt),
                                                       scalar=r3,
                                                       in1=rstg[:, pt*D:(pt+1)*D],
                                                       op0=ALU.mult, op1=ALU.add)
                            for pt in range(NPT):
                                q.tensor_scalar(out=xs(P, pt), in0=xs(P, pt),
                                                scalar1=stc(P, pt, 11), scalar2=None,
                                                op0=ALU.add)
                            for pt in range(NPT):
                                q.tensor_tensor(out=xs(P, pt), in0=xs(P, pt),
                                                in1=xe(P, pt), op=ALU.add)
                            eV.inc(q.drain(), 1)
                        ve(_)

                    def _(q, w=xload, P=P):
                        q.wait_ge(sx0.sem, w)
                        eV.wait(q)
                        for pt in range(NPT):
                            q.activation(out=xe(P, pt), in_=xs(P, pt),
                                         func=AF.Square, accum_out=stc(P, pt, 1))
                        for pt in range(NPT):
                            ii = q.activation(out=xe(P, pt), in_=xs(P, pt),
                                              func=AF.Identity, accum_out=stc(P, pt, 0))
                        eS.inc(ii, 1)
                    sc(_)
                    sq1 = eS.v
                    def _(q, w=xload, w2=sq1, P=P):
                        q.wait_ge(sx0.sem, w)
                        q.wait_ge(eS.sem, w2)
                        eV.wait(q)
                        for pt in range(NPT):
                            q.tensor_scalar_mul(out=stc(P, pt, 2), in0=stc(P, pt, 0),
                                                scalar1=1.0 / D)
                        for pt in range(NPT):
                            q.tensor_scalar_mul(out=stc(P, pt, 3), in0=stc(P, pt, 1),
                                                scalar1=1.0 / D)
                        q.drain()
                        for pt in range(NPT):
                            q.tensor_tensor(out=stc(P, pt, 11), in0=stc(P, pt, 2),
                                            in1=stc(P, pt, 2), op=ALU.mult)
                        q.drain()
                        for pt in range(NPT):
                            q.tensor_tensor(out=stc(P, pt, 3), in0=stc(P, pt, 3),
                                            in1=stc(P, pt, 11), op=ALU.subtract)
                        eV.inc(q.drain(), 1)
                    ve(_)
                    def _(q, P=P):
                        eV.wait(q)
                        for pt in range(NPT):
                            q.activation(out=stc(P, pt, 11), in_=stc(P, pt, 3),
                                         func=AF.Ln, bias=EPS)
                        q.drain()
                        for pt in range(NPT):
                            q.activation(out=stc(P, pt, 4), in_=stc(P, pt, 11),
                                         func=AF.Exp, scale=-0.5)
                        eS.inc(q.drain(), 1)
                    sc(_)
                    def _(q, P=P):
                        eS.wait(q)
                        for pt in range(NPT):
                            q.tensor_tensor(out=stc(P, pt, 5), in0=stc(P, pt, 2),
                                            in1=stc(P, pt, 4), op=ALU.mult)
                        for pt in range(NPT):
                            q.tensor_tensor(out=stc(P, pt, 6), in0=stc(P, pt, 4),
                                            in1=stc(P, pt, 4), op=ALU.mult)
                        q.drain()
                        for pt in range(NPT):
                            q.tensor_scalar_mul(out=stc(P, pt, 5), in0=stc(P, pt, 5),
                                                scalar1=-1.0)
                        for pt in range(NPT):
                            q.tensor_tensor(out=stc(P, pt, 6), in0=stc(P, pt, 6),
                                            in1=stc(P, pt, 3), op=ALU.mult)
                        q.drain()
                        for pt in range(NPT):
                            q.tensor_scalar_mul(out=stc(P, pt, 6), in0=stc(P, pt, 6),
                                                scalar1=float(D))
                        q.drain()
                        for pt in range(NPT):
                            q.tensor_scalar_max(out=stc(P, pt, 6), in0=stc(P, pt, 6),
                                                scalar1=1e-24)
                        eV.inc(q.drain(), 1)
                    ve(_)
                    def _(q, P=P, xw=st["prev_xn2_store"], xp=st["prev_xn2_par"]):
                        eV.wait(q)
                        if phase == 1 and xw is not None:
                            q.wait_ge(sxs[xp].sem, xw)
                        for pt in range(NPT):
                            q.activation(out=stc(P, pt, 11), in_=stc(P, pt, 6),
                                         func=AF.Ln)
                        q.drain()
                        for pt in range(NPT):
                            q.activation(out=stc(P, pt, 6), in_=stc(P, pt, 11),
                                         func=AF.Exp, scale=-0.5)
                        for pt in range(NPT):
                            q.activation(out=us(P, pt), in_=xs(P, pt),
                                         func=AF.Identity, scale=stc(P, pt, 4),
                                         bias=stc(P, pt, 5))
                        eS.inc(q.drain(), 1)
                    sc(_)

                    def _(q, P=P):
                        eS.wait(q)
                        dma(q, dS, u1b_dram.rearrange("(a p) d -> p a d", p=128),
                            u1b[:, P*NPT*D:(P+1)*NPT*D].rearrange("p (a d) -> p a d",
                                                                  a=NPT))
                    sc(_)
                    st["u1st"] = dS.v
                    def _(q, w=st["u1st"], pt_=st["prev_cols_mm_last"]):
                        q.wait_ge(dS.sem, w)
                        q.wait_ge(eT.sem, pt_)
                        for dt8 in range(8):
                            dma(q, su1, U1[:, dt8*RPC:(dt8+1)*RPC],
                                u1b_dram[:, dt8*128:(dt8+1)*128], transpose=True)
                    sy(_)
                    st["u1ld"] = su1.v

                def emit_gates(j):
                    P = j % 2
                    def _(q, w=st["u1ld"], pe=st["prev_proj_evac"]):
                        q.wait_ge(su1.sem, w)
                        q.wait_ge(eS.sem, pe)
                        last = None
                        for m4 in range(4):
                            for kk in range(8):
                                last = q.matmul(out=ps[m4][0:128, :],
                                                lhsT=g1b[:, kk*DH + m4*128: kk*DH + (m4+1)*128],
                                                rhs=U1[:, kk*RPC:(kk+1)*RPC],
                                                start=(kk == 0), stop=(kk == 7))
                        eT.inc(last, 1)
                    te(_)
                    g1t = eT.v
                    def _(q, w=g1t):
                        q.wait_ge(eT.sem, w)
                        for m4 in range(4):
                            ii = q.activation(out=h1b[:, m4*RPC:(m4+1)*RPC],
                                              in_=ps[m4][0:128, :],
                                              func=AF.Gelu, bias=gb1_sb[:, m4:m4+1])
                        eS.inc(ii, 1)
                    sc(_)

                    def _(q):
                        eS.wait(q)
                        last = None
                        for tb in range(NPT):
                            for k4 in range(4):
                                q.matmul(out=ps[tb][0:128, 0:C16],
                                         lhsT=h1b[:, k4*RPC + tb*128: k4*RPC + (tb+1)*128],
                                         rhs=g2b[:, k4*C16:(k4+1)*C16],
                                         start=(k4 == 0), stop=(k4 == 3))
                            for k8 in range(8):
                                last = q.matmul(out=ps[4+tb][0:128, 0:C16],
                                                lhsT=U1[:, k8*RPC + tb*128: k8*RPC + (tb+1)*128],
                                                rhs=cetb[:, k8*C16:(k8+1)*C16],
                                                start=(k8 == 0), stop=(k8 == 7))
                        eT.inc(last, 1)
                    te(_)
                    g2t = eT.v
                    def _(q, w=g2t, P=P):
                        q.wait_ge(eT.sem, w)
                        for tb in range(NPT):
                            q.tensor_tensor(out=sgt[:, tb*C16:(tb+1)*C16],
                                            in0=ps[tb][0:128, 0:C16], in1=b2b[:, :],
                                            op=ALU.add)
                        for tb in range(NPT):
                            q.tensor_tensor(out=lgt[:, tb*C16:(tb+1)*C16],
                                            in0=ps[4+tb][0:128, 0:C16], in1=colrb[:, :],
                                            op=ALU.mult)
                        q.drain()
                        for tb in range(NPT):
                            q.tensor_scalar_mul(out=lgt[:, tb*C16:(tb+1)*C16],
                                                in0=lgt[:, tb*C16:(tb+1)*C16],
                                                scalar1=stc(P, tb, 6))
                        eV.inc(q.drain(), 1)
                    ve(_)
                    def _(q):
                        eV.wait(q)
                        for tb in range(NPT):
                            q.activation(out=sgt[:, tb*C16:(tb+1)*C16],
                                         in_=sgt[:, tb*C16:(tb+1)*C16], func=AF.Sigmoid)
                        eS.inc(q.drain(), 1)
                    sc(_)
                    def _(q, P=P):
                        eS.wait(q)
                        for tb in range(NPT):
                            sl = lgt[:, tb*C16:(tb+1)*C16]
                            q.tensor_tensor(out=sl, in0=sl, in1=sgt[:, tb*C16:(tb+1)*C16],
                                            op=ALU.add)
                        q.drain()
                        for tb in range(NPT):
                            q.tensor_reduce(out=stc(P, tb, 11),
                                            in_=lgt[:, tb*C16:(tb+1)*C16],
                                            op=ALU.max, axis=AX.X)
                        q.drain()
                        for tb in range(NPT):
                            q.tensor_single_scalar(out=wtok[:, tb*C16:(tb+1)*C16],
                                                   in_=lgt[:, tb*C16:(tb+1)*C16],
                                                   scalar=stc(P, tb, 11), op=ALU.is_equal)
                        q.drain()
                        for tb in range(NPT):
                            q.tensor_scalar_mul(out=stc(P, tb, 11), in0=stc(P, tb, 11),
                                                scalar1=-1.0)
                        eV.inc(q.drain(), 1)
                    ve(_)
                    def _(q, P=P):
                        eV.wait(q)
                        for tb in range(NPT):
                            q.activation(out=sgt[:, tb*C16:(tb+1)*C16],
                                         in_=lgt[:, tb*C16:(tb+1)*C16],
                                         func=AF.Exp, bias=stc(P, tb, 11))
                        eS.inc(q.drain(), 1)
                    sc(_)
                    def _(q, P=P):
                        eS.wait(q)
                        for tb in range(NPT):
                            q.tensor_reduce(out=stc(P, tb, 11),
                                            in_=sgt[:, tb*C16:(tb+1)*C16],
                                            op=ALU.add, axis=AX.X)
                        q.drain()
                        for tb in range(NPT):
                            q.reciprocal(out=stc(P, tb, 11), in_=stc(P, tb, 11))
                        for tb in range(NPT):
                            q.tensor_tensor(out=wtok[:, tb*C16:(tb+1)*C16],
                                            in0=wtok[:, tb*C16:(tb+1)*C16],
                                            in1=sgt[:, tb*C16:(tb+1)*C16], op=ALU.mult)
                        q.drain()
                        for tb in range(NPT):
                            q.tensor_scalar_mul(out=wtok[:, tb*C16:(tb+1)*C16],
                                                in0=wtok[:, tb*C16:(tb+1)*C16],
                                                scalar1=stc(P, tb, 11))
                        eV.inc(q.drain(), 1)
                    ve(_)

                    def _(q):
                        eV.wait(q)
                        last = None
                        for tb in range(NPT):
                            last = q.transpose(out=ps[6][0:C16, tb*128:(tb+1)*128],
                                               in_=wtok[:, tb*C16:(tb+1)*C16],
                                               identity=idsb[:, :])
                        eT.inc(last, 1)
                    te(_)
                    wtr = eT.v
                    def _(q, w=wtr):
                        q.wait_ge(eT.sem, w)
                        eS.inc(q.copy(out=wfm[:, :], in_=ps[6][0:C16, 0:RPC]), 1)
                    sc(_)
                    def _(q):
                        eS.wait(q)
                        dma(q, dS, wfm_dram[:, :], wfm[:, :])
                    sc(_)
                    st["wfm_st"] = dS.v

                def emit_cols(j):
                    eS0, eV0 = eS.v, eV.v
                    st["eS0"], st["eV0"] = eS0, eV0
                    pan_ld, pan_used, wrep_ld = [], [], []

                    def _(q, w=st["wfm_st"], eV0=eV0, pv=st["prev_colsw_last"]):
                        q.wait_ge(dS.sem, w)
                        for c in range(C16):
                            if c >= 4:
                                q.wait_ge(eV.sem, eV0 + 2*(c-4) + 2)
                            else:
                                q.wait_ge(eV.sem, pv)
                            dma(q, swr[c % 4], wrep[:, (c % 4)*RPC:((c % 4)+1)*RPC],
                                wfm_dram[c:c+1, :].broadcast_to((128, RPC)))
                            wrep_ld.append(swr[c % 4].v)
                    gp(_)
                    def _(q, eT0=eT.v, pc=st["prev_cols_mm_last"]):
                        for m in range(32):
                            if m >= 4:
                                q.wait_ge(eT.sem, eT0 + m - 3)
                            else:
                                q.wait_ge(eT.sem, pc)
                            dma(q, swc[m % 4], wcpan[:, (m % 4)*1024:((m % 4)+1)*1024],
                                wcols_bf2[m*128:(m+1)*128, :])
                            pan_ld.append(swc[m % 4].v)
                    sy(_)
                    del _

                    def _(q):
                        for m in range(32):
                            q.wait_ge(swc[m % 4].sem, pan_ld[m])
                            if m >= 2:
                                q.wait_ge(eS.sem, eS0 + m - 1)
                            last = None
                            for kk in range(8):
                                last = q.matmul(
                                    out=ps[m % 2][0:128, :],
                                    lhsT=wcpan[:, (m % 4)*1024 + kk*128:
                                               (m % 4)*1024 + (kk+1)*128],
                                    rhs=U1[:, kk*RPC:(kk+1)*RPC],
                                    start=(kk == 0), stop=(kk == 7))
                            eT.inc(last, 1)
                            pan_used.append(eT.v)
                    te(_)
                    def _(q):
                        for m in range(32):
                            q.wait_ge(eT.sem, pan_used[m])
                            if m >= 2:
                                q.wait_ge(eV.sem, eV0 + m - 1)
                            eS.inc(q.activation(out=colsg[:, (m % 2)*RPC:((m % 2)+1)*RPC],
                                                in_=ps[m % 2][0:128, :], func=AF.Gelu,
                                                bias=bcols_sb[:, m:m+1]), 1)
                    sc(_)
                    def _(q):
                        for m in range(32):
                            c = m // 2
                            q.wait_ge(eS.sem, eS0 + m + 1)
                            q.wait_ge(swr[c % 4].sem, wrep_ld[c])
                            eV.inc(q.tensor_tensor(out=colsw[:, m*RPC:(m+1)*RPC],
                                                   in0=colsg[:, (m % 2)*RPC:((m % 2)+1)*RPC],
                                                   in1=wrep[:, (c % 4)*RPC:((c % 4)+1)*RPC],
                                                   op=ALU.mult), 1)
                    ve(_)
                    st["colsw_last"] = eV.v
                    st["prev_cols_mm_last"] = eT.v

                def emit_proj(j):
                    pw_ld = []
                    eT1 = eT.v
                    def _(q, ppm=st["prev_proj_mm"]):
                        for kk in range(32):
                            if kk >= 4:
                                q.wait_ge(eT.sem, eT1 + kk - 3)
                            else:
                                q.wait_ge(eT.sem, ppm)
                            dma(q, spw[kk % 4], pwwin[:, (kk % 4)*D:((kk % 4)+1)*D],
                                projw_bf[kk*128:(kk+1)*128, :])
                            pw_ld.append(spw[kk % 4].v)
                    sy(_)
                    def _(q, eS0=st["eS0"], eV0=st["eV0"]):
                        for kk in range(32):
                            q.wait_ge(spw[kk % 4].sem, pw_ld[kk])
                            q.wait_ge(eV.sem, eV0 + kk + 1)
                            if kk == 0:
                                q.wait_ge(eS.sem, eS0 + 32)
                            last = None
                            for tb in range(NPT):
                                for fh in range(2):
                                    last = q.matmul(
                                        out=ps[tb*2 + fh][0:128, :],
                                        lhsT=colsw[:, kk*RPC + tb*128: kk*RPC + (tb+1)*128],
                                        rhs=pwwin[:, (kk % 4)*D + fh*512:
                                                  (kk % 4)*D + (fh+1)*512],
                                        start=(kk == 0), stop=(kk == 31))
                            eT.inc(last, 1)
                    te(_)
                    st["prev_proj_mm"] = eT.v

                def emit_evac(j):
                    def _(q, tv=st["prev_proj_mm"]):
                        q.wait_ge(eT.sem, tv)
                        for tb in range(NPT):
                            for fh in range(2):
                                ii = q.copy(out=yb[:, tb*D + fh*512: tb*D + (fh+1)*512],
                                            in_=ps[tb*2 + fh][0:128, :])
                        eS.inc(ii, 1)
                    sc(_)
                    st["prev_proj_evac"] = eS.v

                def emit_tail(j):
                    P = j % 2
                    r0 = j * RPC
                    k = j
                    def _(q, P=P):
                        eS.wait(q)
                        for pt in range(NPT):
                            ysl = yb[:, pt*D:(pt+1)*D]
                            q.tensor_tensor(out=ysl, in0=ysl, in1=pbb[:, :], op=ALU.add)
                            q.scalar_tensor_tensor(out=ysl, in0=xs(P, pt),
                                                   scalar=stc(P, pt, 4), in1=ysl,
                                                   op0=ALU.mult, op1=ALU.add)
                            q.tensor_scalar(out=ysl, in0=ysl, scalar1=stc(P, pt, 5),
                                            scalar2=None, op0=ALU.add)
                        eV.inc(q.drain(), 1)
                    ve(_)
                    def _(q, P=P):
                        eV.wait(q)
                        for pt in range(NPT):
                            q.activation(out=xe(P, pt), in_=yb[:, pt*D:(pt+1)*D],
                                         func=AF.Square, accum_out=stc(P, pt, 8))
                        for pt in range(NPT):
                            ii = q.activation(out=xe(P, pt), in_=yb[:, pt*D:(pt+1)*D],
                                              func=AF.Identity, accum_out=stc(P, pt, 7))
                        eS.inc(ii, 1)
                    sc(_)
                    def _(q, P=P):
                        eS.wait(q)
                        for pt in range(NPT):
                            q.tensor_scalar_mul(out=stc(P, pt, 7), in0=stc(P, pt, 7),
                                                scalar1=1.0 / D)
                        for pt in range(NPT):
                            q.tensor_scalar_mul(out=stc(P, pt, 8), in0=stc(P, pt, 8),
                                                scalar1=1.0 / D)
                        q.drain()
                        for pt in range(NPT):
                            q.tensor_tensor(out=stc(P, pt, 11), in0=stc(P, pt, 7),
                                            in1=stc(P, pt, 7), op=ALU.mult)
                        q.drain()
                        for pt in range(NPT):
                            q.tensor_tensor(out=stc(P, pt, 8), in0=stc(P, pt, 8),
                                            in1=stc(P, pt, 11), op=ALU.subtract)
                        eV.inc(q.drain(), 1)
                    ve(_)
                    def _(q, P=P):
                        eV.wait(q)
                        for pt in range(NPT):
                            q.activation(out=stc(P, pt, 11), in_=stc(P, pt, 8),
                                         func=AF.Ln, bias=EPS)
                        q.drain()
                        for pt in range(NPT):
                            q.activation(out=stc(P, pt, 9), in_=stc(P, pt, 11),
                                         func=AF.Exp, scale=-0.5)
                        eS.inc(q.drain(), 1)
                    sc(_)
                    def _(q, P=P):
                        eS.wait(q)
                        for pt in range(NPT):
                            q.tensor_tensor(out=stc(P, pt, 10), in0=stc(P, pt, 7),
                                            in1=stc(P, pt, 9), op=ALU.mult)
                        q.drain()
                        for pt in range(NPT):
                            q.tensor_scalar_mul(out=stc(P, pt, 10), in0=stc(P, pt, 10),
                                                scalar1=-1.0)
                        eV.inc(q.drain(), 1)
                    ve(_)
                    def _(q, P=P):
                        eV.wait(q)
                        for pt in range(NPT):
                            ii = q.activation(out=xe(P, pt), in_=yb[:, pt*D:(pt+1)*D],
                                              func=AF.Identity, scale=stc(P, pt, 9),
                                              bias=stc(P, pt, 10))
                        eS.inc(ii, 1)
                    sc(_)
                    def _(q, P=P):
                        eS.wait(q)
                        for pt in range(NPT):
                            tl = xe(P, pt)
                            q.tensor_tensor(out=tl, in0=tl, in1=gls05[:, :], op=ALU.mult)
                            q.tensor_tensor(out=tl, in0=tl, in1=bls05[:, :], op=ALU.add)
                            ii = q.tensor_tensor(out=xs(P, pt), in0=xs(P, pt), in1=tl,
                                                 op=ALU.add)
                        eV.inc(ii, 1)
                    ve(_)

                    st["prev_colsw_last"] = st["colsw_last"]

                    if phase == 1:
                        def _(q, P=P):
                            eV.wait(q)
                            for pt in range(NPT):
                                q.activation(out=xe(P, pt), in_=xs(P, pt),
                                             func=AF.Square, accum_out=stc(P, pt, 1))
                            for pt in range(NPT):
                                ii = q.activation(out=xe(P, pt), in_=xs(P, pt),
                                                  func=AF.Identity,
                                                  accum_out=stc(P, pt, 0))
                            eS.inc(ii, 1)
                        sc(_)
                        def _(q, k=k, P=P):
                            eS.wait(q)
                            for pt in range(NPT):
                                scol = (k * NPT + pt) * 2
                                q.tensor_scalar_mul(out=stats3[:, scol:scol+1],
                                                    in0=stc(P, pt, 0), scalar1=1.0 / D)
                            for pt in range(NPT):
                                q.tensor_scalar_mul(out=stc(P, pt, 3), in0=stc(P, pt, 1),
                                                    scalar1=1.0 / D)
                            q.drain()
                            for pt in range(NPT):
                                scol = (k * NPT + pt) * 2
                                q.tensor_tensor(out=stc(P, pt, 11),
                                                in0=stats3[:, scol:scol+1],
                                                in1=stats3[:, scol:scol+1], op=ALU.mult)
                            q.drain()
                            for pt in range(NPT):
                                q.tensor_tensor(out=stc(P, pt, 3), in0=stc(P, pt, 3),
                                                in1=stc(P, pt, 11), op=ALU.subtract)
                            eV.inc(q.drain(), 1)
                        ve(_)
                        def _(q, k=k, P=P):
                            eV.wait(q)
                            for pt in range(NPT):
                                q.activation(out=stc(P, pt, 11), in_=stc(P, pt, 3),
                                             func=AF.Ln, bias=EPS)
                            q.drain()
                            for pt in range(NPT):
                                scol = (k * NPT + pt) * 2
                                q.activation(out=stats3[:, scol+1:scol+2],
                                             in_=stc(P, pt, 11), func=AF.Exp, scale=-0.5)
                            eS.inc(q.drain(), 1)
                        sc(_)
                        def _(q, k=k, P=P):
                            eS.wait(q)
                            for pt in range(NPT):
                                scol = (k * NPT + pt) * 2
                                q.tensor_tensor(out=stc(P, pt, 11),
                                                in0=stats3[:, scol+1:scol+2],
                                                in1=stats3[:, scol+1:scol+2], op=ALU.mult)
                            q.drain()
                            for pt in range(NPT):
                                q.tensor_tensor(out=stc(P, pt, 11), in0=stc(P, pt, 11),
                                                in1=stc(P, pt, 3), op=ALU.mult)
                            eV.inc(q.drain(), 1)
                        ve(_)
                        def _(q, P=P):
                            eV.wait(q)
                            for pt in range(NPT):
                                q.activation(out=stc(P, pt, 11), in_=stc(P, pt, 11),
                                             func=AF.Ln, bias=EPS)
                            q.drain()
                            for pt in range(NPT):
                                q.activation(out=stc(P, pt, 11), in_=stc(P, pt, 11),
                                             func=AF.Exp, scale=-0.5)
                            eS.inc(q.drain(), 1)
                        sc(_)
                        def _(q, k=k, P=P):
                            eS.wait(q)
                            for pt in range(NPT):
                                scol = (k * NPT + pt) * 2
                                q.tensor_tensor(out=stc(P, pt, 11), in0=stc(P, pt, 11),
                                                in1=stats3[:, scol+1:scol+2], op=ALU.mult)
                            q.drain()
                            for pt in range(NPT):
                                scol = (k * NPT + pt) * 2
                                q.tensor_tensor(out=stc(P, pt, 10),
                                                in0=stats3[:, scol:scol+1],
                                                in1=stc(P, pt, 11), op=ALU.mult)
                            q.drain()
                            for pt in range(NPT):
                                q.tensor_scalar_mul(out=stc(P, pt, 10), in0=stc(P, pt, 10),
                                                    scalar1=-1.0)
                            eV.inc(q.drain(), 1)
                        ve(_)
                        def _(q, P=P):
                            eV.wait(q)
                            for pt in range(NPT):
                                ii = q.activation(out=us(P, pt), in_=xs(P, pt),
                                                  func=AF.Identity, scale=stc(P, pt, 11),
                                                  bias=stc(P, pt, 10))
                            eS.inc(ii, 1)
                        sc(_)
                        def _(q, r0=r0, k=k, P=P):
                            eS.wait(q)
                            dma(q, sxs[k % 2],
                                xn2_slice[r0:r0+RPC, :].rearrange("(a p) d -> p a d", p=128),
                                u1b[:, P*NPT*D:(P+1)*NPT*D].rearrange("p (a d) -> p a d",
                                                                      a=NPT))
                            dma(q, dS, x1_dram[r0:r0+RPC, :].rearrange("(a p) d -> p a d",
                                                                       p=128),
                                x0[:, P*NPT*D:(P+1)*NPT*D].rearrange("p (a d) -> p a d",
                                                                     a=NPT))
                        sc(_)
                        st["prev_stores"] = dS.v
                        st["prev_xn2_store"] = sxs[k % 2].v
                        st["prev_xn2_par"] = k % 2
                        def _(q, k=k, w=st["prev_xn2_store"]):
                            q.wait_ge(sxs[k % 2].sem, w)
                            eG.inc(q.collective_compute(
                                "AllGather", ALU.bypass, replica_groups=GROUPS,
                                ins=[xn2_slice[k*RPC:(k+1)*RPC, :].opt()],
                                outs=[cag2[k, :, :].opt()]), 1)
                            ag_done_list.append(eG.v)
                        gp(_)
                    else:
                        def _(q, r0=r0, P=P):
                            eV.wait(q)
                            dma(q, dS,
                                outrows[r0:r0+RPC, :].rearrange("(a p) d -> p a d", p=128),
                                x0[:, P*NPT*D:(P+1)*NPT*D].rearrange("p (a d) -> p a d",
                                                                     a=NPT))
                        sc(_)
                        st["prev_stores"] = dS.v

                emit_head_load(0)
                emit_head_rest(0)
                emit_gates(0)
                emit_cols(0)
                for k in range(NCHK):
                    if k + 1 < NCHK:
                        emit_head_load(k + 1)
                    emit_proj(k)
                    if k + 1 < NCHK:
                        emit_head_rest(k + 1)
                    emit_evac(k)
                    if k + 1 < NCHK:
                        emit_gates(k + 1)
                        emit_cols(k + 1)
                    emit_tail(k)
                barrier()

        ffn_phase(1)

        with ExitStack() as ph:
            Em = ph.enter_context
            Xn2 = Em(nc.sbuf_tensor("Xn2", [128, 8 * TS], BF16))
            xbwin = Em(nc.sbuf_tensor("xbwin", [128, TS + 3], F32))
            halo = Em(nc.sbuf_tensor("halo", [128, 12], F32))
            zsb = Em(nc.sbuf_tensor("zsb", [128, 8 * TS], BF16))
            xcsb = Em(nc.sbuf_tensor("xcsb", [128, 4 * TS], F32))
            xcbw = Em(nc.sbuf_tensor("xcbw", [128, TS], BF16))
            dt4 = Em(nc.sbuf_tensor("dt4", [128, 4 * TS], F32))
            dtx4 = Em(nc.sbuf_tensor("dtx4", [128, 4 * TS], BF16))
            dtrawb = Em(nc.sbuf_tensor("dtrawb", [128, TS], F32))
            Ebuf = Em(nc.sbuf_tensor("Ebuf", [128, 2 * TS], F32))
            ubuf = Em(nc.sbuf_tensor("ubuf", [128, 2 * TS], BF16))
            hbuf = Em(nc.sbuf_tensor("hbuf", [128, TS], BF16))
            ytmp = Em(nc.sbuf_tensor("ytmp", [128, TS], BF16))
            tmpb = Em(nc.sbuf_tensor("tmpb", [128, TS], F32))
            acc4 = Em(nc.sbuf_tensor("acc4", [128, 4 * TS], F32))
            Brep = Em(nc.sbuf_tensor("Brep", [128, 2 * TS], BF16))
            Crep = Em(nc.sbuf_tensor("Crep", [128, 2 * TS], BF16))
            ygsb = Em(nc.sbuf_tensor("ygsb", [128, 4 * TS], BF16))
            ssm_sb = Em(nc.sbuf_tensor("ssm_sb", [128, TS], F32))
            evsb = Em(nc.sbuf_tensor("evsb", [128, D], BF16))
            winb = Em(nc.sbuf_tensor("winb", [128, 8 * 1024], BF16))
            woutb = Em(nc.sbuf_tensor("woutb", [128, 4 * D], BF16))
            wxb = Em(nc.sbuf_tensor("wxb", [128, 4 * 33], BF16))
            evsc = Em(nc.sbuf_tensor("evsc", [128, TS], F32))

            def _(q):
                q.memset(halo[:, :], 0.0)
                eV.inc(q.memset(state[:, :], 0.0), 1)
            ve(_)

            def _(q):
                dma(q, dL, winb[:, :], winb_dram[:, :])
                dma(q, dL, woutb[:, :], woutb_dram[:, 0:4*D])
                dma(q, dL, wxb[:, :], woutb_dram[:, 4*D:4*D + 4*33])
            sy(_)
            mwl = dL.v

            st2 = dict(win_done=eT.v, mh_evac=eS.v, wo_mm=eT.v, bc_cast=0)
            prev_scan_eV1 = None

            def emit_mhead(s):
                kch = s // 2
                hh = s % 2
                zb = (s % 2) * 4 * TS

                def _(q, kch=kch, hh=hh, a=st2["win_done"], g=ag_done_list[kch]):
                    q.wait_ge(eG.sem, g)
                    q.wait_ge(eT.sem, a)
                    for rb in range(2):
                        row0 = (2*hh + rb) * RPC
                        for dt8 in range(8):
                            dma(q, sxn, Xn2[:, dt8*TS + rb*512: dt8*TS + (rb+1)*512],
                                cag2[kch, row0:row0+512, dt8*128:(dt8+1)*128], transpose=True)
                sy(_)
                xn_ld = sxn.v

                eT0, eS0, eV0 = eT.v, eS.v, eV.v
                def winpos(m):
                    return m + 1 if m < 4 else 2*m - 2
                def wxpos(c):
                    return 5 + 2*c
                def evacpos(m):
                    return 2*m + 1 if m < 4 else m + 5

                def _(q, w=xn_ld, eT0=eT0, eS0=eS0, eV0=eV0, pw_=st2["mh_evac"], s=s):
                    q.wait_ge(sxn.sem, w)
                    if s == 0:
                        q.wait_ge(dL.sem, mwl)
                    order = [("w", 0), ("w", 1), ("w", 2), ("w", 3),
                             ("x", 0), ("w", 4), ("x", 1), ("w", 5),
                             ("x", 2), ("w", 6), ("x", 3), ("w", 7)]
                    for kind, i in order:
                        if kind == "w":
                            m = i
                            if m == 0:
                                q.wait_ge(eS.sem, pw_)
                            if m >= 1:
                                q.wait_ge(eS.sem, eS0 + evacpos(m-1))
                            last = None
                            for half in range(2):
                                for k8 in range(8):
                                    last = q.matmul(
                                        out=ps[6 + half][0:128, :],
                                        lhsT=winb[:, k8*1024 + m*128: k8*1024 + (m+1)*128],
                                        rhs=Xn2[:, k8*TS + half*512: k8*TS + (half+1)*512],
                                        start=(k8 == 0), stop=(k8 == 7))
                            eT.inc(last, 1)
                        else:
                            c = i
                            q.wait_ge(eV.sem, eV0 + 2*c + 2)
                            last = None
                            for half in range(2):
                                last = q.matmul(out=ps[4 + half][0:33, :],
                                                lhsT=wxb[:, c*33:(c+1)*33],
                                                rhs=xcbw[:, half*512:(half+1)*512],
                                                start=(c == 0), stop=(c == 3))
                            eT.inc(last, 1)
                te(_)
                st2["win_done"] = eT.v
                def _(q, eT0=eT0, eV0=eV0, zb=zb):
                    for m in range(8):
                        q.wait_ge(eT.sem, eT0 + winpos(m))
                        if 1 <= m < 4:
                            q.wait_ge(eV.sem, eV0 + 2*(m-1) + 1)
                        for half in range(2):
                            if m < 4:
                                ii = q.activation(out=xbwin[:, 3 + half*512: 3 + (half+1)*512],
                                                  in_=ps[6 + half][0:128, :], func=AF.Identity,
                                                  bias=cb_sb[:, m:m+1])
                            else:
                                ii = q.activation(out=zsb[:, zb + (m-4)*TS + half*512:
                                                          zb + (m-4)*TS + (half+1)*512],
                                                  in_=ps[6 + half][0:128, :], func=AF.Identity,
                                                  bias=cb_sb[:, m:m+1])
                        eS.inc(ii, 1)
                        if m < 4:
                            c = m
                            q.wait_ge(eV.sem, eV0 + 2*c + 1)
                            eS.inc(q.activation(out=xcsb[:, c*TS:(c+1)*TS],
                                                in_=xcsb[:, c*TS:(c+1)*TS], func=AF.Silu), 1)
                sc(_)
                st2["mh_evac"] = eS.v
                def _(q, eS0=eS0, eT0=eT0):
                    for c in range(4):
                        q.wait_ge(eS.sem, eS0 + 2*c + 1)
                        q.tensor_copy(out=xbwin[:, 0:3], in_=halo[:, c*3:(c+1)*3])
                        q.tensor_scalar(out=xcsb[:, c*TS:(c+1)*TS], in0=xbwin[:, 3:3+TS],
                                        scalar1=convw_sb[:, c*DC+3:c*DC+4],
                                        scalar2=convb_sb[:, c:c+1],
                                        op0=ALU.mult, op1=ALU.add)
                        for kk in (2, 1, 0):
                            q.scalar_tensor_tensor(out=xcsb[:, c*TS:(c+1)*TS],
                                                   in0=xbwin[:, kk:kk+TS],
                                                   scalar=convw_sb[:, c*DC+kk:c*DC+kk+1],
                                                   in1=xcsb[:, c*TS:(c+1)*TS],
                                                   op0=ALU.mult, op1=ALU.add)
                        eV.inc(q.tensor_copy(out=halo[:, c*3:(c+1)*3],
                                             in_=xbwin[:, TS:TS+3]), 1)
                        q.wait_ge(eS.sem, eS0 + 2*c + 2)
                        if c >= 1:
                            q.wait_ge(eT.sem, eT0 + wxpos(c-1))
                        eV.inc(q.tensor_copy(out=xcbw[:, :], in_=xcsb[:, c*TS:(c+1)*TS]), 1)
                ve(_)

                def _(q):
                    eT.wait(q)
                    q.copy(out=ssm_sb[0:33, 0:512], in_=ps[4][0:33, :])
                    eS.inc(q.copy(out=ssm_sb[0:33, 512:1024], in_=ps[5][0:33, :]), 1)
                    q.wait_ge(eS.sem, eS.v)
                    dma(q, dS, ssm_in[s, :, :], ssm_sb[0:33, 0:TS])
                sc(_)
                def _(q, s=s):
                    q.wait_ge(dS.sem, dS.v)
                    eG.inc(q.collective_compute("AllReduce", ALU.add, replica_groups=GROUPS,
                                                ins=[ssm_in[s, :, :].opt()],
                                                outs=[ssm_out[s, :, :].opt()]), 1)
                gp(_)
                ar_done = eG.v
                def _(q, s=s, w=ar_done):
                    q.wait_ge(eG.sem, w)
                    dma(q, dG, ssm_bf[s, :, :], ssm_out[s, 1:33, :])
                gp(_)
                st2["bc_cast"] = dG.v

                def _(q, s=s, w=ar_done):
                    q.wait_ge(eG.sem, w)
                    dma(q, sdtr, dtrawb[:, :], ssm_out[s, 0:1, :].broadcast_to((128, TS)))
                sy(_)
                dtr_ld = sdtr.v
                eSd0, eVd0 = eS.v, eV.v
                def _(q, w=dtr_ld, eVd0=eVd0):
                    q.wait_ge(sdtr.sem, w)
                    for c in range(4):
                        if c >= 1:
                            q.wait_ge(eV.sem, eVd0 + c)
                        q.activation(out=tmpb[:, :], in_=dtrawb[:, :], func=AF.Abs,
                                     scale=wdt_sb[:, c:c+1], bias=bdt_sb[:, c:c+1])
                        q.activation(out=tmpb[:, :], in_=tmpb[:, :], func=AF.Exp, scale=-1.0)
                        q.activation(out=tmpb[:, :], in_=tmpb[:, :], func=AF.Ln, bias=1.0)
                        eS.inc(q.activation(out=dt4[:, c*TS:(c+1)*TS], in_=dtrawb[:, :],
                                            func=AF.Relu, scale=wdt_sb[:, c:c+1],
                                            bias=bdt_sb[:, c:c+1]), 1)
                sc(_)
                def _(q, eSd0=eSd0):
                    for c in range(4):
                        q.wait_ge(eS.sem, eSd0 + c + 1)
                        q.tensor_tensor(out=dt4[:, c*TS:(c+1)*TS], in0=dt4[:, c*TS:(c+1)*TS],
                                        in1=tmpb[:, :], op=ALU.add)
                        eV.inc(q.tensor_tensor(out=dtx4[:, c*TS:(c+1)*TS],
                                               in0=dt4[:, c*TS:(c+1)*TS],
                                               in1=xcsb[:, c*TS:(c+1)*TS],
                                               op=ALU.mult), 1)
                ve(_)

            emit_mhead(0)
            for s in range(NSUB):
                eS1, eV1 = eS.v, eV.v
                bc_ld = []
                def _(q, s=s, eV1=eV1, pv=prev_scan_eV1, wc=st2["bc_cast"]):
                    q.wait_ge(dG.sem, wc)
                    for n in range(DS):
                        slot = n % 2
                        if n >= 2:
                            q.wait_ge(eV.sem, eV1 + 4*(n-2) + 4)
                        elif pv is not None:
                            q.wait_ge(eV.sem, pv + 60 + 4*n)
                        dma(q, sbc[slot], Brep[:, slot*TS:(slot+1)*TS],
                            ssm_bf[s, n:n+1, :].broadcast_to((128, TS)))
                        dma(q, sbc[slot], Crep[:, slot*TS:(slot+1)*TS],
                            ssm_bf[s, 16+n:17+n, :].broadcast_to((128, TS)))
                        bc_ld.append(sbc[slot].v)
                sy(_)
                def _(q, eS1=eS1, eV1=eV1):
                    for g in range(4 * DS):
                        n, c = g // 4, g % 4
                        eslot = g % 2
                        if g >= 2:
                            q.wait_ge(eV.sem, eV1 + g - 1)
                        eS.inc(q.activation(out=Ebuf[:, eslot*TS:(eslot+1)*TS],
                                            in_=dt4[:, c*TS:(c+1)*TS],
                                            func=AF.Exp,
                                            scale=A_sb[:, c*DS+n:c*DS+n+1]), 1)
                sc(_)
                def _(q, eS1=eS1):
                    for g in range(4 * DS):
                        n, c = g // 4, g % 4
                        slot = n % 2
                        eslot = g % 2
                        q.wait_ge(eS.sem, eS1 + g + 1)
                        q.wait_ge(sbc[n % 2].sem, bc_ld[n])
                        q.tensor_tensor(out=ubuf[:, eslot*TS:(eslot+1)*TS],
                                        in0=dtx4[:, c*TS:(c+1)*TS],
                                        in1=Brep[:, slot*TS:(slot+1)*TS], op=ALU.mult)
                        q.tensor_tensor_scan(out=hbuf[:, :],
                                             data0=Ebuf[:, eslot*TS:(eslot+1)*TS],
                                             data1=ubuf[:, eslot*TS:(eslot+1)*TS],
                                             initial=state[:, c*DS+n:c*DS+n+1],
                                             op0=ALU.mult, op1=ALU.add)
                        if n == 0:
                            q.tensor_tensor(out=acc4[:, c*TS:(c+1)*TS], in0=hbuf[:, :],
                                            in1=Crep[:, slot*TS:(slot+1)*TS], op=ALU.mult)
                        else:
                            q.tensor_tensor(out=ytmp[:, :], in0=hbuf[:, :],
                                            in1=Crep[:, slot*TS:(slot+1)*TS], op=ALU.mult)
                            q.tensor_tensor(out=acc4[:, c*TS:(c+1)*TS],
                                            in0=acc4[:, c*TS:(c+1)*TS],
                                            in1=ytmp[:, :], op=ALU.add)
                        eV.inc(q.tensor_copy(out=state[:, c*DS+n:c*DS+n+1],
                                             in_=hbuf[:, TS-1:TS]), 1)
                    for c in range(4):
                        q.scalar_tensor_tensor(out=acc4[:, c*TS:(c+1)*TS],
                                               in0=xcsb[:, c*TS:(c+1)*TS],
                                               scalar=dparam_sb[:, c:c+1],
                                               in1=acc4[:, c*TS:(c+1)*TS],
                                               op0=ALU.mult, op1=ALU.add)
                ve(_)

                prev_scan_eV1 = eV1
                if s + 1 < NSUB:
                    emit_mhead(s + 1)

                eS2, eV2 = eS.v, eV.v
                zb2 = (s % 2) * 4 * TS
                def _(q, eV2=eV2, zb2=zb2):
                    for c in range(4):
                        q.wait_ge(eV.sem, eV2 + c)
                        eS.inc(q.activation(out=evsc[:, :],
                                            in_=zsb[:, zb2 + c*TS: zb2 + (c+1)*TS],
                                            func=AF.Silu), 1)
                sc(_)
                def _(q, eS2=eS2, pw_=st2["wo_mm"]):
                    q.wait_ge(eT.sem, pw_)
                    for c in range(4):
                        q.wait_ge(eS.sem, eS2 + c + 1)
                        eV.inc(q.tensor_tensor(out=ygsb[:, c*TS:(c+1)*TS],
                                               in0=acc4[:, c*TS:(c+1)*TS],
                                               in1=evsc[:, :], op=ALU.mult), 1)
                ve(_)

                eT2, eS3, dS0, eV3 = eT.v, eS.v, dS.v, eV.v
                def _(q, eT2=eT2, eS3=eS3, eV3=eV3):
                    q.wait_ge(eV.sem, eV3)
                    for tb in range(8):
                        if tb >= 2:
                            q.wait_ge(eS.sem, eS3 + tb - 1)
                        last = None
                        for fh in range(2):
                            for c in range(4):
                                last = q.matmul(out=ps[2*(tb % 2) + fh][0:128, :],
                                                lhsT=ygsb[:, c*TS + tb*128: c*TS + (tb+1)*128],
                                                rhs=woutb[:, c*D + fh*512: c*D + (fh+1)*512],
                                                start=(c == 0), stop=(c == 3))
                        eT.inc(last, 1)
                te(_)
                st2["wo_mm"] = eT.v
                swp0 = swp.v
                def _(q, s=s, eT2=eT2, eS3=eS3, swp0=swp0):
                    for tb in range(8):
                        q.wait_ge(eT.sem, eT2 + tb + 1)
                        if tb >= 1:
                            q.wait_ge(swp.sem, swp0 + 16*tb)
                        for fh in range(2):
                            ii = q.copy(out=evsb[:, fh*512:(fh+1)*512],
                                        in_=ps[2*(tb % 2) + fh][0:128, :])
                        eS.inc(ii, 1)
                        q.wait_ge(eS.sem, eS3 + tb + 1)
                        p = s * TS + tb * 128
                        r = (p % 2048) // 512
                        row = (p // 2048) * 4 * RPC + r * RPC + (p % 512)
                        dma(q, swp, wpart[row:row+128, :], evsb[:, 0:D])
                sc(_)
                if s % 2 == 1:
                    kc = s // 2
                    def _(q, kc=kc, w=swp.v):
                        q.wait_ge(swp.sem, w)
                        eG.inc(q.collective_compute(
                            "ReduceScatter", ALU.add, replica_groups=GROUPS,
                            ins=[wpart[kc*4*RPC:(kc+1)*4*RPC, :].opt()],
                            outs=[rsout[kc*RPC:(kc+1)*RPC, :].opt()]), 1)
                        rs_done_list.append(eG.v)
                    gp(_)
            barrier()

        ffn_phase(2)

        def _(q):
            q.wait_ge(dS.sem, dS.v)
            if DBG:
                q.wait_ge(dL.sem, dL.v)
                dma(q, dG, dbg_x1[:, :], x1_dram[:, :])
                dma(q, dG, dbg_xn2[:, :], xn2_slice[:, :])
                dma(q, dG, dbg_cag[:, :], cag2.rearrange("k r d -> (k r) d"))
                dma(q, dG, dbg_ssm[:, :, :], ssm_out[:, :, :])
                dma(q, dG, dbg_rs[:, :], rsout[:, :])
                dma(q, dG, dbg_wp[:, :], wpart[:, :])
                q.wait_ge(dG.sem, dG.v)
        gp(_)

    return nc


LAST_RES = None


def kernel(**inputs):
    x = np.asarray(inputs["L0_attn_x"], np.float32)
    W_in = np.asarray(inputs["L0_attn_W_in"], np.float32)
    gam = np.asarray(inputs["L0_attn_ln_gamma"], np.float32)
    bet = np.asarray(inputs["L0_attn_ln_beta"], np.float32)
    conv_w = np.asarray(inputs["L0_attn_conv_w"], np.float32)
    conv_b = np.asarray(inputs["L0_attn_conv_b"], np.float32)
    W_x = np.asarray(inputs["L0_attn_W_x"], np.float32)
    w_dt = np.asarray(inputs["L0_attn_w_dt"], np.float32)
    b_dt = np.asarray(inputs["L0_attn_b_dt"], np.float32)
    A_log = np.asarray(inputs["L0_attn_A_log"], np.float32)
    D_param = np.asarray(inputs["L0_attn_D_param"], np.float32)
    W_out = np.asarray(inputs["L0_attn_W_out"], np.float32)
    col_emb = np.asarray(inputs["L0_ffn_col_emb"], np.float32)
    gate_w1 = np.asarray(inputs["L0_ffn_gate_w1"], np.float32)
    gate_b1 = np.asarray(inputs["L0_ffn_gate_b1"], np.float32)
    gate_w2 = np.asarray(inputs["L0_ffn_gate_w2"], np.float32)
    gate_b2 = np.asarray(inputs["L0_ffn_gate_b2"], np.float32)
    W_cols = np.asarray(inputs["L0_ffn_W_cols"], np.float32)
    b_cols = np.asarray(inputs["L0_ffn_b_cols"], np.float32)
    proj_w = np.asarray(inputs["L0_ffn_proj_w"], np.float32)
    proj_b = np.asarray(inputs["L0_ffn_proj_b"], np.float32)
    ln_g = np.asarray(inputs["L0_ffn_ln_g"], np.float32)
    ln_b = np.asarray(inputs["L0_ffn_ln_b"], np.float32)

    nc = build_nc()

    C = np.ascontiguousarray
    in_maps = []
    for core in range(NCORE):
        b = core // 4
        qq = core % 4
        ch0 = qq * CHS
        xr = C(x[b].reshape(NCHK, 4, RPC, D)[:, qq].reshape(NROWS, D))
        win_slice = np.concatenate([W_in[ch0:ch0+CHS], W_in[DI+ch0:DI+ch0+CHS]], axis=0)
        m = dict(
            xrows=xr,
            win_t=C(win_slice.T),
            gam_m=C(gam.reshape(D, 1)),
            bet_m=C(bet.reshape(D, 1)),
            wout_t=C(W_out[:, ch0:ch0+CHS].T),
            wx_t=C(W_x[:, ch0:ch0+CHS].T),
            convw=C(conv_w[ch0:ch0+CHS]),
            convb=C(conv_b[ch0:ch0+CHS].reshape(CHS, 1)),
            dparam=C(D_param[ch0:ch0+CHS].reshape(CHS, 1)),
            wdt=C(w_dt[ch0:ch0+CHS].reshape(CHS, 1)),
            bdt=C(b_dt[ch0:ch0+CHS].reshape(CHS, 1)),
            alog=C(A_log[ch0:ch0+CHS]),
            gw1=gate_w1,
            gb1=C(gate_b1.reshape(DH, 1)),
            gw2=gate_w2,
            gb2=C(gate_b2.reshape(1, C16)),
            wcols_t=C(W_cols.transpose(1, 0, 2).reshape(D, CN)),
            bcols=C(b_cols.reshape(CN, 1)),
            projw=proj_w,
            projb=C(proj_b.reshape(1, D)),
            lng=C(ln_g.reshape(1, D)),
            lnb=C(ln_b.reshape(1, D)),
            colemb=col_emb,
            colembt=C(col_emb.T),
            ident=np.eye(128, dtype=np.float32),
        )
        in_maps.append(m)

    res = run_bass_kernel_spmd(nc, in_maps, core_ids=list(range(NCORE)))
    global LAST_RES
    LAST_RES = res

    out = np.empty((B, T, D), np.float32)
    for core in range(NCORE):
        b = core // 4
        qq = core % 4
        out[b].reshape(NCHK, 4, RPC, D)[:, qq] = \
            res.results[core]["outrows"].reshape(NCHK, RPC, D)
    return out
